# version 14
# baseline (speedup 1.0000x reference)
import numpy as np
from contextlib import ExitStack

import concourse.bass as bass
import concourse.mybir as mybir
from concourse.bass_utils import run_bass_kernel_spmd

F32 = mybir.dt.float32
BF16 = mybir.dt.bfloat16
AF = mybir.ActivationFunctionType
ALU = mybir.AluOpType
AX = mybir.AxisListType

ENGINES = ("tensor", "vector", "scalar", "gpsimd", "sync")
NEG = -1.0e30
D = 1024



class Buf:
    __slots__ = ("h", "name", "dep", "is_psum", "eng")

    def __init__(self, h, name, is_psum=False):
        self.h = h
        self.name = name
        self.dep = {}
        self.is_psum = is_psum
        self.eng = {}

    def __getitem__(self, s):
        return self.h[s]

    def ap(self):
        return self.h.ap()


class _Op:
    __slots__ = ("fn", "waits", "incs")

    def __init__(self, fn):
        self.fn = fn
        self.waits = []
        self.incs = []


class Prog:

    def __init__(self, nc, ctx):
        self.nc = nc
        self.ctx = ctx
        self.phase = 0
        self.ops = {e: [] for e in ENGINES}
        self.chan = {}
        self.know = {e: {} for e in ENGINES}
        self.dma_sems = {}
        self.pending = []
        self.nblk = 0

    def _sem(self, name):
        return self.ctx.enter_context(self.nc.semaphore(name))

    def chan_sem(self, pe):
        if pe not in self.chan:
            self.chan[pe] = [self._sem(f"dn_{pe[:2]}"), 0]
        return self.chan[pe]

    def dma_sem(self, name):
        if name not in self.dma_sems:
            self.dma_sems[name] = [self._sem(f"dm_{name}"), 0]
        return self.dma_sems[name]

    def _edges_for(self, reads, writes):
        toks = []
        for (buf, key) in reads:
            for k, st in buf.dep.items():
                if key is None or k is None or k == key:
                    if st[0] is not None:
                        toks.append(st[0])
        for (buf, key) in writes:
            for k, st in buf.dep.items():
                if key is None or k is None or k == key:
                    if st[0] is not None:
                        toks.append(st[0])
                    toks.extend(st[1])
        return toks

    _CLS = {"tensor": "t", "vector": "v", "scalar": "s"}

    def op(self, eng, fn, r=(), w=(), record=True):
        pos = len(self.ops[eng])
        toks = self._edges_for(r, w)
        cls = self._CLS.get(eng)
        mytok_pending = []
        if cls is not None:
            for (buf, key) in tuple(r) + tuple(w):
                if not getattr(buf, "is_psum", False):
                    continue
                for k, st in buf.eng.items():
                    if key is None or k is None or k == key:
                        for c2, tok2 in st.items():
                            if c2 != cls and tok2 is not None:
                                toks.append(tok2)
                mytok_pending.append((buf, key))
        entry = _Op(fn)
        self.ops[eng].append(entry)
        imax = {}
        smax = {}
        for tok in toks:
            if tok[0] == 'i':
                _, pe, ph, p = tok
                if ph < self.phase:
                    continue
                if pe == eng == "tensor":
                    continue
                if imax.get(pe, -1) < p:
                    imax[pe] = p
            elif tok[0] == 'S':
                name = tok[1]
                val = self.dma_sems[name][1]
                if smax.get(name, 0) < val:
                    smax[name] = val
            else:
                _, name, val = tok
                if smax.get(name, 0) < val:
                    smax[name] = val
        for pe, p in imax.items():
            self.pending.append((eng, pos, pe, p))
        for name, val in smax.items():
            if self.know[eng].get(name, 0) < val:
                self.know[eng][name] = val
                entry.waits.append((self.dma_sems[name][0], val))
        tok = ('i', eng, self.phase, pos)
        if cls is not None:
            for (buf, key) in mytok_pending:
                buf.eng.setdefault(key, {})[cls] = tok
        if record:
            for (buf, key) in r:
                buf.dep.setdefault(key, [None, []])[1].append(tok)
            for (buf, key) in w:
                if key is None:
                    buf.dep.clear()
                buf.dep[key] = [tok, []]
        return tok

    def dma(self, qeng, out, in_, sem_name=None, r=(), w=(), transpose=False):
        if sem_name is None:
            assert w, "dma needs sem_name or a write target"
            buf, key = w[0]
            sem_name = f"{buf.name}.{key}"
        sem = self.dma_sem(sem_name)

        if transpose:
            def fn(e, out=out, in_=in_):
                return e.dma_start_transpose(out, in_)
        else:
            def fn(e, out=out, in_=in_):
                return e.dma_start(out=out, in_=in_)

        self.op(qeng, fn, r=r, w=w, record=False)
        self.ops[qeng][-1].incs.append((sem[0], 16))
        sem[1] += 16
        tok_done = ('s', sem_name, sem[1])
        tok_tot = ('S', sem_name)
        for (buf, key) in w:
            if key is None:
                buf.dep.clear()
            buf.dep[key] = [tok_tot, []]
        for (buf, key) in r:
            buf.dep.setdefault(key, [None, []])[1].append(tok_tot)
        return tok_done

    def wait_all_dma(self, eng, sem_name):
        s = self.dma_sems[sem_name]
        h, v = s[0], s[1]
        self.op(eng, lambda e: e.wait_ge(h, v))

    def flush(self, name=None):
        self._last_flush = name
        needed = {}
        for (ce, cpos, pe, p) in self.pending:
            needed.setdefault(pe, set()).add(p)
        tagval = {}
        for pe, ps in needed.items():
            ch = self.chan_sem(pe)
            for pp in sorted(ps):
                ch[1] += 1
                assert not self.ops[pe][pp].incs, (pe, pp)
                self.ops[pe][pp].incs.append((ch[0], 1))
                tagval[(pe, pp)] = ch[1]
        lastwait = {}
        for (ce, cpos, pe, pp) in sorted(self.pending, key=lambda t: (t[0], t[1])):
            v = tagval[(pe, pp)]
            if lastwait.get((pe, ce), 0) >= v:
                continue
            lastwait[(pe, ce)] = v
            self.ops[ce][cpos].waits.append((self.chan_sem(pe)[0], v))

        self.nblk += 1
        with self.nc.Block(name or f"ph{self.nblk}", no_gpsimd_drain=True) as blk:
            for eng in ENGINES:
                entries = self.ops[eng]
                if not entries:
                    continue

                def body(e, entries=entries):
                    for ent in entries:
                        for (s, v) in ent.waits:
                            e.wait_ge(s, v)
                        inst = ent.fn(e)
                        for (s, k) in ent.incs:
                            inst.then_inc(s, k)

                getattr(blk, eng)(body)
        self.ops = {e: [] for e in ENGINES}
        self.pending = []
        self.phase += 1




def _prep_weights(i):
    out = {}
    f32 = np.float32

    def c(a):
        return np.ascontiguousarray(np.asarray(a, dtype=f32))

    def pm_proj(wT):
        a = np.asarray(wT, f32).reshape(8, 128, 4, 256)
        return c(a.transpose(2, 1, 0, 3).reshape(4, 128, 8 * 256))

    out["wqT"] = pm_proj(np.asarray(i["L0_attn_Wq"]).T)
    out["wkT"] = pm_proj(np.asarray(i["L0_attn_Wk"]).T)
    out["wvT"] = pm_proj(np.asarray(i["L0_attn_Wv"]).T)
    out["woPM"] = c(np.asarray(i["L0_attn_Wo"]).T.reshape(8, 128, 1024)
                    .transpose(1, 0, 2).reshape(128, 8192))
    out["qg10"] = c(np.concatenate([np.asarray(i["L0_attn_Wtq"]).T,
                                    np.asarray(i["L0_ffn_gate_w"]).T], axis=1))
    out["bqp10"] = c(np.concatenate([np.asarray(i["L0_attn_btq"]), np.zeros(8, f32)]))[None, :]
    out["bq0"] = c(np.asarray(i["L0_attn_bq"]).reshape(8, 128).T)
    out["bk0"] = c(np.asarray(i["L0_attn_bk"]).reshape(8, 128).T)
    out["bv0_row"] = c(i["L0_attn_bv"])[None, :]
    out["bv0c"] = c(np.asarray(i["L0_attn_bv"]).reshape(8, 128).T)
    out["bo_row"] = c(i["L0_attn_bo"])[None, :]
    out["bfq_row"] = c(i["L0_attn_bfq"])[None, :]
    out["wfqT"] = c(np.asarray(i["L0_attn_Wfq"]).T)
    def pm_df(a, slices):
        e_, d_, f_ = a.shape
        parts = []
        for (f0, fl) in slices:
            b = a[:, :, f0:f0 + fl].reshape(e_, 8, 128, fl)
            parts.append(b.transpose(0, 2, 1, 3).reshape(e_, 128, 8 * fl))
        return c(np.concatenate(parts, axis=2))

    def pm_fd(a):
        e_, f_, o_ = a.shape
        nfc = f_ // 128
        return c(a.reshape(e_, nfc, 128, o_).transpose(0, 2, 1, 3).reshape(e_, 128, nfc * o_))

    wg0a = np.concatenate([np.transpose(np.asarray(i["L0_ffn_wg"]), (0, 2, 1)),
                           np.asarray(i["L0_ffn_sg"]).T[None]], axis=0)
    wu0a = np.concatenate([np.transpose(np.asarray(i["L0_ffn_wu"]), (0, 2, 1)),
                           np.asarray(i["L0_ffn_su"]).T[None]], axis=0)
    wd0a = np.concatenate([np.transpose(np.asarray(i["L0_ffn_wd"]), (0, 2, 1)),
                           np.asarray(i["L0_ffn_sd"]).T[None]], axis=0)
    out["wg0"] = pm_df(wg0a, [(0, 512)])
    out["wu0"] = pm_df(wu0a, [(0, 512)])
    out["wd0"] = pm_fd(wd0a)
    out["wq1T"] = pm_proj(np.asarray(i["L1_attn_Wq"]).reshape(D, D).T)
    out["wk1T"] = pm_proj(np.asarray(i["L1_attn_Wk"]).reshape(D, D).T)
    out["wv1T"] = pm_proj(np.asarray(i["L1_attn_Wv"]).reshape(D, D).T)
    out["bq1"] = c(np.asarray(i["L1_attn_bq"]).reshape(D).reshape(8, 128).T)
    out["bk1"] = c(np.asarray(i["L1_attn_bk"]).reshape(D).reshape(8, 128).T)
    out["bv1_row"] = c(np.asarray(i["L1_attn_bv"]).reshape(D))[None, :]
    out["bv1c"] = c(np.asarray(i["L1_attn_bv"]).reshape(D).reshape(8, 128).T)
    out["proj1"] = c(i["L1_attn_proj"])
    out["gate1T"] = c(np.asarray(i["L1_ffn_gate_w"]).T)
    swg = np.asarray(i["L1_ffn_swg"], f32)
    swu = np.asarray(i["L1_ffn_swu"], f32)
    swd = np.asarray(i["L1_ffn_swd"], f32)
    wg1a = np.concatenate([np.asarray(i["L1_ffn_wg"]),
                           np.stack([swg[:, :1408], swg[:, 1408:]])], axis=0)
    wu1a = np.concatenate([np.asarray(i["L1_ffn_wu"]),
                           np.stack([swu[:, :1408], swu[:, 1408:]])], axis=0)
    wd1a = np.concatenate([np.asarray(i["L1_ffn_wd"]),
                           swd.reshape(2, 1408, 1024)], axis=0)
    L1SL = [(0, 512), (512, 512), (1024, 384)]
    out["wg1"] = pm_df(wg1a, L1SL)
    out["wu1"] = pm_df(wu1a, L1SL)
    out["wd1"] = pm_fd(wd1a)
    out["id_f"] = np.eye(128, dtype=f32)
    out["ones_col"] = np.ones((1, 128), f32)
    out["ones128"] = np.ones((128, 1), f32)
    cb = np.zeros((128, 128), f32)
    cb[np.triu_indices(128, 1)] = NEG
    out["causal"] = cb
    return out


WEIGHT_SHAPES = [
    ("wqT", (4, 128, 2048)), ("wkT", (4, 128, 2048)), ("wvT", (4, 128, 2048)), ("woPM", (128, 8 * D)),
    ("qg10", (D, 10)), ("bqp10", (1, 10)), ("bq0", (128, 8)), ("bk0", (128, 8)),
    ("bv0_row", (1, D)), ("bo_row", (1, D)), ("bfq_row", (1, D)), ("wfqT", (2, D)),
    ("wg0", (9, 128, 4096)), ("wu0", (9, 128, 4096)), ("wd0", (9, 128, 4096)),
    ("wq1T", (4, 128, 2048)), ("wk1T", (4, 128, 2048)), ("wv1T", (4, 128, 2048)),
    ("bq1", (128, 8)), ("bk1", (128, 8)), ("bv1_row", (1, D)),
    ("bv0c", (128, 8)), ("bv1c", (128, 8)),
    ("proj1", (D, D)), ("gate1T", (D, 8)),
    ("wg1", (10, 128, 11264)), ("wu1", (10, 128, 11264)), ("wd1", (10, 128, 11264)),
    ("id_f", (128, 128)), ("ones_col", (1, 128)), ("ones128", (128, 1)),
    ("causal", (128, 128)),
]




def build_program(T=2048, taps=(), stop_after=None):
    NCH = T // 128
    TS = min(512, T)
    NTS = T // TS
    nc = bass.Bass("TRN2", target_bir_lowering=False)

    di = {}
    di["x"] = nc.dram_tensor("x", [T, D], F32, kind="ExternalInput").ap()
    for name, shape in WEIGHT_SHAPES:
        di[name] = nc.dram_tensor(name, list(shape), F32, kind="ExternalInput").ap()
    out_hbm = nc.dram_tensor("out", [T, D], F32, kind="ExternalOutput").ap()
    tap_ap = {}

    def tap(name, shape):
        if name in taps and name not in tap_ap:
            tap_ap[name] = nc.dram_tensor("tap_" + name, list(shape), F32,
                                          kind="ExternalOutput").ap()
        return tap_ap.get(name)

    ctx = ExitStack()
    with ctx:
        p = Prog(nc, ctx)

        SB_TOP = nc.SBUF_PARTITION_SIZE_BYTES

        SB_BASE = (SB_TOP - nc.sbuf_bytes_remaining + 127) & ~63

        class SAlloc:
            def __init__(self):
                self.free_l = [[SB_BASE, SB_TOP, []]]

            def alloc(self, size):
                size = (size + 63) & ~63
                for reg in self.free_l:
                    if reg[1] - reg[0] >= size:
                        off = reg[0]
                        toks = list(reg[2])
                        reg[0] += size
                        if reg[0] >= reg[1]:
                            self.free_l.remove(reg)
                        return off, size, toks
                raise RuntimeError(f"SBUF OOM: need {size}, free={[(r[0], r[1]) for r in self.free_l]}")

            def release(self, off, size, toks):
                self.free_l.append([off, off + size, toks])
                self.free_l.sort(key=lambda r: r[0])
                merged = []
                for reg in self.free_l:
                    if merged and merged[-1][1] == reg[0]:
                        merged[-1][1] = reg[1]
                        merged[-1][2] = merged[-1][2] + reg[2]
                    else:
                        merged.append(reg)
                self.free_l = merged

        salloc = SAlloc()
        _uid = [0]

        class _StopBuild(Exception):
            pass

        _orig_flush = Prog.flush

        def _flush_ck(name=None):
            _orig_flush(p, name)
            if stop_after is not None and name == stop_after:
                raise _StopBuild()
        p.flush = _flush_ck

        class PScope:
            def __init__(self):
                self.bufs = []
                self.es = ExitStack()

            def enter_context(self, cm):
                return self.es.enter_context(cm)

            def register(self, buf, off, size):
                self.bufs.append((buf, off, size))

            def close(self):
                self.es.close()
                for (buf, off, size) in self.bufs:
                    toks = []
                    for st in buf.dep.values():
                        if st[0] is not None:
                            toks.append(st[0])
                        toks.extend(st[1])
                    salloc.release(off, size, toks)
                self.bufs = []

            def __enter__(self):
                return self

            def __exit__(self, *a):
                self.close()
                return False

        perm_scope = PScope()

        def sb(name, part, free, dt, scope=None):
            nbytes = free * (2 if dt == BF16 else 4)
            off, size, toks = salloc.alloc(nbytes)
            _uid[0] += 1
            h = nc.alloc_sbuf_tensor_at(f"s{_uid[0]}_{name}", [part, free], dt, offset=off)
            b = Buf(h, name)
            if toks:
                b.dep[("alias",)] = [None, toks]
            (scope or perm_scope).register(b, off, size)
            return b

        def psum(name, part, free, scope=None, dt=F32):
            _uid[0] += 1
            return Buf((scope or ctx).enter_context(
                nc.psum_tensor(f"p{_uid[0]}_{name}", [part, free], dt)), name,
                is_psum=True)

        x_tm = sb("x_tm", 128, NCH * D, F32)
        qp_tm = sb("qp_tm", 128, NCH * 2, F32)
        g0_tm = sb("g0_tm", 128, NCH * 8, F32)
        cw_tm = sb("cw_tm", 128, NCH * 8, F32)
        id_f = sb("id_f", 128, 128, F32)
        id_b = sb("id_b", 128, 128, BF16)
        ones_col = sb("ones_col", 1, 128, F32)
        ones128 = sb("ones128", 128, 1, F32)
        causal = sb("causal", 128, 128, F32)
        qg10 = sb("qg10", 128, 8 * 10, F32)
        bqp10 = sb("bqp10", 1, 10, F32)
        bcol = sb("bcol", 128, 4 * 8, F32)
        tthr = sb("tthr", 128, 1, F32)
        stat = sb("stat", 1, 32, F32)
        sm_sc = sb("sm_sc", 128, 8, F32)

        p.dma("sync", x_tm.ap().rearrange("p (c o) -> p c o", c=NCH),
              di["x"].rearrange("(c p) o -> p c o", p=128), None,
              w=[(x_tm, c) for c in range(NCH)])
        p.dma("sync", id_f[:], di["id_f"], None, w=[(id_f, None)])
        p.dma("gpsimd", id_b[:], di["id_f"], None, w=[(id_b, None)])
        p.dma("sync", ones_col[:], di["ones_col"], None, w=[(ones_col, None)])
        p.dma("sync", ones128[:], di["ones128"], None, w=[(ones128, None)])
        p.dma("sync", causal[:], di["causal"], None, w=[(causal, None)])
        p.dma("sync", qg10.ap().rearrange("p (c o) -> p c o", c=8),
              di["qg10"].rearrange("(c p) o -> p c o", p=128), None, w=[(qg10, None)])
        p.dma("sync", bqp10[:], di["bqp10"], None, w=[(bqp10, None)])
        p.dma("sync", bcol[:, 0:8], di["bq0"], None, w=[(bcol, 0)])
        p.dma("sync", bcol[:, 8:16], di["bk0"], None, w=[(bcol, 1)])
        p.dma("sync", bcol[:, 16:24], di["bq1"], None, w=[(bcol, 2)])
        p.dma("sync", bcol[:, 24:32], di["bk1"], None, w=[(bcol, 3)])
        try:
            p.flush("load")
        except _StopBuild:
            stop_after = "__stopped__"

        def ln_pass(layer, ph, xn_fm):
            xn_rot = sb(f"xnrot{layer}", 128, 4 * D, F32, ph)
            stg = sb(f"stg{layer}", 128, 4 * D, F32, ph)
            sqs = sb(f"sqs{layer}", 128, 2 * D, F32, ph)
            e8 = sb(f"e8_{layer}", 128, 4 * 8, F32, ph)
            lsc = sb(f"lsc{layer}", 128, 32, F32, ph)
            gsc = sb(f"gsc{layer}", 128, 32, F32, ph)
            xt_ps = psum(f"xtp{layer}", 128, 2 * D, ph)
            qpg_ps = psum(f"qgp{layer}", 128, 1024, ph)
            nqg = 10 if layer == 0 else 8
            if layer == 1:
                p.dma("sync", qg10.ap().rearrange("p (c o) -> p c o", c=8)[:, :, 0:8],
                      di["gate1T"].rearrange("(c p) o -> p c o", p=128), None,
                      w=[(qg10, None)])
            for c in range(NCH):
                par = c % 4
                p2 = c % 2
                L = lambda k, par=par: (lsc, (par, k))
                Ls = lambda k, par=par: lsc[:, par * 8 + k:par * 8 + k + 1]
                xsl = x_tm[:, c * D:(c + 1) * D]
                xnsl = xn_rot[:, par * D:(par + 1) * D]
                stgs = stg[:, par * D:(par + 1) * D]
                p.op("vector", lambda e, xsl=xsl, a=Ls(0), c=c: e.reduce_sum(a, xsl, axis=AX.X, negate=True),
                     r=[(x_tm, c)], w=[L(0)])
                p.op("scalar", lambda e, xsl=xsl, b=Ls(2), c=c, p2=p2: e.activation(
                        sqs[:, p2 * D:(p2 + 1) * D], xsl, AF.Square, accum_out=b),
                     r=[(x_tm, c)], w=[(sqs, p2), L(2)])
                p.op("vector", lambda e, a=Ls(0): e.tensor_scalar(a, a, 1.0 / D, None, ALU.mult),
                     r=[L(0)], w=[L(0)])
                p.op("vector", lambda e, a=Ls(3), b=Ls(0): e.tensor_tensor(a, b, b, ALU.mult),
                     r=[L(0)], w=[L(3)])
                p.op("vector", lambda e, a=Ls(4), b=Ls(2), m2=Ls(3): e.scalar_tensor_tensor(
                        a, b, 1.0 / D, m2, ALU.mult, ALU.subtract),
                     r=[L(2), L(3)], w=[L(4)])
                p.op("scalar", lambda e, a=Ls(5), b=Ls(4): e.activation(a, b, AF.Sqrt, bias=np.float32(1e-5)),
                     r=[L(4)], w=[L(5)])
                p.op("vector", lambda e, a=Ls(6), b=Ls(5): e.reciprocal(a, b),
                     r=[L(5)], w=[L(6)])
                p.op("vector", lambda e, xsl=xsl, xnsl=xnsl, m=Ls(0), rs=Ls(6), c=c: e.tensor_scalar(
                        xnsl, xsl, m, rs, ALU.add, ALU.mult),
                     r=[(x_tm, c), L(0), L(6)], w=[(xn_rot, par)])
                for dc in range(8):
                    p.op("tensor", lambda e, xnsl=xnsl, dc=dc, p2=p2: e.transpose(
                            xt_ps[:, p2 * D + dc * 128:p2 * D + (dc + 1) * 128],
                            xnsl[:, dc * 128:(dc + 1) * 128], id_f[:]),
                         r=[(xn_rot, par), (id_f, None)], w=[(xt_ps, (p2, dc))])
                rd_xt = [(xt_ps, (p2, dc)) for dc in range(8)]
                p.op("vector", lambda e, c=c, p2=p2: e.tensor_copy(
                        xn_fm.ap().rearrange("p (d t) -> p d t", d=8)[:, :, c * 128:(c + 1) * 128],
                        xt_ps.ap().rearrange("p (b d t) -> p b d t", b=2, d=8)[:, p2]),
                     r=rd_xt, w=[(xn_fm, c)])
                p.op("scalar", lambda e, stgs=stgs, p2=p2: e.copy(
                        stgs, xt_ps[:, p2 * D:(p2 + 1) * D]),
                     r=rd_xt, w=[(stg, par)])
                if layer == 0:
                    p.op("vector", lambda e, xsl=xsl, xnsl=xnsl, c=c, par=par: e.tensor_tensor(
                            xsl, xsl, xnsl, ALU.add),
                         r=[(x_tm, c), (xn_rot, par)], w=[(x_tm, c)])
                for dc in range(8):
                    p.op("tensor", lambda e, dc=dc, stgs=stgs, p2=p2: e.matmul(
                            qpg_ps[:, p2 * 512:p2 * 512 + nqg],
                            stgs[:, dc * 128:(dc + 1) * 128],
                            qg10[:, dc * 10:dc * 10 + nqg],
                            start=(dc == 0), stop=(layer == 1 and dc == 7)),
                         r=[(stg, par), (qg10, None)], w=[(qpg_ps, p2)])
                if layer == 0:
                    p.op("tensor", lambda e, p2=p2: e.matmul(
                            qpg_ps[:, p2 * 512:p2 * 512 + nqg], ones_col[:],
                            bqp10[0:1, 0:nqg], start=False, stop=True),
                         r=[(ones_col, None), (bqp10, None)], w=[(qpg_ps, p2)])
                    p.op("vector", lambda e, c=c, p2=p2: e.tensor_copy(
                            qp_tm[:, c * 2:(c + 1) * 2], qpg_ps[:, p2 * 512:p2 * 512 + 2]),
                         r=[(qpg_ps, p2)], w=[(qp_tm, c)])
                    p.op("vector", lambda e, c=c, p2=p2: e.tensor_copy(
                            g0_tm[:, c * 8:(c + 1) * 8], qpg_ps[:, p2 * 512 + 2:p2 * 512 + 10]),
                         r=[(qpg_ps, p2)], w=[(g0_tm, c)])
                else:
                    p.op("vector", lambda e, c=c, p2=p2: e.tensor_copy(
                            g0_tm[:, c * 8:(c + 1) * 8], qpg_ps[:, p2 * 512:p2 * 512 + 8]),
                         r=[(qpg_ps, p2)], w=[(g0_tm, c)])
                G = lambda k, par=par: (gsc, (par, k))
                Gs = lambda k, par=par: gsc[:, par * 8 + k:par * 8 + k + 1]
                gsl = g0_tm[:, c * 8:(c + 1) * 8]
                esl = e8[:, par * 8:(par + 1) * 8]
                csl = cw_tm[:, c * 8:(c + 1) * 8]
                p.op("vector", lambda e, gsl=gsl, a=Gs(0), c=c: e.reduce_max(a, gsl, axis=AX.X, negate=True),
                     r=[(g0_tm, c)], w=[G(0)])
                p.op("scalar", lambda e, gsl=gsl, esl=esl, a=Gs(0), l=Gs(1), c=c, par=par: e.activation(
                        esl, gsl, AF.Exp, bias=a, accum_out=l),
                     r=[(g0_tm, c), G(0)], w=[(e8, par), G(1)])
                p.op("vector", lambda e, a=Gs(2), l=Gs(1): e.reciprocal(a, l),
                     r=[G(1)], w=[G(2)])
                p.op("vector", lambda e, esl=esl, a=Gs(2), par=par: e.tensor_scalar(esl, esl, a, None, ALU.mult),
                     r=[(e8, par), G(2)], w=[(e8, par)])
                p.op("vector", lambda e, esl=esl, a=Gs(3), par=par: e.reduce_max(a, esl, axis=AX.X),
                     r=[(e8, par)], w=[G(3)])
                p.op("vector", lambda e, esl=esl, csl=csl, m1=Gs(3), c=c, par=par: e.tensor_scalar(
                        csl, esl, m1, m1, ALU.is_equal, ALU.mult),
                     r=[(e8, par), G(3)], w=[(cw_tm, c)])
                p.op("vector", lambda e, esl=esl, csl=csl, c=c, par=par: e.tensor_tensor(
                        esl, esl, csl, ALU.subtract),
                     r=[(e8, par), (cw_tm, c)], w=[(e8, par)])
                p.op("vector", lambda e, esl=esl, a=Gs(4), par=par: e.reduce_max(a, esl, axis=AX.X),
                     r=[(e8, par)], w=[G(4)])
                p.op("vector", lambda e, esl=esl, m2=Gs(4), par=par: e.tensor_scalar(
                        esl, esl, m2, m2, ALU.is_equal, ALU.mult),
                     r=[(e8, par), G(4)], w=[(e8, par)])
                p.op("vector", lambda e, esl=esl, csl=csl, c=c, par=par: e.tensor_tensor(
                        csl, csl, esl, ALU.add),
                     r=[(e8, par), (cw_tm, c)], w=[(cw_tm, c)])
                if layer == 1:
                    p.op("vector", lambda e, a=Gs(5), m1=Gs(3), m2=Gs(4): e.tensor_tensor(
                            a, m1, m2, ALU.add),
                         r=[G(3), G(4)], w=[G(5)])
                    p.op("vector", lambda e, a=Gs(5): e.tensor_scalar(a, a, 1e-20, None, ALU.add),
                         r=[G(5)], w=[G(5)])
                    p.op("vector", lambda e, a=Gs(6), b=Gs(5): e.reciprocal(a, b),
                         r=[G(5)], w=[G(6)])
                    p.op("vector", lambda e, csl=csl, a=Gs(6), c=c: e.tensor_scalar(
                            csl, csl, a, None, ALU.mult),
                         r=[(cw_tm, c), G(6)], w=[(cw_tm, c)])

        def stats_pass(ph, tabi, tabj, r01f):
            d9 = sb("d9", 128, NCH * 9, F32, ph)
            rtm = sb("rtm", 128, 4, F32, ph)
            s12 = sb("s12", 128, 12, F32, ph)
            st_ps = psum("st_ps", 1, 16, ph)
            tp_ps = psum("tp_ps", 4, 1024, ph)
            th_ps = psum("th_ps", 128, 8, ph)
            N2 = float(T) * float(T)
            for c in range(NCH):
                xs = qp_tm[:, c * 2:c * 2 + 1]
                ys = qp_tm[:, c * 2 + 1:c * 2 + 2]
                ds = lambda k, c=c: d9[:, c * 9 + k:c * 9 + k + 1]
                K = lambda k, c=c: (d9, (c, k))
                p.op("vector", lambda e, a=ds(4), xs=xs, c=c: e.tensor_tensor(a, xs, xs, ALU.mult),
                     r=[(qp_tm, c)], w=[K(4)])
                p.op("vector", lambda e, a=ds(6), ys=ys, c=c: e.tensor_tensor(a, ys, ys, ALU.mult),
                     r=[(qp_tm, c)], w=[K(6)])
                p.op("vector", lambda e, a=ds(0), b=ds(4), cc=ds(6), c=c: e.tensor_tensor(a, b, cc, ALU.add),
                     r=[K(4), K(6)], w=[K(0)])
                p.op("vector", lambda e, a=ds(1), b=ds(0), c=c: e.tensor_tensor(a, b, b, ALU.mult),
                     r=[K(0)], w=[K(1)])
                p.op("vector", lambda e, a=ds(2), xs=xs, c=c: e.tensor_copy(a, xs),
                     r=[(qp_tm, c)], w=[K(2)])
                p.op("vector", lambda e, a=ds(3), ys=ys, c=c: e.tensor_copy(a, ys),
                     r=[(qp_tm, c)], w=[K(3)])
                p.op("vector", lambda e, a=ds(5), xs=xs, ys=ys, c=c: e.tensor_tensor(a, xs, ys, ALU.mult),
                     r=[(qp_tm, c)], w=[K(5)])
                p.op("vector", lambda e, a=ds(7), b=ds(0), xs=xs, c=c: e.tensor_tensor(a, b, xs, ALU.mult),
                     r=[K(0), (qp_tm, c)], w=[K(7)])
                p.op("vector", lambda e, a=ds(8), b=ds(0), ys=ys, c=c: e.tensor_tensor(a, b, ys, ALU.mult),
                     r=[K(0), (qp_tm, c)], w=[K(8)])
            for c in range(NCH):
                p.op("tensor", lambda e, c=c: e.matmul(
                        st_ps[0:1, 0:9], ones128[:], d9[:, c * 9:(c + 1) * 9],
                        start=(c == 0), stop=(c == NCH - 1)),
                     r=[(ones128, None)] + [(d9, (c, k)) for k in range(9)], w=[(st_ps, 0)])
            p.op("vector", lambda e: e.tensor_copy(stat[0:1, 0:9], st_ps[0:1, 0:9]),
                 r=[(st_ps, 0)], w=[(stat, 0)])
            S = lambda k: stat[0:1, k:k + 1]

            def vop(fn, rks, wk):
                p.op("vector", fn, r=[(stat, 0)] + [(stat, k) for k in rks], w=[(stat, wk)])
            vop(lambda e: e.tensor_tensor(S(9), S(2), S(2), ALU.mult), [], 9)
            vop(lambda e: e.tensor_tensor(S(10), S(3), S(3), ALU.mult), [], 10)
            vop(lambda e: e.tensor_tensor(S(9), S(9), S(10), ALU.add), [9, 10], 9)
            vop(lambda e: e.tensor_scalar(S(11), S(9), -2.0, None, ALU.mult), [9], 11)
            vop(lambda e: e.scalar_tensor_tensor(S(11), S(0), 2.0 * T, S(11), ALU.mult, ALU.add), [11], 11)
            vop(lambda e: e.tensor_tensor(S(9), S(4), S(4), ALU.mult), [], 9)
            vop(lambda e: e.tensor_tensor(S(10), S(5), S(5), ALU.mult), [], 10)
            vop(lambda e: e.scalar_tensor_tensor(S(9), S(10), 2.0, S(9), ALU.mult, ALU.add), [9, 10], 9)
            vop(lambda e: e.tensor_tensor(S(10), S(6), S(6), ALU.mult), [], 10)
            vop(lambda e: e.tensor_tensor(S(9), S(9), S(10), ALU.add), [9, 10], 9)
            vop(lambda e: e.tensor_tensor(S(10), S(7), S(2), ALU.mult), [], 10)
            vop(lambda e: e.tensor_tensor(S(12), S(8), S(3), ALU.mult), [], 12)
            vop(lambda e: e.tensor_tensor(S(10), S(10), S(12), ALU.add), [10, 12], 10)
            vop(lambda e: e.tensor_tensor(S(12), S(0), S(0), ALU.mult), [], 12)
            vop(lambda e: e.scalar_tensor_tensor(S(12), S(1), float(T), S(12), ALU.mult, ALU.add), [12], 12)
            vop(lambda e: e.tensor_scalar(S(12), S(12), 2.0, None, ALU.mult), [12], 12)
            vop(lambda e: e.scalar_tensor_tensor(S(12), S(9), 4.0, S(12), ALU.mult, ALU.add), [9, 12], 12)
            vop(lambda e: e.scalar_tensor_tensor(S(12), S(10), -8.0, S(12), ALU.mult, ALU.add), [10, 12], 12)
            vop(lambda e: e.tensor_scalar(S(13), S(11), 1.0 / N2, None, ALU.mult), [11], 13)
            vop(lambda e: e.tensor_scalar(S(14), S(12), 1.0 / N2, None, ALU.mult), [12], 14)
            vop(lambda e: e.tensor_tensor(S(12), S(13), S(13), ALU.mult), [13], 12)
            vop(lambda e: e.tensor_tensor(S(14), S(14), S(12), ALU.subtract), [12, 14], 14)
            vop(lambda e: e.tensor_scalar(S(14), S(14), N2 / (N2 - 1.0), None, ALU.mult), [14], 14)
            p.op("scalar", lambda e: e.activation(S(15), S(14), AF.Sqrt),
                 r=[(stat, 14)], w=[(stat, 15)])
            vop(lambda e: e.scalar_tensor_tensor(S(15), S(15), 1.25, S(13), ALU.mult, ALU.add), [13, 15], 15)
            p.op("tensor", lambda e: e.matmul(th_ps[:, 0:1], ones_col[:], S(15), start=True, stop=True),
                 r=[(ones_col, None), (stat, 15)], w=[(th_ps, 0)])
            p.op("vector", lambda e: e.tensor_copy(tthr[:], th_ps[:, 0:1]),
                 r=[(th_ps, 0)], w=[(tthr, None)])
            for c in range(NCH):
                sl = c % 2
                xs = qp_tm[:, c * 2:c * 2 + 1]
                ys = qp_tm[:, c * 2 + 1:c * 2 + 2]
                asl = d9[:, c * 9:c * 9 + 1]
                p.op("vector", lambda e, xs=xs, c=c: e.tensor_copy(s12[:, 0:1], xs),
                     r=[(qp_tm, c)], w=[(s12, 0)])
                p.op("vector", lambda e, ys=ys, c=c: e.tensor_copy(s12[:, 1:2], ys),
                     r=[(qp_tm, c)], w=[(s12, 1)])
                p.op("vector", lambda e: e.memset(s12[:, 2:3], 1.0), w=[(s12, 2)])
                p.op("vector", lambda e: e.memset(s12[:, 3:5], 0.0), w=[(s12, 3), (s12, 4)])
                p.op("vector", lambda e: e.memset(s12[:, 4:5], 1.0), w=[(s12, 4)])
                p.op("vector", lambda e: e.memset(s12[:, 5:6], 0.0), w=[(s12, 5)])
                p.op("vector", lambda e, xs=xs, c=c: e.tensor_scalar(s12[:, 6:7], xs, -1.0, None, ALU.mult),
                     r=[(qp_tm, c)], w=[(s12, 6)])
                p.op("vector", lambda e: e.memset(s12[:, 7:9], 0.0), w=[(s12, 7), (s12, 8)])
                p.op("vector", lambda e: e.memset(s12[:, 9:10], 1.0), w=[(s12, 9)])
                p.op("vector", lambda e, ys=ys, c=c: e.tensor_scalar(s12[:, 10:11], ys, -1.0, None, ALU.mult),
                     r=[(qp_tm, c)], w=[(s12, 10)])
                p.op("vector", lambda e: e.memset(s12[:, 11:12], 0.0), w=[(s12, 11)])
                rall = [(s12, k) for k in range(12)]
                for (dst, off, c0) in ((tabi, 0, 0), (tabj, 0, 4), (tabj, T, 8)):
                    p.op("tensor", lambda e, c0=c0, sl=sl: e.transpose(
                            tp_ps[0:4, sl * 512:sl * 512 + 128], s12[:, c0:c0 + 4], id_f[:]),
                         r=rall + [(id_f, None)], w=[(tp_ps, sl)])
                    p.op("vector", lambda e, dst=dst, off=off, c=c, sl=sl: e.tensor_copy(
                            dst[0:4, off + c * 128:off + (c + 1) * 128],
                            tp_ps[0:4, sl * 512:sl * 512 + 128]),
                         r=[(tp_ps, sl)], w=[(dst, (off, c))])
                p.op("scalar", lambda e, asl=asl, a=rtm[:, 0:1], c=c: e.activation(a, asl, AF.Sqrt),
                     r=[(d9, (c, 0))], w=[(rtm, 0)])
                p.op("vector", lambda e, a=rtm[:, 0:1]: e.tensor_scalar(a, a, 1e-12, None, ALU.max),
                     r=[(rtm, 0)], w=[(rtm, 0)])
                p.op("vector", lambda e, a=rtm[:, 1:2], b=rtm[:, 0:1]: e.reciprocal(a, b),
                     r=[(rtm, 0)], w=[(rtm, 1)])
                p.op("vector", lambda e, a=rtm[:, 1:2]: e.tensor_scalar(a, a, 0.1, None, ALU.mult),
                     r=[(rtm, 1)], w=[(rtm, 1)])
                p.op("vector", lambda e, c=c, a=rtm[:, 2:4], sc=rtm[:, 1:2]: e.tensor_scalar(
                        a, qp_tm[:, c * 2:(c + 1) * 2], sc, None, ALU.mult),
                     r=[(qp_tm, c), (rtm, 1)], w=[(rtm, 2)])
                p.op("tensor", lambda e, c=c, sl=sl: e.transpose(
                        tp_ps[0:2, sl * 512:sl * 512 + 128], rtm[:, 2:4], id_f[:]),
                     r=[(rtm, 2), (id_f, None)], w=[(tp_ps, sl)])
                p.op("vector", lambda e, c=c, sl=sl: e.tensor_copy(
                        r01f[0:2, c * 128:(c + 1) * 128], tp_ps[0:2, sl * 512:sl * 512 + 128]),
                     r=[(tp_ps, sl)], w=[(r01f, c)])

        def quantum_pass(ph, r01f):
            wfq_sb = sb("wfq_sb", 2, D, F32, ph)
            b01 = sb("b01", 1, 2 * D, F32, ph)
            qps = psum("qps", 128, 2 * 1024, ph)
            p.dma("sync", wfq_sb[:], di["wfqT"], None, w=[(wfq_sb, None)])
            p.dma("sync", b01[0:1, 0:D], di["bo_row"], None, w=[(b01, 0)])
            p.dma("sync", b01[0:1, D:2 * D], di["bfq_row"], None, w=[(b01, 1)])
            p.op("vector", lambda e: e.scalar_tensor_tensor(
                    b01[0:1, 0:D], b01[0:1, D:2 * D], 0.1, b01[0:1, 0:D], ALU.mult, ALU.add),
                 r=[(b01, 0), (b01, 1)], w=[(b01, 0)])
            for c in range(NCH):
                sl = c % 2
                for half in range(2):
                    p.op("tensor", lambda e, c=c, sl=sl, half=half: e.matmul(
                            qps[:, sl * 1024 + half * 512:sl * 1024 + (half + 1) * 512],
                            r01f[0:2, c * 128:(c + 1) * 128],
                            wfq_sb[:, half * 512:(half + 1) * 512],
                            start=True, stop=False),
                         r=[(r01f, c), (wfq_sb, None)], w=[(qps, (sl, half))])
                    p.op("tensor", lambda e, sl=sl, half=half: e.matmul(
                            qps[:, sl * 1024 + half * 512:sl * 1024 + (half + 1) * 512],
                            ones_col[:], b01[0:1, half * 512:(half + 1) * 512],
                            start=False, stop=True),
                         r=[(ones_col, None), (b01, 0)], w=[(qps, (sl, half))])
                p.op("vector", lambda e, c=c, sl=sl: e.tensor_tensor(
                        x_tm[:, c * D:(c + 1) * D], x_tm[:, c * D:(c + 1) * D],
                        qps[:, sl * 1024:(sl + 1) * 1024], ALU.add),
                     r=[(x_tm, c), (qps, (sl, 0)), (qps, (sl, 1))], w=[(x_tm, c)])

        def qkv_pass(layer, ph, q_fm, k_fm, v_tm, xn_fm):
            wq = di["wqT"] if layer == 0 else di["wq1T"]
            wk = di["wkT"] if layer == 0 else di["wk1T"]
            wv = di["wvT"] if layer == 0 else di["wv1T"]
            bq_k = 0 if layer == 0 else 2
            bk_k = 1 if layer == 0 else 3
            wring = sb(f"wring{layer}", 128, 2 * 2048, BF16, ph)
            bvrow_ = sb(f"bvrow{layer}", 1, D, F32, ph)
            pq = psum(f"pq{layer}", 128, 2048, ph)
            p.dma("sync", bvrow_[:], di["bv0_row" if layer == 0 else "bv1_row"], None,
                  w=[(bvrow_, None)])
            for (mode, dst, wsrc, bk) in (("q", q_fm, wq, bq_k), ("k", k_fm, wk, bk_k),
                                          ("v", v_tm, wv, None)):
                for grp in range(4):
                    slot = grp % 2
                    p.dma("gpsimd", wring[:, slot * 2048:(slot + 1) * 2048],
                          wsrc[grp], None, w=[(wring, slot)])
                    for oc_l in range(2):
                        oc = grp * 2 + oc_l
                        if mode != "v":
                            for dc in range(8):
                                for ts in range(NTS):
                                    p.op("tensor", lambda e, dc=dc, ts=ts, slot=slot, oc_l=oc_l: e.matmul(
                                            pq[:, (ts % 4) * 512:(ts % 4) * 512 + TS],
                                            wring[:, slot * 2048 + dc * 256 + oc_l * 128:
                                                  slot * 2048 + dc * 256 + oc_l * 128 + 128],
                                            xn_fm[:, dc * T + ts * TS:dc * T + (ts + 1) * TS],
                                            start=(dc == 0), stop=(dc == 7)),
                                         r=[(wring, slot)] + [(xn_fm, (ts * TS + j * 128) // 128)
                                                              for j in range(TS // 128)],
                                         w=[(pq, ts % 4)])
                            for ts in range(NTS):
                                p.op("vector", lambda e, dst=dst, oc=oc, ts=ts, bk=bk: e.tensor_scalar(
                                        dst[:, oc * T + ts * TS:oc * T + (ts + 1) * TS],
                                        pq[:, (ts % 4) * 512:(ts % 4) * 512 + TS],
                                        bcol[:, bk * 8 + oc:bk * 8 + oc + 1], None, ALU.add),
                                     r=[(pq, ts % 4), (bcol, bk)], w=[(dst, (oc, ts))])
                    if mode == "v":
                        for c in range(NCH):
                            pb = c % 4
                            for dc in range(8):
                                p.op("tensor", lambda e, dc=dc, c=c, slot=slot, pb=pb: e.matmul(
                                        pq[:, pb * 512:pb * 512 + 256],
                                        xn_fm[:, dc * T + c * 128:dc * T + (c + 1) * 128],
                                        wring[:, slot * 2048 + dc * 256:slot * 2048 + (dc + 1) * 256],
                                        start=(dc == 0), stop=False),
                                     r=[(xn_fm, c), (wring, slot)], w=[(pq, pb)])
                            p.op("tensor", lambda e, c=c, grp=grp, pb=pb: e.matmul(
                                    pq[:, pb * 512:pb * 512 + 256],
                                    ones_col[:], bvrow_[0:1, grp * 256:(grp + 1) * 256],
                                    start=False, stop=True),
                                 r=[(ones_col, None), (bvrow_, None)], w=[(pq, pb)])
                            p.op("vector", lambda e, c=c, grp=grp, pb=pb: e.tensor_copy(
                                    v_tm[:, c * D + grp * 256:c * D + (grp + 1) * 256],
                                    pq[:, pb * 512:pb * 512 + 256]),
                                 r=[(pq, pb)],
                                 w=[(v_tm, (c, grp * 2)), (v_tm, (c, grp * 2 + 1))])

        def attn0_pass(ph, q_fm, k_fm, v_tm, tabi, tabj):
            wo_sb = sb("wo_sb", 128, 8 * D, BF16, ph)
            p.dma("gpsimd", wo_sb[:], di["woPM"], None, w=[(wo_sb, None)])
            pbuf = sb("pbuf", 128, T, BF16, ph)
            ptb = sb("ptb", 128, T, BF16, ph)
            ctm = sb("ctm", 128, D, BF16, ph)
            cfm = sb("cfm", 128, 8 * 128, BF16, ph)
            ssc = sb("ssc", 128, 2 * 512, F32, ph)
            psA = psum("psA", 128, 2048, ph)
            ddx = psum("ddx", 128, 512, ph)
            psC = psum("psC", 128, 1024, ph)
            ptp = psum("ptp", 128, 1024, ph, dt=BF16)

            def sc_banks(qt):
                ntile = ((qt + 1) * 128 + 511) // 512
                if ntile <= 2:
                    return [(qt % 2) * 2 + k for k in range(ntile)]
                return list(range(ntile))

            def sc_base(qt):
                ntile = ((qt + 1) * 128 + 511) // 512
                return 1024 if (ntile <= 2 and qt % 2 == 1) else 0

            def stA_mm(qt):
                Lq = (qt + 1) * 128
                bks = sc_banks(qt)
                base = sc_base(qt)
                for kjt in range(len(bks)):
                    j0 = kjt * 512
                    w512 = min(512, Lq - j0)
                    scp = psA[:, base + j0:base + j0 + w512]
                    for dc in range(8):
                        p.op("tensor", lambda e, dc=dc, qt=qt, j0=j0, w512=w512, scp=scp: e.matmul(
                                scp, q_fm[:, dc * T + qt * 128:dc * T + (qt + 1) * 128],
                                k_fm[:, dc * T + j0:dc * T + j0 + w512],
                                start=(dc == 0), stop=(dc == 7)),
                             r=[(q_fm, (dc, qt * 128 // TS))] +
                               [(k_fm, (dc, (j0 + jj * 128) // TS)) for jj in range(w512 // 128)],
                             w=[(psA, bks[kjt])])

            def stA_chain(qt):
                Lq = (qt + 1) * 128
                bks = sc_banks(qt)
                base = sc_base(qt)
                for kjt in range(len(bks)):
                    j0 = kjt * 512
                    w512 = min(512, Lq - j0)
                    scp = psA[:, base + j0:base + j0 + w512]
                    bk = bks[kjt]
                    dp = ddx[:, 0:w512]
                    tp2 = 0
                    sA = ssc[:, 0:w512]
                    sB = ssc[:, 512:512 + w512]
                    p.op("tensor", lambda e, qt=qt, j0=j0, w512=w512, dp=dp: e.matmul(
                            dp, tabi[0:4, qt * 128:(qt + 1) * 128],
                            tabj[0:4, T + j0:T + j0 + w512], start=True, stop=True),
                         r=[(tabi, (0, qt))] + [(tabj, (T, j0 // 128 + jj)) for jj in range(w512 // 128)],
                         w=[(ddx, 0)])
                    p.op("scalar", lambda e, sB=sB, dp=dp: e.activation(sB, dp, AF.Square),
                         r=[(ddx, 0)], w=[(ssc, (tp2, 1))])
                    p.op("tensor", lambda e, qt=qt, j0=j0, w512=w512, dp=dp: e.matmul(
                            dp, tabi[0:4, qt * 128:(qt + 1) * 128],
                            tabj[0:4, j0:j0 + w512], start=True, stop=True),
                         r=[(tabi, (0, qt))] + [(tabj, (0, j0 // 128 + jj)) for jj in range(w512 // 128)],
                         w=[(ddx, 0)])
                    p.op("scalar", lambda e, sA=sA, dp=dp: e.activation(sA, dp, AF.Square),
                         r=[(ddx, 0)], w=[(ssc, (tp2, 0))])
                    p.op("vector", lambda e, sA=sA, sB=sB: e.tensor_tensor(sB, sB, sA, ALU.add),
                         r=[(ssc, (tp2, 0)), (ssc, (tp2, 1))], w=[(ssc, (tp2, 1))])
                    p.op("vector", lambda e, sA=sA, sB=sB: e.tensor_scalar(
                            sA, sB, tthr[:], NEG, ALU.is_gt, ALU.mult),
                         r=[(ssc, (tp2, 1)), (tthr, None)], w=[(ssc, (tp2, 0))])
                    p.op("vector", lambda e, scp=scp, sA=sA, bk=bk: e.tensor_tensor(
                            scp, scp, sA, ALU.add),
                         r=[(psA, bk), (ssc, (tp2, 0))], w=[(psA, bk)])
                    p.op("scalar", lambda e, sB=sB: e.activation(sB, sB, AF.Sqrt, bias=np.float32(1e-30)),
                         r=[(ssc, (tp2, 1))], w=[(ssc, (tp2, 1))])
                    p.op("vector", lambda e, sB=sB, dp=dp: e.tensor_tensor(sB, sB, dp, ALU.add),
                         r=[(ssc, (tp2, 1)), (ddx, 0)], w=[(ssc, (tp2, 1))])
                    p.op("vector", lambda e, sB=sB: e.tensor_scalar(
                            sB, sB, 0.0, NEG, ALU.is_le, ALU.mult),
                         r=[(ssc, (tp2, 1))], w=[(ssc, (tp2, 1))])
                    p.op("vector", lambda e, scp=scp, sB=sB, bk=bk: e.tensor_tensor(
                            scp, scp, sB, ALU.add),
                         r=[(psA, bk), (ssc, (tp2, 1))], w=[(psA, bk)])
                dbk = bks[-1]
                dcol = base + qt * 128
                p.op("vector", lambda e, dcol=dcol, dbk=dbk: e.tensor_tensor(
                        psA[:, dcol:dcol + 128], psA[:, dcol:dcol + 128],
                        causal[:], ALU.add),
                     r=[(psA, dbk), (causal, None)], w=[(psA, dbk)])

            def stB(qt):
                Lq = (qt + 1) * 128
                bks = sc_banks(qt)
                base = sc_base(qt)
                qb = qt % 2
                rmask = [(psA, b) for b in bks]
                p.op("vector", lambda e, Lq=Lq, base=base, qb=qb: e.reduce_max(
                        sm_sc[:, qb * 4:qb * 4 + 1], psA[:, base:base + Lq],
                        axis=AX.X, negate=True),
                     r=rmask, w=[(sm_sc, (qb, 0))])
                p.op("vector", lambda e, qb=qb: e.tensor_scalar(
                        sm_sc[:, qb * 4 + 1:qb * 4 + 2], sm_sc[:, qb * 4:qb * 4 + 1],
                        1.0 / 32.0, None, ALU.mult),
                     r=[(sm_sc, (qb, 0))], w=[(sm_sc, (qb, 1))])
                p.op("scalar", lambda e, Lq=Lq, base=base, qb=qb, qt=qt: e.activation(
                        pbuf[:, 0:Lq], psA[:, base:base + Lq], AF.Exp,
                        bias=sm_sc[:, qb * 4 + 1:qb * 4 + 2],
                        scale=np.float32(1.0 / 32.0), accum_out=sm_sc[:, qb * 4 + 2:qb * 4 + 3]),
                     r=rmask + [(sm_sc, (qb, 1))],
                     w=[(pbuf, kc) for kc in range(qt + 1)] + [(sm_sc, (qb, 2))])
                p.op("vector", lambda e, qb=qb: e.reciprocal(
                        sm_sc[:, qb * 4 + 3:qb * 4 + 4], sm_sc[:, qb * 4 + 2:qb * 4 + 3]),
                     r=[(sm_sc, (qb, 2))], w=[(sm_sc, (qb, 3))])

            def stC(qt):
                qb = qt % 2
                seg = []
                for kc in range(qt + 1):
                    slot = kc % 8
                    p.op("tensor", lambda e, kc=kc, slot=slot: e.transpose(
                            ptp[:, slot * 128:(slot + 1) * 128],
                            pbuf[:, kc * 128:(kc + 1) * 128], id_b[:]),
                         r=[(pbuf, kc), (id_b, None)], w=[(ptp, 0)])
                    seg.append(kc)
                    if len(seg) == 8 or kc == qt:
                        k0 = seg[0]
                        p.op("scalar", lambda e, k0=k0, kc=kc: e.copy(
                                ptb[:, k0 * 128:(kc + 1) * 128],
                                ptp[:, 0:(kc + 1 - k0) * 128]),
                             r=[(ptp, 0)], w=[(ptb, k) for k in seg])
                        seg = []
                for kc in range(qt + 1):
                    for half in range(2):
                        p.op("tensor", lambda e, kc=kc, half=half, qt=qt: e.matmul(
                                psC[:, half * 512:(half + 1) * 512],
                                ptb[:, kc * 128:(kc + 1) * 128],
                                v_tm[:, kc * D + half * 512:kc * D + (half + 1) * 512],
                                start=(kc == 0), stop=(kc == qt)),
                             r=[(ptb, kc)] + [(v_tm, (kc, oo)) for oo in range(8)],
                             w=[(psC, half)])
                p.op("vector", lambda e, qb=qb: e.tensor_scalar(
                        ctm[:], psC[:, 0:1024],
                        sm_sc[:, qb * 4 + 3:qb * 4 + 4], None, ALU.mult),
                     r=[(psC, 0), (psC, 1), (sm_sc, (qb, 3))], w=[(ctm, None)])

            def stD(qt):
                qb = qt % 2
                for dc in range(8):
                    p.op("tensor", lambda e, dc=dc: e.transpose(
                            ptp[:, dc * 128:(dc + 1) * 128],
                            ctm[:, dc * 128:(dc + 1) * 128], id_b[:]),
                         r=[(ctm, None), (id_b, None)], w=[(ptp, 0)])
                p.op("scalar", lambda e: e.copy(cfm[:], ptp[:, 0:1024]),
                     r=[(ptp, 0)], w=[(cfm, dc) for dc in range(8)])
                for dc in range(8):
                    for half in range(2):
                        p.op("tensor", lambda e, dc=dc, half=half: e.matmul(
                                psC[:, half * 512:(half + 1) * 512],
                                cfm[:, dc * 128:(dc + 1) * 128],
                                wo_sb[:, dc * D + half * 512:dc * D + (half + 1) * 512],
                                start=(dc == 0), stop=(dc == 7)),
                             r=[(cfm, dc), (wo_sb, None)],
                             w=[(psC, half)])
                p.op("vector", lambda e, qt=qt: e.tensor_tensor(
                        x_tm[:, qt * D:(qt + 1) * D], x_tm[:, qt * D:(qt + 1) * D],
                        psC[:, 0:1024], ALU.add),
                     r=[(x_tm, qt), (psC, 0), (psC, 1)],
                     w=[(x_tm, qt)])

            for i in range(NCH + 2):
                if i >= 2:
                    stC(i - 2)
                if 0 <= i - 1 < NCH:
                    stB(i - 1)
                if i < NCH:
                    stA_mm(i)
                if i >= 2:
                    stD(i - 2)
                if i < NCH:
                    stA_chain(i)

        def attn1_pass(ph, q_fm, k_fm, v_tm):
            sc_sb = sb("sc_sb", 128, 2 * T, F32, ph)
            pbufA = sb("pbufA", 128, T, BF16, ph)
            pbufB = sb("pbufB", 128, T, BF16, ph)
            ptbA = sb("ptbA", 128, T, BF16, ph)
            ptbB = sb("ptbB", 128, T, BF16, ph)
            zfm = sb("zfm", 128, 2 * 8 * 128, BF16, ph)
            prring = sb("prring", 128, 2 * D, BF16, ph)
            psA = psum("psA1", 128, 2048, ph)
            psB = psum("psB1", 128, 1024, ph)
            ptp = psum("ptp1", 128, 2048, ph, dt=BF16)

            def st_scores(qt, h, Lq, ntile):
                hb = h % 2
                oc, prt = h // 2, (h % 2) * 64
                for kjt in range(ntile):
                    j0 = kjt * 512
                    w512 = min(512, Lq - j0)
                    bk = (hb * 2 + kjt) % 4
                    p.op("tensor", lambda e, oc=oc, prt=prt, qt=qt, j0=j0, w512=w512, bk=bk: e.matmul(
                            psA[:, bk * 512:bk * 512 + w512],
                            q_fm[prt:prt + 64, oc * T + qt * 128:oc * T + (qt + 1) * 128],
                            k_fm[prt:prt + 64, oc * T + j0:oc * T + j0 + w512],
                            start=True, stop=True),
                         r=[(q_fm, (oc, qt * 128 // TS))] +
                           [(k_fm, (oc, (j0 + jj * 128) // TS)) for jj in range(w512 // 128)],
                         w=[(psA, bk)])
                    wdiag = (j0 + w512 == Lq)
                    ee = "vector" if kjt % 2 == 0 else "scalar"
                    if wdiag:
                        if w512 > 128:
                            p.op(ee, lambda e, hb=hb, j0=j0, w512=w512, bk=bk: (
                                    e.tensor_copy(sc_sb[:, hb * T + j0:hb * T + j0 + w512 - 128],
                                                  psA[:, bk * 512:bk * 512 + w512 - 128])
                                    if hasattr(e, "tensor_copy") else
                                    e.copy(sc_sb[:, hb * T + j0:hb * T + j0 + w512 - 128],
                                           psA[:, bk * 512:bk * 512 + w512 - 128])),
                                 r=[(psA, bk)], w=[(sc_sb, (hb, kjt))])
                        p.op("vector", lambda e, hb=hb, j0=j0, qt=qt, bk=bk: e.tensor_tensor(
                                sc_sb[:, hb * T + qt * 128:hb * T + (qt + 1) * 128],
                                psA[:, bk * 512 + (qt * 128 - j0):bk * 512 + (qt * 128 - j0) + 128],
                                causal[:], ALU.add),
                             r=[(psA, bk), (causal, None)], w=[(sc_sb, (hb, kjt, "d"))])
                    else:
                        p.op(ee, lambda e, hb=hb, j0=j0, w512=w512, bk=bk: (
                                e.tensor_copy(sc_sb[:, hb * T + j0:hb * T + j0 + w512],
                                              psA[:, bk * 512:bk * 512 + w512])
                                if hasattr(e, "tensor_copy") else
                                e.copy(sc_sb[:, hb * T + j0:hb * T + j0 + w512],
                                       psA[:, bk * 512:bk * 512 + w512])),
                             r=[(psA, bk)], w=[(sc_sb, (hb, kjt))])

            def st_soft(qt, h, Lq, ntile):
                hb = h % 2
                pb = pbufA if hb == 0 else pbufB
                rsc = [(sc_sb, (hb, k)) for k in range(ntile)] + [(sc_sb, (hb, ntile - 1, "d"))]
                p.op("vector", lambda e, Lq=Lq, hb=hb: e.reduce_max(
                        sm_sc[:, hb * 4:hb * 4 + 1], sc_sb[:, hb * T:hb * T + Lq],
                        axis=AX.X, negate=True),
                     r=rsc, w=[(sm_sc, (hb, 0))])
                p.op("vector", lambda e, hb=hb: e.tensor_scalar(
                        sm_sc[:, hb * 4 + 1:hb * 4 + 2], sm_sc[:, hb * 4:hb * 4 + 1],
                        0.125, None, ALU.mult),
                     r=[(sm_sc, (hb, 0))], w=[(sm_sc, (hb, 1))])
                p.op("scalar", lambda e, pb=pb, Lq=Lq, hb=hb, qt=qt: e.activation(
                        pb[:, 0:Lq], sc_sb[:, hb * T:hb * T + Lq], AF.Exp,
                        bias=sm_sc[:, hb * 4 + 1:hb * 4 + 2],
                        scale=np.float32(0.125), accum_out=sm_sc[:, hb * 4 + 2:hb * 4 + 3]),
                     r=rsc + [(sm_sc, (hb, 1))],
                     w=[(pb, kc) for kc in range(qt + 1)] + [(sm_sc, (hb, 2))])
                p.op("vector", lambda e, hb=hb: e.tensor_scalar(
                        sm_sc[:, hb * 4 + 2:hb * 4 + 3], sm_sc[:, hb * 4 + 2:hb * 4 + 3],
                        8.0, None, ALU.mult),
                     r=[(sm_sc, (hb, 2))], w=[(sm_sc, (hb, 2))])
                p.op("vector", lambda e, hb=hb: e.reciprocal(
                        sm_sc[:, hb * 4 + 3:hb * 4 + 4], sm_sc[:, hb * 4 + 2:hb * 4 + 3]),
                     r=[(sm_sc, (hb, 2))], w=[(sm_sc, (hb, 3))])
                p.op("vector", lambda e, pb=pb, Lq=Lq, hb=hb, qt=qt: e.tensor_scalar(
                        pb[:, 0:Lq], pb[:, 0:Lq], sm_sc[:, hb * 4 + 3:hb * 4 + 4],
                        None, ALU.mult),
                     r=[(pb, kc) for kc in range(qt + 1)] + [(sm_sc, (hb, 3))],
                     w=[(pb, kc) for kc in range(qt + 1)])

            def st_pt(qt, h):
                hb = h % 2
                pb = pbufA if hb == 0 else pbufB
                ptb = ptbA if hb == 0 else ptbB
                seg = []
                for kc in range(qt + 1):
                    slot = hb * 8 + kc % 8
                    p.op("tensor", lambda e, kc=kc, slot=slot, pb=pb: e.transpose(
                            ptp[:, slot * 128:(slot + 1) * 128],
                            pb[:, kc * 128:(kc + 1) * 128], id_b[:]),
                         r=[(pb, kc), (id_b, None)], w=[(ptp, hb)])
                    seg.append(kc)
                    if len(seg) == 8 or kc == qt:
                        k0 = seg[0]
                        p.op("scalar", lambda e, k0=k0, kc=kc, hb=hb, ptb=ptb: e.copy(
                                ptb[:, k0 * 128:(kc + 1) * 128],
                                ptp[:, hb * 1024 + (k0 % 8) * 128:
                                    hb * 1024 + (k0 % 8) * 128 + (kc + 1 - k0) * 128]),
                             r=[(ptp, hb)],
                             w=[(ptb, k) for k in seg])
                        seg = []

            def st_ctx(qt, h):
                for kc in range(qt + 1):
                    for g in range(2):
                        hh = h - 1 + g
                        ptg = ptbA if g == 0 else ptbB
                        p.op("tensor", lambda e, kc=kc, g=g, hh=hh, ptg=ptg, qt=qt: e.matmul(
                                psB[(g * 64):(g * 64 + 64), g * 512:g * 512 + 128],
                                v_tm[:, kc * D + hh * 64:kc * D + (hh + 1) * 64],
                                ptg[:, kc * 128:(kc + 1) * 128],
                                start=(kc == 0), stop=(kc == qt),
                                tile_position=(0, g * 64)),
                             r=[(v_tm, (kc, hh // 2))] + [(ptg, k) for k in range(qt + 1)],
                             w=[(psB, g)])
                oc = h // 2
                for g in range(2):
                    p.op("scalar", lambda e, oc=oc, qt=qt, g=g: e.copy(
                            zfm[g * 64:(g + 1) * 64,
                                (qt % 2) * 1024 + oc * 128:(qt % 2) * 1024 + (oc + 1) * 128],
                            psB[g * 64:(g + 1) * 64, g * 512:g * 512 + 128]),
                         r=[(psB, g)],
                         w=[(zfm, (qt % 2, oc, g))])

            for qt in range(NCH):
                Lq = (qt + 1) * 128
                ntile = (Lq + 511) // 512
                for i in range(16 + 3):
                    if i < 16:
                        st_scores(qt, i, Lq, ntile)
                    if 0 <= i - 1 < 16:
                        st_soft(qt, i - 1, Lq, ntile)
                    if 0 <= i - 2 < 16:
                        st_pt(qt, i - 2)
                        if (i - 2) % 2 == 1:
                            st_ctx(qt, i - 2)
                if qt % 2 == 1:
                    for dc in range(8):
                        wsl = dc % 2
                        p.dma("gpsimd", prring[:, wsl * D:(wsl + 1) * D],
                              di["proj1"][dc * 128:(dc + 1) * 128, :], None,
                              w=[(prring, wsl)])
                        for q2 in range(2):
                            for half in range(2):
                                p.op("tensor", lambda e, dc=dc, q2=q2, half=half, wsl=wsl: e.matmul(
                                        psA[:, q2 * 1024 + half * 512:q2 * 1024 + (half + 1) * 512],
                                        zfm[:, q2 * 1024 + dc * 128:q2 * 1024 + (dc + 1) * 128],
                                        prring[:, wsl * D + half * 512:wsl * D + (half + 1) * 512],
                                        start=(dc == 0), stop=(dc == 7)),
                                     r=[(zfm, (q2, dc, 0)), (zfm, (q2, dc, 1)), (prring, wsl)],
                                     w=[(psA, q2 * 2 + half)])
                    for q2 in range(2):
                        qq = qt - 1 + q2
                        p.op("vector", lambda e, qq=qq, q2=q2: e.tensor_tensor(
                                x_tm[:, qq * D:(qq + 1) * D], x_tm[:, qq * D:(qq + 1) * D],
                                psA[:, q2 * 1024:(q2 + 1) * 1024], ALU.add),
                             r=[(x_tm, qq), (psA, q2 * 2), (psA, q2 * 2 + 1)],
                             w=[(x_tm, qq)])

        def ffn_pass(layer, ph, xn_fm):
            ne = 9 if layer == 0 else 10
            inter = 512 if layer == 0 else 1408
            nfc = inter // 128
            slices = [(0, 512)] if layer == 0 else [(0, 512), (512, 512), (1024, 384)]
            nsl = len(slices)
            wgd = di["wg0" if layer == 0 else "wg1"]
            wud = di["wu0" if layer == 0 else "wu1"]
            wdd = di["wd0" if layer == 0 else "wd1"]
            wgr = sb(f"wgr{layer}", 128, 2 * 8 * 512, BF16, ph)
            wur = sb(f"wur{layer}", 128, 2 * 8 * 512, BF16, ph)
            wd_sb = sb(f"wd_sb{layer}", 128, nfc * D, BF16, ph)
            hfm = sb(f"hfm{layer}", 128, nfc * T, BF16, ph)
            sil = sb(f"sil{layer}", 128, 2 * 512, F32, ph)
            gps = psum(f"gps{layer}", 128, 2048, ph)
            yps = psum(f"yps{layer}", 128, 2048, ph)

            def load_gu(e_, si, rslot):
                f0, fl = slices[si]
                p.dma("gpsimd",
                      wgr.ap().rearrange("p (s c f) -> p s c f", s=2, c=8)[:, rslot, :, 0:fl],
                      wgd[e_][:, 8 * f0:8 * (f0 + fl)].rearrange("p (c f) -> p c f", c=8),
                      None, w=[(wgr, rslot)])
                p.dma("gpsimd",
                      wur.ap().rearrange("p (s c f) -> p s c f", s=2, c=8)[:, rslot, :, 0:fl],
                      wud[e_][:, 8 * f0:8 * (f0 + fl)].rearrange("p (c f) -> p c f", c=8),
                      None, w=[(wur, rslot)])

            def load_d(e_):
                p.dma("gpsimd", wd_sb[:], wdd[e_], None, w=[(wd_sb, None)])

            NSL4 = max(1, T // 512)
            TSL = min(512, T)
            load_gu(0, 0, 0)
            for e_ in range(ne):
                load_d(e_)
                for (si, (f0, fl)) in enumerate(slices):
                    rslot = (e_ * nsl + si) % 2
                    nxt = e_ * nsl + si + 1
                    if nxt < ne * nsl:
                        load_gu(nxt // nsl, nxt % nsl, nxt % 2)
                    for lo in range(0, fl, 128):
                        fc = f0 // 128 + lo // 128
                        for ts in range(NSL4):
                            gb = ts % 2
                            gp = gps[:, gb * 512:gb * 512 + TSL]
                            up = gps[:, 1024 + gb * 512:1024 + gb * 512 + TSL]
                            for dc in range(8):
                                p.op("tensor", lambda e, dc=dc, ts=ts, lo=lo, rslot=rslot, gp=gp: e.matmul(
                                        gp, wgr[:, rslot * 4096 + dc * 512 + lo:
                                                rslot * 4096 + dc * 512 + lo + 128],
                                        xn_fm[:, dc * T + ts * TSL:dc * T + (ts + 1) * TSL],
                                        start=(dc == 0), stop=(dc == 7)),
                                     r=[(wgr, rslot)] + [(xn_fm, (ts * TSL + j * 128) // 128)
                                                         for j in range(TSL // 128)],
                                     w=[(gps, ("g", gb))])
                                p.op("tensor", lambda e, dc=dc, ts=ts, lo=lo, rslot=rslot, up=up: e.matmul(
                                        up, wur[:, rslot * 4096 + dc * 512 + lo:
                                                rslot * 4096 + dc * 512 + lo + 128],
                                        xn_fm[:, dc * T + ts * TSL:dc * T + (ts + 1) * TSL],
                                        start=(dc == 0), stop=(dc == 7)),
                                     r=[(wur, rslot)] + [(xn_fm, (ts * TSL + j * 128) // 128)
                                                         for j in range(TSL // 128)],
                                     w=[(gps, ("u", gb))])
                            ssl = sil[:, gb * 512:gb * 512 + TSL]
                            p.op("scalar", lambda e, ssl=ssl, gp=gp, gb=gb: e.activation(ssl, gp, AF.Silu),
                                 r=[(gps, ("g", gb))], w=[(sil, gb)])
                            p.op("vector", lambda e, fc=fc, ts=ts, ssl=ssl, up=up, gb=gb: e.tensor_tensor(
                                    hfm[:, fc * T + ts * TSL:fc * T + (ts + 1) * TSL], up, ssl, ALU.mult),
                                 r=[(gps, ("u", gb)), (sil, gb)], w=[(hfm, (fc, ts))])
                for c in range(NCH):
                    yb = c % 2
                    for fc in range(nfc):
                        for half in range(2):
                            p.op("tensor", lambda e, fc=fc, half=half, c=c, yb=yb: e.matmul(
                                    yps[:, yb * 1024 + half * 512:yb * 1024 + (half + 1) * 512],
                                    hfm[:, fc * T + c * 128:fc * T + (c + 1) * 128],
                                    wd_sb[:, fc * D + half * 512:fc * D + (half + 1) * 512],
                                    start=(fc == 0), stop=(fc == nfc - 1)),
                                 r=[(hfm, (fc, (c * 128) // TSL)), (wd_sb, None)],
                                 w=[(yps, yb * 2 + half)])
                    if e_ < 8:
                        p.op("vector", lambda e, c=c, yb=yb, e_=e_: e.scalar_tensor_tensor(
                                x_tm[:, c * D:(c + 1) * D],
                                yps[:, yb * 1024:(yb + 1) * 1024],
                                cw_tm[:, c * 8 + e_:c * 8 + e_ + 1],
                                x_tm[:, c * D:(c + 1) * D],
                                ALU.mult, ALU.add),
                             r=[(x_tm, c), (cw_tm, c), (yps, yb * 2), (yps, yb * 2 + 1)],
                             w=[(x_tm, c)])
                    else:
                        p.op("vector", lambda e, c=c, yb=yb: e.tensor_tensor(
                                x_tm[:, c * D:(c + 1) * D], x_tm[:, c * D:(c + 1) * D],
                                yps[:, yb * 1024:(yb + 1) * 1024], ALU.add),
                             r=[(x_tm, c), (yps, yb * 2), (yps, yb * 2 + 1)],
                             w=[(x_tm, c)])

        try:
            if stop_after == "__stopped__":
                raise _StopBuild()
            for layer in (0, 1):
                xn_scope = PScope()
                xn_fm = sb(f"xn_fm{layer}", 128, 8 * T, BF16, xn_scope)
                with PScope() as ph:
                    ln_pass(layer, ph, xn_fm)
                    p.flush(f"ln{layer}")
                if f"cw{layer}" in taps:
                    a = tap(f"cw{layer}", (T, 8))
                    p.dma("sync", a.rearrange("(c p) o -> p c o", p=128),
                          cw_tm.ap().rearrange("p (c o) -> p c o", c=NCH), f"tapcw{layer}",
                          r=[(cw_tm, c) for c in range(NCH)])
                    p.flush(f"tapcw{layer}")
                with PScope() as ph:
                    ffn_pass(layer, ph, xn_fm)
                    p.flush(f"ffn{layer}")
                if f"xf{layer}" in taps:
                    a = tap(f"xf{layer}", (T, D))
                    p.dma("sync", a.rearrange("(c p) o -> p c o", p=128),
                          x_tm.ap().rearrange("p (c o) -> p c o", c=NCH), f"tapxf{layer}",
                          r=[(x_tm, c) for c in range(NCH)])
                    p.flush(f"tapxf{layer}")
                attn_scope = PScope()
                q_fm = sb(f"q_fm{layer}", 128, 8 * T, BF16, attn_scope)
                k_fm = sb(f"k_fm{layer}", 128, 8 * T, BF16, attn_scope)
                v_tm = sb(f"v_tm{layer}", 128, NCH * D, BF16, attn_scope)
                with PScope() as ph2:
                    qkv_pass(layer, ph2, q_fm, k_fm, v_tm, xn_fm)
                    p.flush(f"qkv{layer}")
                xn_scope.close()
                if layer == 0:
                    rows_scope = PScope()
                    tabi = sb("tabi", 4, T, BF16, rows_scope)
                    tabj = sb("tabj", 4, 2 * T, BF16, rows_scope)
                    r01_scope = PScope()
                    r01f = sb("r01f", 2, T, F32, r01_scope)
                    with PScope() as ph:
                        stats_pass(ph, tabi, tabj, r01f)
                        p.flush("stats")
                    with PScope() as ph:
                        quantum_pass(ph, r01f)
                        p.flush("quantum")
                    r01_scope.close()
                    if "qp" in taps:
                        a = tap("qp", (T, 2))
                        p.dma("sync", a.rearrange("(c p) o -> p c o", p=128),
                              qp_tm.ap().rearrange("p (c o) -> p c o", c=NCH), "tapqp",
                              r=[(qp_tm, c) for c in range(NCH)])
                        p.flush("tapqp")
                if f"v{layer}" in taps:
                    a = tap(f"v{layer}", (T, D))
                    p.dma("gpsimd", a.rearrange("(c p) o -> p c o", p=128),
                          v_tm.ap().rearrange("p (c o) -> p c o", c=NCH), f"tapv{layer}",
                          r=[(v_tm, None)])
                    p.flush(f"tapv{layer}")
                with PScope() as ph2:
                    if layer == 0:
                        attn0_pass(ph2, q_fm, k_fm, v_tm, tabi, tabj)
                    else:
                        attn1_pass(ph2, q_fm, k_fm, v_tm)
                    p.flush(f"attn{layer}")
                attn_scope.close()
                if layer == 0:
                    rows_scope.close()
                if f"x{layer}" in taps:
                    a = tap(f"x{layer}", (T, D))
                    p.dma("sync", a.rearrange("(c p) o -> p c o", p=128),
                          x_tm.ap().rearrange("p (c o) -> p c o", c=NCH), f"tapx{layer}",
                          r=[(x_tm, c) for c in range(NCH)])
                    p.flush(f"tapx{layer}")


        except _StopBuild:
            pass
        p.flush = lambda name=None: _orig_flush(p, name)

        p.dma("sync", out_hbm.rearrange("(c p) o -> p c o", p=128),
              x_tm.ap().rearrange("p (c o) -> p c o", c=NCH), "st",
              r=[(x_tm, c) for c in range(NCH)])
        p.wait_all_dma("sync", "st")
        p.flush("store")

    return nc


def kernel(**inputs):
    w = _prep_weights(inputs)
    x = np.ascontiguousarray(np.asarray(inputs["L0_attn_hidden_states"], np.float32))
    nc = build_program(T=2048)
    in_maps = []
    for b in range(8):
        m = dict(w)
        m["x"] = np.ascontiguousarray(x[b])
        in_maps.append(m)
    res = run_bass_kernel_spmd(nc, in_maps, core_ids=list(range(8)))
    out = np.stack([np.asarray(res.results[b]["out"]) for b in range(8)], axis=0)
    return out.astype(np.float32)


# version 15 (final)
# speedup vs baseline: 1.0752x; 1.0752x over previous
import numpy as np
from contextlib import ExitStack

import concourse.bass as bass
import concourse.mybir as mybir
from concourse.bass_utils import run_bass_kernel_spmd

F32 = mybir.dt.float32
BF16 = mybir.dt.bfloat16
AF = mybir.ActivationFunctionType
ALU = mybir.AluOpType
AX = mybir.AxisListType

ENGINES = ("tensor", "vector", "scalar", "gpsimd", "sync")
NEG = -1.0e30
D = 1024



class Buf:
    __slots__ = ("h", "name", "dep", "is_psum", "eng")

    def __init__(self, h, name, is_psum=False):
        self.h = h
        self.name = name
        self.dep = {}
        self.is_psum = is_psum
        self.eng = {}

    def __getitem__(self, s):
        return self.h[s]

    def ap(self):
        return self.h.ap()


class _Op:
    __slots__ = ("fn", "waits", "incs")

    def __init__(self, fn):
        self.fn = fn
        self.waits = []
        self.incs = []


class Prog:

    def __init__(self, nc, ctx):
        self.nc = nc
        self.ctx = ctx
        self.phase = 0
        self.ops = {e: [] for e in ENGINES}
        self.chan = {}
        self.know = {e: {} for e in ENGINES}
        self.dma_sems = {}
        self.pending = []
        self.nblk = 0

    def _sem(self, name):
        return self.ctx.enter_context(self.nc.semaphore(name))

    def chan_sem(self, pe):
        if pe not in self.chan:
            self.chan[pe] = [self._sem(f"dn_{pe[:2]}"), 0]
        return self.chan[pe]

    def dma_sem(self, name):
        if name not in self.dma_sems:
            self.dma_sems[name] = [self._sem(f"dm_{name}"), 0]
        return self.dma_sems[name]

    def _edges_for(self, reads, writes):
        toks = []
        for (buf, key) in reads:
            for k, st in buf.dep.items():
                if key is None or k is None or k == key:
                    if st[0] is not None:
                        toks.append(st[0])
        for (buf, key) in writes:
            for k, st in buf.dep.items():
                if key is None or k is None or k == key:
                    if st[0] is not None:
                        toks.append(st[0])
                    toks.extend(st[1])
        return toks

    _CLS = {"tensor": "t", "vector": "v", "scalar": "s"}

    def op(self, eng, fn, r=(), w=(), record=True):
        pos = len(self.ops[eng])
        toks = self._edges_for(r, w)
        cls = self._CLS.get(eng)
        mytok_pending = []
        if cls is not None:
            for (buf, key) in tuple(r) + tuple(w):
                if not getattr(buf, "is_psum", False):
                    continue
                for k, st in buf.eng.items():
                    if key is None or k is None or k == key:
                        for c2, tok2 in st.items():
                            if c2 != cls and tok2 is not None:
                                toks.append(tok2)
                mytok_pending.append((buf, key))
        entry = _Op(fn)
        self.ops[eng].append(entry)
        imax = {}
        smax = {}
        for tok in toks:
            if tok[0] == 'i':
                _, pe, ph, p = tok
                if ph < self.phase:
                    continue
                if pe == eng == "tensor":
                    continue
                if imax.get(pe, -1) < p:
                    imax[pe] = p
            elif tok[0] == 'S':
                name = tok[1]
                val = self.dma_sems[name][1]
                if smax.get(name, 0) < val:
                    smax[name] = val
            else:
                _, name, val = tok
                if smax.get(name, 0) < val:
                    smax[name] = val
        for pe, p in imax.items():
            self.pending.append((eng, pos, pe, p))
        for name, val in smax.items():
            if self.know[eng].get(name, 0) < val:
                self.know[eng][name] = val
                entry.waits.append((self.dma_sems[name][0], val))
        tok = ('i', eng, self.phase, pos)
        if cls is not None:
            for (buf, key) in mytok_pending:
                buf.eng.setdefault(key, {})[cls] = tok
        if record:
            for (buf, key) in r:
                buf.dep.setdefault(key, [None, []])[1].append(tok)
            for (buf, key) in w:
                if key is None:
                    buf.dep.clear()
                buf.dep[key] = [tok, []]
        return tok

    def dma(self, qeng, out, in_, sem_name=None, r=(), w=(), transpose=False):
        if sem_name is None:
            assert w, "dma needs sem_name or a write target"
            buf, key = w[0]
            sem_name = f"{buf.name}.{key}"
        sem = self.dma_sem(sem_name)

        if transpose:
            def fn(e, out=out, in_=in_):
                return e.dma_start_transpose(out, in_)
        else:
            def fn(e, out=out, in_=in_):
                return e.dma_start(out=out, in_=in_)

        self.op(qeng, fn, r=r, w=w, record=False)
        self.ops[qeng][-1].incs.append((sem[0], 16))
        sem[1] += 16
        tok_done = ('s', sem_name, sem[1])
        tok_tot = ('S', sem_name)
        for (buf, key) in w:
            if key is None:
                buf.dep.clear()
            buf.dep[key] = [tok_tot, []]
        for (buf, key) in r:
            buf.dep.setdefault(key, [None, []])[1].append(tok_tot)
        return tok_done

    def wait_all_dma(self, eng, sem_name):
        s = self.dma_sems[sem_name]
        h, v = s[0], s[1]
        self.op(eng, lambda e: e.wait_ge(h, v))

    def flush(self, name=None):
        self._last_flush = name
        needed = {}
        for (ce, cpos, pe, p) in self.pending:
            needed.setdefault(pe, set()).add(p)
        tagval = {}
        for pe, ps in needed.items():
            ch = self.chan_sem(pe)
            for pp in sorted(ps):
                ch[1] += 1
                assert not self.ops[pe][pp].incs, (pe, pp)
                self.ops[pe][pp].incs.append((ch[0], 1))
                tagval[(pe, pp)] = ch[1]
        lastwait = {}
        for (ce, cpos, pe, pp) in sorted(self.pending, key=lambda t: (t[0], t[1])):
            v = tagval[(pe, pp)]
            if lastwait.get((pe, ce), 0) >= v:
                continue
            lastwait[(pe, ce)] = v
            self.ops[ce][cpos].waits.append((self.chan_sem(pe)[0], v))

        self.nblk += 1
        with self.nc.Block(name or f"ph{self.nblk}", no_gpsimd_drain=True) as blk:
            for eng in ENGINES:
                entries = self.ops[eng]
                if not entries:
                    continue

                def body(e, entries=entries):
                    for ent in entries:
                        for (s, v) in ent.waits:
                            e.wait_ge(s, v)
                        inst = ent.fn(e)
                        for (s, k) in ent.incs:
                            inst.then_inc(s, k)

                getattr(blk, eng)(body)
        self.ops = {e: [] for e in ENGINES}
        self.pending = []
        self.phase += 1




def _prep_weights(i):
    out = {}
    f32 = np.float32

    def c(a):
        return np.ascontiguousarray(np.asarray(a, dtype=f32))

    def pm_proj(wT):
        a = np.asarray(wT, f32).reshape(8, 128, 4, 256)
        return c(a.transpose(2, 1, 0, 3).reshape(4, 128, 8 * 256))

    out["wqT"] = pm_proj(np.asarray(i["L0_attn_Wq"]).T)
    out["wkT"] = pm_proj(np.asarray(i["L0_attn_Wk"]).T)
    out["wvT"] = pm_proj(np.asarray(i["L0_attn_Wv"]).T)
    out["woPM"] = c(np.asarray(i["L0_attn_Wo"]).T.reshape(8, 128, 1024)
                    .transpose(1, 0, 2).reshape(128, 8192))
    out["qg10"] = c(np.concatenate([np.asarray(i["L0_attn_Wtq"]).T,
                                    np.asarray(i["L0_ffn_gate_w"]).T], axis=1))
    out["bqp10"] = c(np.concatenate([np.asarray(i["L0_attn_btq"]), np.zeros(8, f32)]))[None, :]
    out["bq0"] = c(np.asarray(i["L0_attn_bq"]).reshape(8, 128).T)
    out["bk0"] = c(np.asarray(i["L0_attn_bk"]).reshape(8, 128).T)
    out["bv0_row"] = c(i["L0_attn_bv"])[None, :]
    out["bv0c"] = c(np.asarray(i["L0_attn_bv"]).reshape(8, 128).T)
    out["bo_row"] = c(i["L0_attn_bo"])[None, :]
    out["bfq_row"] = c(i["L0_attn_bfq"])[None, :]
    out["wfqT"] = c(np.asarray(i["L0_attn_Wfq"]).T)
    def pm_df(a, slices):
        e_, d_, f_ = a.shape
        parts = []
        for (f0, fl) in slices:
            b = a[:, :, f0:f0 + fl].reshape(e_, 8, 128, fl)
            parts.append(b.transpose(0, 2, 1, 3).reshape(e_, 128, 8 * fl))
        return c(np.concatenate(parts, axis=2))

    def pm_fd(a):
        e_, f_, o_ = a.shape
        nfc = f_ // 128
        return c(a.reshape(e_, nfc, 128, o_).transpose(0, 2, 1, 3).reshape(e_, 128, nfc * o_))

    wg0a = np.concatenate([np.transpose(np.asarray(i["L0_ffn_wg"]), (0, 2, 1)),
                           np.asarray(i["L0_ffn_sg"]).T[None]], axis=0)
    wu0a = np.concatenate([np.transpose(np.asarray(i["L0_ffn_wu"]), (0, 2, 1)),
                           np.asarray(i["L0_ffn_su"]).T[None]], axis=0)
    wd0a = np.concatenate([np.transpose(np.asarray(i["L0_ffn_wd"]), (0, 2, 1)),
                           np.asarray(i["L0_ffn_sd"]).T[None]], axis=0)
    out["wg0"] = pm_df(wg0a, [(0, 512)])
    out["wu0"] = pm_df(wu0a, [(0, 512)])
    out["wd0"] = pm_fd(wd0a)
    out["wq1T"] = pm_proj(np.asarray(i["L1_attn_Wq"]).reshape(D, D).T)
    out["wk1T"] = pm_proj(np.asarray(i["L1_attn_Wk"]).reshape(D, D).T)
    out["wv1T"] = pm_proj(np.asarray(i["L1_attn_Wv"]).reshape(D, D).T)
    out["bq1"] = c(np.asarray(i["L1_attn_bq"]).reshape(D).reshape(8, 128).T)
    out["bk1"] = c(np.asarray(i["L1_attn_bk"]).reshape(D).reshape(8, 128).T)
    out["bv1_row"] = c(np.asarray(i["L1_attn_bv"]).reshape(D))[None, :]
    out["bv1c"] = c(np.asarray(i["L1_attn_bv"]).reshape(D).reshape(8, 128).T)
    out["proj1"] = c(i["L1_attn_proj"])
    out["gate1T"] = c(np.asarray(i["L1_ffn_gate_w"]).T)
    swg = np.asarray(i["L1_ffn_swg"], f32)
    swu = np.asarray(i["L1_ffn_swu"], f32)
    swd = np.asarray(i["L1_ffn_swd"], f32)
    wg1a = np.concatenate([np.asarray(i["L1_ffn_wg"]),
                           np.stack([swg[:, :1408], swg[:, 1408:]])], axis=0)
    wu1a = np.concatenate([np.asarray(i["L1_ffn_wu"]),
                           np.stack([swu[:, :1408], swu[:, 1408:]])], axis=0)
    wd1a = np.concatenate([np.asarray(i["L1_ffn_wd"]),
                           swd.reshape(2, 1408, 1024)], axis=0)
    L1SL = [(0, 512), (512, 512), (1024, 384)]
    out["wg1"] = pm_df(wg1a, L1SL)
    out["wu1"] = pm_df(wu1a, L1SL)
    out["wd1"] = pm_fd(wd1a)
    out["id_f"] = np.eye(128, dtype=f32)
    out["ones_col"] = np.ones((1, 128), f32)
    out["ones128"] = np.ones((128, 1), f32)
    cb = np.zeros((128, 128), f32)
    cb[np.triu_indices(128, 1)] = NEG
    out["causal"] = cb
    return out


WEIGHT_SHAPES = [
    ("wqT", (4, 128, 2048)), ("wkT", (4, 128, 2048)), ("wvT", (4, 128, 2048)), ("woPM", (128, 8 * D)),
    ("qg10", (D, 10)), ("bqp10", (1, 10)), ("bq0", (128, 8)), ("bk0", (128, 8)),
    ("bv0_row", (1, D)), ("bo_row", (1, D)), ("bfq_row", (1, D)), ("wfqT", (2, D)),
    ("wg0", (9, 128, 4096)), ("wu0", (9, 128, 4096)), ("wd0", (9, 128, 4096)),
    ("wq1T", (4, 128, 2048)), ("wk1T", (4, 128, 2048)), ("wv1T", (4, 128, 2048)),
    ("bq1", (128, 8)), ("bk1", (128, 8)), ("bv1_row", (1, D)),
    ("bv0c", (128, 8)), ("bv1c", (128, 8)),
    ("proj1", (D, D)), ("gate1T", (D, 8)),
    ("wg1", (10, 128, 11264)), ("wu1", (10, 128, 11264)), ("wd1", (10, 128, 11264)),
    ("id_f", (128, 128)), ("ones_col", (1, 128)), ("ones128", (128, 1)),
    ("causal", (128, 128)),
]




def build_program(T=2048, taps=(), stop_after=None):
    NCH = T // 128
    TS = min(512, T)
    NTS = T // TS
    nc = bass.Bass("TRN2", target_bir_lowering=False)

    di = {}
    di["x"] = nc.dram_tensor("x", [T, D], F32, kind="ExternalInput").ap()
    for name, shape in WEIGHT_SHAPES:
        di[name] = nc.dram_tensor(name, list(shape), F32, kind="ExternalInput").ap()
    out_hbm = nc.dram_tensor("out", [T, D], F32, kind="ExternalOutput").ap()
    tap_ap = {}

    def tap(name, shape):
        if name in taps and name not in tap_ap:
            tap_ap[name] = nc.dram_tensor("tap_" + name, list(shape), F32,
                                          kind="ExternalOutput").ap()
        return tap_ap.get(name)

    ctx = ExitStack()
    with ctx:
        p = Prog(nc, ctx)

        SB_TOP = nc.SBUF_PARTITION_SIZE_BYTES

        SB_BASE = (SB_TOP - nc.sbuf_bytes_remaining + 127) & ~63

        class SAlloc:
            def __init__(self):
                self.free_l = [[SB_BASE, SB_TOP, []]]

            def alloc(self, size):
                size = (size + 63) & ~63
                for reg in self.free_l:
                    if reg[1] - reg[0] >= size:
                        off = reg[0]
                        toks = list(reg[2])
                        reg[0] += size
                        if reg[0] >= reg[1]:
                            self.free_l.remove(reg)
                        return off, size, toks
                raise RuntimeError(f"SBUF OOM: need {size}, free={[(r[0], r[1]) for r in self.free_l]}")

            def release(self, off, size, toks):
                self.free_l.append([off, off + size, toks])
                self.free_l.sort(key=lambda r: r[0])
                merged = []
                for reg in self.free_l:
                    if merged and merged[-1][1] == reg[0]:
                        merged[-1][1] = reg[1]
                        merged[-1][2] = merged[-1][2] + reg[2]
                    else:
                        merged.append(reg)
                self.free_l = merged

        salloc = SAlloc()
        _uid = [0]

        class _StopBuild(Exception):
            pass

        _orig_flush = Prog.flush

        def _flush_ck(name=None):
            _orig_flush(p, name)
            if stop_after is not None and name == stop_after:
                raise _StopBuild()
        p.flush = _flush_ck

        class PScope:
            def __init__(self):
                self.bufs = []
                self.es = ExitStack()

            def enter_context(self, cm):
                return self.es.enter_context(cm)

            def register(self, buf, off, size):
                self.bufs.append((buf, off, size))

            def close(self):
                self.es.close()
                for (buf, off, size) in self.bufs:
                    toks = []
                    for st in buf.dep.values():
                        if st[0] is not None:
                            toks.append(st[0])
                        toks.extend(st[1])
                    salloc.release(off, size, toks)
                self.bufs = []

            def __enter__(self):
                return self

            def __exit__(self, *a):
                self.close()
                return False

        perm_scope = PScope()

        def sb(name, part, free, dt, scope=None):
            nbytes = free * (2 if dt == BF16 else 4)
            off, size, toks = salloc.alloc(nbytes)
            _uid[0] += 1
            h = nc.alloc_sbuf_tensor_at(f"s{_uid[0]}_{name}", [part, free], dt, offset=off)
            b = Buf(h, name)
            if toks:
                b.dep[("alias",)] = [None, toks]
            (scope or perm_scope).register(b, off, size)
            return b

        def psum(name, part, free, scope=None, dt=F32):
            _uid[0] += 1
            return Buf((scope or ctx).enter_context(
                nc.psum_tensor(f"p{_uid[0]}_{name}", [part, free], dt)), name,
                is_psum=True)

        x_tm = sb("x_tm", 128, NCH * D, F32)
        qp_tm = sb("qp_tm", 128, NCH * 2, F32)
        g0_tm = sb("g0_tm", 128, NCH * 8, F32)
        cw_tm = sb("cw_tm", 128, NCH * 8, F32)
        id_f = sb("id_f", 128, 128, F32)
        id_b = sb("id_b", 128, 128, BF16)
        ones_col = sb("ones_col", 1, 128, F32)
        ones128 = sb("ones128", 128, 1, F32)
        causal = sb("causal", 128, 128, F32)
        qg10 = sb("qg10", 128, 8 * 10, F32)
        bqp10 = sb("bqp10", 1, 10, F32)
        bcol = sb("bcol", 128, 4 * 8, F32)
        tthr = sb("tthr", 128, 1, F32)
        stat = sb("stat", 1, 32, F32)
        sm_sc = sb("sm_sc", 128, 8, F32)

        p.dma("sync", x_tm.ap().rearrange("p (c o) -> p c o", c=NCH),
              di["x"].rearrange("(c p) o -> p c o", p=128), None,
              w=[(x_tm, c) for c in range(NCH)])
        p.dma("sync", id_f[:], di["id_f"], None, w=[(id_f, None)])
        p.dma("gpsimd", id_b[:], di["id_f"], None, w=[(id_b, None)])
        p.dma("sync", ones_col[:], di["ones_col"], None, w=[(ones_col, None)])
        p.dma("sync", ones128[:], di["ones128"], None, w=[(ones128, None)])
        p.dma("sync", causal[:], di["causal"], None, w=[(causal, None)])
        p.dma("sync", qg10.ap().rearrange("p (c o) -> p c o", c=8),
              di["qg10"].rearrange("(c p) o -> p c o", p=128), None, w=[(qg10, None)])
        p.dma("sync", bqp10[:], di["bqp10"], None, w=[(bqp10, None)])
        p.dma("sync", bcol[:, 0:8], di["bq0"], None, w=[(bcol, 0)])
        p.dma("sync", bcol[:, 8:16], di["bk0"], None, w=[(bcol, 1)])
        p.dma("sync", bcol[:, 16:24], di["bq1"], None, w=[(bcol, 2)])
        p.dma("sync", bcol[:, 24:32], di["bk1"], None, w=[(bcol, 3)])
        try:
            p.flush("load")
        except _StopBuild:
            stop_after = "__stopped__"

        def ln_pass(layer, ph, xn_fm):
            xn_rot = sb(f"xnrot{layer}", 128, 4 * D, F32, ph)
            stg = sb(f"stg{layer}", 128, 4 * D, F32, ph)
            sqs = sb(f"sqs{layer}", 128, 2 * D, F32, ph)
            e8 = sb(f"e8_{layer}", 128, 4 * 8, F32, ph)
            lsc = sb(f"lsc{layer}", 128, 32, F32, ph)
            gsc = sb(f"gsc{layer}", 128, 32, F32, ph)
            xt_ps = psum(f"xtp{layer}", 128, 2 * D, ph)
            qpg_ps = psum(f"qgp{layer}", 128, 1024, ph)
            nqg = 10 if layer == 0 else 8
            if layer == 1:
                p.dma("sync", qg10.ap().rearrange("p (c o) -> p c o", c=8)[:, :, 0:8],
                      di["gate1T"].rearrange("(c p) o -> p c o", p=128), None,
                      w=[(qg10, None)])
            for c in range(NCH):
                par = c % 4
                p2 = c % 2
                L = lambda k, par=par: (lsc, (par, k))
                Ls = lambda k, par=par: lsc[:, par * 8 + k:par * 8 + k + 1]
                xsl = x_tm[:, c * D:(c + 1) * D]
                xnsl = xn_rot[:, par * D:(par + 1) * D]
                stgs = stg[:, par * D:(par + 1) * D]
                p.op("vector", lambda e, xsl=xsl, a=Ls(0), c=c: e.reduce_sum(a, xsl, axis=AX.X, negate=True),
                     r=[(x_tm, c)], w=[L(0)])
                p.op("scalar", lambda e, xsl=xsl, b=Ls(2), c=c, p2=p2: e.activation(
                        sqs[:, p2 * D:(p2 + 1) * D], xsl, AF.Square, accum_out=b),
                     r=[(x_tm, c)], w=[(sqs, p2), L(2)])
                p.op("vector", lambda e, a=Ls(0): e.tensor_scalar(a, a, 1.0 / D, None, ALU.mult),
                     r=[L(0)], w=[L(0)])
                p.op("vector", lambda e, a=Ls(3), b=Ls(0): e.tensor_tensor(a, b, b, ALU.mult),
                     r=[L(0)], w=[L(3)])
                p.op("vector", lambda e, a=Ls(4), b=Ls(2), m2=Ls(3): e.scalar_tensor_tensor(
                        a, b, 1.0 / D, m2, ALU.mult, ALU.subtract),
                     r=[L(2), L(3)], w=[L(4)])
                p.op("scalar", lambda e, a=Ls(5), b=Ls(4): e.activation(a, b, AF.Sqrt, bias=np.float32(1e-5)),
                     r=[L(4)], w=[L(5)])
                p.op("vector", lambda e, a=Ls(6), b=Ls(5): e.reciprocal(a, b),
                     r=[L(5)], w=[L(6)])
                p.op("vector", lambda e, xsl=xsl, xnsl=xnsl, m=Ls(0), rs=Ls(6), c=c: e.tensor_scalar(
                        xnsl, xsl, m, rs, ALU.add, ALU.mult),
                     r=[(x_tm, c), L(0), L(6)], w=[(xn_rot, par)])
                for dc in range(8):
                    p.op("tensor", lambda e, xnsl=xnsl, dc=dc, p2=p2: e.transpose(
                            xt_ps[:, p2 * D + dc * 128:p2 * D + (dc + 1) * 128],
                            xnsl[:, dc * 128:(dc + 1) * 128], id_f[:]),
                         r=[(xn_rot, par), (id_f, None)], w=[(xt_ps, (p2, dc))])
                rd_xt = [(xt_ps, (p2, dc)) for dc in range(8)]
                p.op("vector", lambda e, c=c, p2=p2: e.tensor_copy(
                        xn_fm.ap().rearrange("p (d t) -> p d t", d=8)[:, :, c * 128:(c + 1) * 128],
                        xt_ps.ap().rearrange("p (b d t) -> p b d t", b=2, d=8)[:, p2]),
                     r=rd_xt, w=[(xn_fm, c)])
                p.op("scalar", lambda e, stgs=stgs, p2=p2: e.copy(
                        stgs, xt_ps[:, p2 * D:(p2 + 1) * D]),
                     r=rd_xt, w=[(stg, par)])
                if layer == 0:
                    p.op("vector", lambda e, xsl=xsl, xnsl=xnsl, c=c, par=par: e.tensor_tensor(
                            xsl, xsl, xnsl, ALU.add),
                         r=[(x_tm, c), (xn_rot, par)], w=[(x_tm, c)])
                for dc in range(8):
                    p.op("tensor", lambda e, dc=dc, stgs=stgs, p2=p2: e.matmul(
                            qpg_ps[:, p2 * 512:p2 * 512 + nqg],
                            stgs[:, dc * 128:(dc + 1) * 128],
                            qg10[:, dc * 10:dc * 10 + nqg],
                            start=(dc == 0), stop=(layer == 1 and dc == 7)),
                         r=[(stg, par), (qg10, None)], w=[(qpg_ps, p2)])
                if layer == 0:
                    p.op("tensor", lambda e, p2=p2: e.matmul(
                            qpg_ps[:, p2 * 512:p2 * 512 + nqg], ones_col[:],
                            bqp10[0:1, 0:nqg], start=False, stop=True),
                         r=[(ones_col, None), (bqp10, None)], w=[(qpg_ps, p2)])
                    p.op("vector", lambda e, c=c, p2=p2: e.tensor_copy(
                            qp_tm[:, c * 2:(c + 1) * 2], qpg_ps[:, p2 * 512:p2 * 512 + 2]),
                         r=[(qpg_ps, p2)], w=[(qp_tm, c)])
                    p.op("vector", lambda e, c=c, p2=p2: e.tensor_copy(
                            g0_tm[:, c * 8:(c + 1) * 8], qpg_ps[:, p2 * 512 + 2:p2 * 512 + 10]),
                         r=[(qpg_ps, p2)], w=[(g0_tm, c)])
                else:
                    p.op("vector", lambda e, c=c, p2=p2: e.tensor_copy(
                            g0_tm[:, c * 8:(c + 1) * 8], qpg_ps[:, p2 * 512:p2 * 512 + 8]),
                         r=[(qpg_ps, p2)], w=[(g0_tm, c)])
                G = lambda k, par=par: (gsc, (par, k))
                Gs = lambda k, par=par: gsc[:, par * 8 + k:par * 8 + k + 1]
                gsl = g0_tm[:, c * 8:(c + 1) * 8]
                esl = e8[:, par * 8:(par + 1) * 8]
                csl = cw_tm[:, c * 8:(c + 1) * 8]
                p.op("vector", lambda e, gsl=gsl, a=Gs(0), c=c: e.reduce_max(a, gsl, axis=AX.X, negate=True),
                     r=[(g0_tm, c)], w=[G(0)])
                p.op("scalar", lambda e, gsl=gsl, esl=esl, a=Gs(0), l=Gs(1), c=c, par=par: e.activation(
                        esl, gsl, AF.Exp, bias=a, accum_out=l),
                     r=[(g0_tm, c), G(0)], w=[(e8, par), G(1)])
                p.op("vector", lambda e, a=Gs(2), l=Gs(1): e.reciprocal(a, l),
                     r=[G(1)], w=[G(2)])
                p.op("vector", lambda e, esl=esl, a=Gs(2), par=par: e.tensor_scalar(esl, esl, a, None, ALU.mult),
                     r=[(e8, par), G(2)], w=[(e8, par)])
                p.op("vector", lambda e, esl=esl, a=Gs(3), par=par: e.reduce_max(a, esl, axis=AX.X),
                     r=[(e8, par)], w=[G(3)])
                p.op("vector", lambda e, esl=esl, csl=csl, m1=Gs(3), c=c, par=par: e.tensor_scalar(
                        csl, esl, m1, m1, ALU.is_equal, ALU.mult),
                     r=[(e8, par), G(3)], w=[(cw_tm, c)])
                p.op("vector", lambda e, esl=esl, csl=csl, c=c, par=par: e.tensor_tensor(
                        esl, esl, csl, ALU.subtract),
                     r=[(e8, par), (cw_tm, c)], w=[(e8, par)])
                p.op("vector", lambda e, esl=esl, a=Gs(4), par=par: e.reduce_max(a, esl, axis=AX.X),
                     r=[(e8, par)], w=[G(4)])
                p.op("vector", lambda e, esl=esl, m2=Gs(4), par=par: e.tensor_scalar(
                        esl, esl, m2, m2, ALU.is_equal, ALU.mult),
                     r=[(e8, par), G(4)], w=[(e8, par)])
                p.op("vector", lambda e, esl=esl, csl=csl, c=c, par=par: e.tensor_tensor(
                        csl, csl, esl, ALU.add),
                     r=[(e8, par), (cw_tm, c)], w=[(cw_tm, c)])
                if layer == 1:
                    p.op("vector", lambda e, a=Gs(5), m1=Gs(3), m2=Gs(4): e.tensor_tensor(
                            a, m1, m2, ALU.add),
                         r=[G(3), G(4)], w=[G(5)])
                    p.op("vector", lambda e, a=Gs(5): e.tensor_scalar(a, a, 1e-20, None, ALU.add),
                         r=[G(5)], w=[G(5)])
                    p.op("vector", lambda e, a=Gs(6), b=Gs(5): e.reciprocal(a, b),
                         r=[G(5)], w=[G(6)])
                    p.op("vector", lambda e, csl=csl, a=Gs(6), c=c: e.tensor_scalar(
                            csl, csl, a, None, ALU.mult),
                         r=[(cw_tm, c), G(6)], w=[(cw_tm, c)])

        def stats_pass(ph, tabi, tabj, r01f):
            d9 = sb("d9", 128, NCH * 9, F32, ph)
            rtm = sb("rtm", 128, 4, F32, ph)
            s12 = sb("s12", 128, 12, F32, ph)
            st_ps = psum("st_ps", 1, 16, ph)
            tp_ps = psum("tp_ps", 4, 1024, ph)
            th_ps = psum("th_ps", 128, 8, ph)
            N2 = float(T) * float(T)
            for c in range(NCH):
                xs = qp_tm[:, c * 2:c * 2 + 1]
                ys = qp_tm[:, c * 2 + 1:c * 2 + 2]
                ds = lambda k, c=c: d9[:, c * 9 + k:c * 9 + k + 1]
                K = lambda k, c=c: (d9, (c, k))
                p.op("vector", lambda e, a=ds(4), xs=xs, c=c: e.tensor_tensor(a, xs, xs, ALU.mult),
                     r=[(qp_tm, c)], w=[K(4)])
                p.op("vector", lambda e, a=ds(6), ys=ys, c=c: e.tensor_tensor(a, ys, ys, ALU.mult),
                     r=[(qp_tm, c)], w=[K(6)])
                p.op("vector", lambda e, a=ds(0), b=ds(4), cc=ds(6), c=c: e.tensor_tensor(a, b, cc, ALU.add),
                     r=[K(4), K(6)], w=[K(0)])
                p.op("vector", lambda e, a=ds(1), b=ds(0), c=c: e.tensor_tensor(a, b, b, ALU.mult),
                     r=[K(0)], w=[K(1)])
                p.op("vector", lambda e, a=ds(2), xs=xs, c=c: e.tensor_copy(a, xs),
                     r=[(qp_tm, c)], w=[K(2)])
                p.op("vector", lambda e, a=ds(3), ys=ys, c=c: e.tensor_copy(a, ys),
                     r=[(qp_tm, c)], w=[K(3)])
                p.op("vector", lambda e, a=ds(5), xs=xs, ys=ys, c=c: e.tensor_tensor(a, xs, ys, ALU.mult),
                     r=[(qp_tm, c)], w=[K(5)])
                p.op("vector", lambda e, a=ds(7), b=ds(0), xs=xs, c=c: e.tensor_tensor(a, b, xs, ALU.mult),
                     r=[K(0), (qp_tm, c)], w=[K(7)])
                p.op("vector", lambda e, a=ds(8), b=ds(0), ys=ys, c=c: e.tensor_tensor(a, b, ys, ALU.mult),
                     r=[K(0), (qp_tm, c)], w=[K(8)])
            for c in range(NCH):
                p.op("tensor", lambda e, c=c: e.matmul(
                        st_ps[0:1, 0:9], ones128[:], d9[:, c * 9:(c + 1) * 9],
                        start=(c == 0), stop=(c == NCH - 1)),
                     r=[(ones128, None)] + [(d9, (c, k)) for k in range(9)], w=[(st_ps, 0)])
            p.op("vector", lambda e: e.tensor_copy(stat[0:1, 0:9], st_ps[0:1, 0:9]),
                 r=[(st_ps, 0)], w=[(stat, 0)])
            S = lambda k: stat[0:1, k:k + 1]

            def vop(fn, rks, wk):
                p.op("vector", fn, r=[(stat, 0)] + [(stat, k) for k in rks], w=[(stat, wk)])
            vop(lambda e: e.tensor_tensor(S(9), S(2), S(2), ALU.mult), [], 9)
            vop(lambda e: e.tensor_tensor(S(10), S(3), S(3), ALU.mult), [], 10)
            vop(lambda e: e.tensor_tensor(S(9), S(9), S(10), ALU.add), [9, 10], 9)
            vop(lambda e: e.tensor_scalar(S(11), S(9), -2.0, None, ALU.mult), [9], 11)
            vop(lambda e: e.scalar_tensor_tensor(S(11), S(0), 2.0 * T, S(11), ALU.mult, ALU.add), [11], 11)
            vop(lambda e: e.tensor_tensor(S(9), S(4), S(4), ALU.mult), [], 9)
            vop(lambda e: e.tensor_tensor(S(10), S(5), S(5), ALU.mult), [], 10)
            vop(lambda e: e.scalar_tensor_tensor(S(9), S(10), 2.0, S(9), ALU.mult, ALU.add), [9, 10], 9)
            vop(lambda e: e.tensor_tensor(S(10), S(6), S(6), ALU.mult), [], 10)
            vop(lambda e: e.tensor_tensor(S(9), S(9), S(10), ALU.add), [9, 10], 9)
            vop(lambda e: e.tensor_tensor(S(10), S(7), S(2), ALU.mult), [], 10)
            vop(lambda e: e.tensor_tensor(S(12), S(8), S(3), ALU.mult), [], 12)
            vop(lambda e: e.tensor_tensor(S(10), S(10), S(12), ALU.add), [10, 12], 10)
            vop(lambda e: e.tensor_tensor(S(12), S(0), S(0), ALU.mult), [], 12)
            vop(lambda e: e.scalar_tensor_tensor(S(12), S(1), float(T), S(12), ALU.mult, ALU.add), [12], 12)
            vop(lambda e: e.tensor_scalar(S(12), S(12), 2.0, None, ALU.mult), [12], 12)
            vop(lambda e: e.scalar_tensor_tensor(S(12), S(9), 4.0, S(12), ALU.mult, ALU.add), [9, 12], 12)
            vop(lambda e: e.scalar_tensor_tensor(S(12), S(10), -8.0, S(12), ALU.mult, ALU.add), [10, 12], 12)
            vop(lambda e: e.tensor_scalar(S(13), S(11), 1.0 / N2, None, ALU.mult), [11], 13)
            vop(lambda e: e.tensor_scalar(S(14), S(12), 1.0 / N2, None, ALU.mult), [12], 14)
            vop(lambda e: e.tensor_tensor(S(12), S(13), S(13), ALU.mult), [13], 12)
            vop(lambda e: e.tensor_tensor(S(14), S(14), S(12), ALU.subtract), [12, 14], 14)
            vop(lambda e: e.tensor_scalar(S(14), S(14), N2 / (N2 - 1.0), None, ALU.mult), [14], 14)
            p.op("scalar", lambda e: e.activation(S(15), S(14), AF.Sqrt),
                 r=[(stat, 14)], w=[(stat, 15)])
            vop(lambda e: e.scalar_tensor_tensor(S(15), S(15), 1.25, S(13), ALU.mult, ALU.add), [13, 15], 15)
            p.op("tensor", lambda e: e.matmul(th_ps[:, 0:1], ones_col[:], S(15), start=True, stop=True),
                 r=[(ones_col, None), (stat, 15)], w=[(th_ps, 0)])
            p.op("vector", lambda e: e.tensor_copy(tthr[:], th_ps[:, 0:1]),
                 r=[(th_ps, 0)], w=[(tthr, None)])
            for c in range(NCH):
                sl = c % 2
                xs = qp_tm[:, c * 2:c * 2 + 1]
                ys = qp_tm[:, c * 2 + 1:c * 2 + 2]
                asl = d9[:, c * 9:c * 9 + 1]
                p.op("vector", lambda e, xs=xs, c=c: e.tensor_copy(s12[:, 0:1], xs),
                     r=[(qp_tm, c)], w=[(s12, 0)])
                p.op("vector", lambda e, ys=ys, c=c: e.tensor_copy(s12[:, 1:2], ys),
                     r=[(qp_tm, c)], w=[(s12, 1)])
                p.op("vector", lambda e: e.memset(s12[:, 2:3], 1.0), w=[(s12, 2)])
                p.op("vector", lambda e: e.memset(s12[:, 3:5], 0.0), w=[(s12, 3), (s12, 4)])
                p.op("vector", lambda e: e.memset(s12[:, 4:5], 1.0), w=[(s12, 4)])
                p.op("vector", lambda e: e.memset(s12[:, 5:6], 0.0), w=[(s12, 5)])
                p.op("vector", lambda e, xs=xs, c=c: e.tensor_scalar(s12[:, 6:7], xs, -1.0, None, ALU.mult),
                     r=[(qp_tm, c)], w=[(s12, 6)])
                p.op("vector", lambda e: e.memset(s12[:, 7:9], 0.0), w=[(s12, 7), (s12, 8)])
                p.op("vector", lambda e: e.memset(s12[:, 9:10], 1.0), w=[(s12, 9)])
                p.op("vector", lambda e, ys=ys, c=c: e.tensor_scalar(s12[:, 10:11], ys, -1.0, None, ALU.mult),
                     r=[(qp_tm, c)], w=[(s12, 10)])
                p.op("vector", lambda e: e.memset(s12[:, 11:12], 0.0), w=[(s12, 11)])
                rall = [(s12, k) for k in range(12)]
                for (dst, off, c0) in ((tabi, 0, 0), (tabj, 0, 4), (tabj, T, 8)):
                    p.op("tensor", lambda e, c0=c0, sl=sl: e.transpose(
                            tp_ps[0:4, sl * 512:sl * 512 + 128], s12[:, c0:c0 + 4], id_f[:]),
                         r=rall + [(id_f, None)], w=[(tp_ps, sl)])
                    p.op("vector", lambda e, dst=dst, off=off, c=c, sl=sl: e.tensor_copy(
                            dst[0:4, off + c * 128:off + (c + 1) * 128],
                            tp_ps[0:4, sl * 512:sl * 512 + 128]),
                         r=[(tp_ps, sl)], w=[(dst, (off, c))])
                p.op("scalar", lambda e, asl=asl, a=rtm[:, 0:1], c=c: e.activation(a, asl, AF.Sqrt),
                     r=[(d9, (c, 0))], w=[(rtm, 0)])
                p.op("vector", lambda e, a=rtm[:, 0:1]: e.tensor_scalar(a, a, 1e-12, None, ALU.max),
                     r=[(rtm, 0)], w=[(rtm, 0)])
                p.op("vector", lambda e, a=rtm[:, 1:2], b=rtm[:, 0:1]: e.reciprocal(a, b),
                     r=[(rtm, 0)], w=[(rtm, 1)])
                p.op("vector", lambda e, a=rtm[:, 1:2]: e.tensor_scalar(a, a, 0.1, None, ALU.mult),
                     r=[(rtm, 1)], w=[(rtm, 1)])
                p.op("vector", lambda e, c=c, a=rtm[:, 2:4], sc=rtm[:, 1:2]: e.tensor_scalar(
                        a, qp_tm[:, c * 2:(c + 1) * 2], sc, None, ALU.mult),
                     r=[(qp_tm, c), (rtm, 1)], w=[(rtm, 2)])
                p.op("tensor", lambda e, c=c, sl=sl: e.transpose(
                        tp_ps[0:2, sl * 512:sl * 512 + 128], rtm[:, 2:4], id_f[:]),
                     r=[(rtm, 2), (id_f, None)], w=[(tp_ps, sl)])
                p.op("vector", lambda e, c=c, sl=sl: e.tensor_copy(
                        r01f[0:2, c * 128:(c + 1) * 128], tp_ps[0:2, sl * 512:sl * 512 + 128]),
                     r=[(tp_ps, sl)], w=[(r01f, c)])

        def quantum_pass(ph, r01f):
            wfq_sb = sb("wfq_sb", 2, D, F32, ph)
            b01 = sb("b01", 1, 2 * D, F32, ph)
            qps = psum("qps", 128, 2 * 1024, ph)
            p.dma("sync", wfq_sb[:], di["wfqT"], None, w=[(wfq_sb, None)])
            p.dma("sync", b01[0:1, 0:D], di["bo_row"], None, w=[(b01, 0)])
            p.dma("sync", b01[0:1, D:2 * D], di["bfq_row"], None, w=[(b01, 1)])
            p.op("vector", lambda e: e.scalar_tensor_tensor(
                    b01[0:1, 0:D], b01[0:1, D:2 * D], 0.1, b01[0:1, 0:D], ALU.mult, ALU.add),
                 r=[(b01, 0), (b01, 1)], w=[(b01, 0)])
            for c in range(NCH):
                sl = c % 2
                for half in range(2):
                    p.op("tensor", lambda e, c=c, sl=sl, half=half: e.matmul(
                            qps[:, sl * 1024 + half * 512:sl * 1024 + (half + 1) * 512],
                            r01f[0:2, c * 128:(c + 1) * 128],
                            wfq_sb[:, half * 512:(half + 1) * 512],
                            start=True, stop=False),
                         r=[(r01f, c), (wfq_sb, None)], w=[(qps, (sl, half))])
                    p.op("tensor", lambda e, sl=sl, half=half: e.matmul(
                            qps[:, sl * 1024 + half * 512:sl * 1024 + (half + 1) * 512],
                            ones_col[:], b01[0:1, half * 512:(half + 1) * 512],
                            start=False, stop=True),
                         r=[(ones_col, None), (b01, 0)], w=[(qps, (sl, half))])
                p.op("vector", lambda e, c=c, sl=sl: e.tensor_tensor(
                        x_tm[:, c * D:(c + 1) * D], x_tm[:, c * D:(c + 1) * D],
                        qps[:, sl * 1024:(sl + 1) * 1024], ALU.add),
                     r=[(x_tm, c), (qps, (sl, 0)), (qps, (sl, 1))], w=[(x_tm, c)])

        def qkv_pass(layer, ph, q_fm, k_fm, v_tm, xn_fm):
            wq = di["wqT"] if layer == 0 else di["wq1T"]
            wk = di["wkT"] if layer == 0 else di["wk1T"]
            wv = di["wvT"] if layer == 0 else di["wv1T"]
            bq_k = 0 if layer == 0 else 2
            bk_k = 1 if layer == 0 else 3
            wring = sb(f"wring{layer}", 128, 2 * 2048, BF16, ph)
            bvrow_ = sb(f"bvrow{layer}", 1, D, F32, ph)
            pq = psum(f"pq{layer}", 128, 2048, ph)
            p.dma("sync", bvrow_[:], di["bv0_row" if layer == 0 else "bv1_row"], None,
                  w=[(bvrow_, None)])
            for (mode, dst, wsrc, bk) in (("q", q_fm, wq, bq_k), ("k", k_fm, wk, bk_k),
                                          ("v", v_tm, wv, None)):
                for grp in range(4):
                    slot = grp % 2
                    p.dma("gpsimd", wring[:, slot * 2048:(slot + 1) * 2048],
                          wsrc[grp], None, w=[(wring, slot)])
                    for oc_l in range(2):
                        oc = grp * 2 + oc_l
                        if mode != "v":
                            for dc in range(8):
                                for ts in range(NTS):
                                    p.op("tensor", lambda e, dc=dc, ts=ts, slot=slot, oc_l=oc_l: e.matmul(
                                            pq[:, (ts % 4) * 512:(ts % 4) * 512 + TS],
                                            wring[:, slot * 2048 + dc * 256 + oc_l * 128:
                                                  slot * 2048 + dc * 256 + oc_l * 128 + 128],
                                            xn_fm[:, dc * T + ts * TS:dc * T + (ts + 1) * TS],
                                            start=(dc == 0), stop=(dc == 7)),
                                         r=[(wring, slot)] + [(xn_fm, (ts * TS + j * 128) // 128)
                                                              for j in range(TS // 128)],
                                         w=[(pq, ts % 4)])
                            qpre = (mode == "q" and layer == 1)
                            for ts in range(NTS):
                                if qpre:
                                    fn = (lambda e, dst=dst, oc=oc, ts=ts, bk=bk: e.tensor_scalar(
                                            dst[:, oc * T + ts * TS:oc * T + (ts + 1) * TS],
                                            pq[:, (ts % 4) * 512:(ts % 4) * 512 + TS],
                                            bcol[:, bk * 8 + oc:bk * 8 + oc + 1],
                                            0.125, ALU.add, ALU.mult))
                                else:
                                    fn = (lambda e, dst=dst, oc=oc, ts=ts, bk=bk: e.tensor_scalar(
                                            dst[:, oc * T + ts * TS:oc * T + (ts + 1) * TS],
                                            pq[:, (ts % 4) * 512:(ts % 4) * 512 + TS],
                                            bcol[:, bk * 8 + oc:bk * 8 + oc + 1], None, ALU.add))
                                p.op("vector", fn,
                                     r=[(pq, ts % 4), (bcol, bk)], w=[(dst, (oc, ts))])
                    if mode == "v":
                        for c in range(NCH):
                            pb = c % 4
                            for dc in range(8):
                                p.op("tensor", lambda e, dc=dc, c=c, slot=slot, pb=pb: e.matmul(
                                        pq[:, pb * 512:pb * 512 + 256],
                                        xn_fm[:, dc * T + c * 128:dc * T + (c + 1) * 128],
                                        wring[:, slot * 2048 + dc * 256:slot * 2048 + (dc + 1) * 256],
                                        start=(dc == 0), stop=False),
                                     r=[(xn_fm, c), (wring, slot)], w=[(pq, pb)])
                            p.op("tensor", lambda e, c=c, grp=grp, pb=pb: e.matmul(
                                    pq[:, pb * 512:pb * 512 + 256],
                                    ones_col[:], bvrow_[0:1, grp * 256:(grp + 1) * 256],
                                    start=False, stop=True),
                                 r=[(ones_col, None), (bvrow_, None)], w=[(pq, pb)])
                            p.op("vector", lambda e, c=c, grp=grp, pb=pb: e.tensor_copy(
                                    v_tm[:, c * D + grp * 256:c * D + (grp + 1) * 256],
                                    pq[:, pb * 512:pb * 512 + 256]),
                                 r=[(pq, pb)],
                                 w=[(v_tm, (c, grp * 2)), (v_tm, (c, grp * 2 + 1))])

        def attn0_pass(ph, q_fm, k_fm, v_tm, tabi, tabj):
            wo_sb = sb("wo_sb", 128, 8 * D, BF16, ph)
            p.dma("gpsimd", wo_sb[:], di["woPM"], None, w=[(wo_sb, None)])
            pbuf = sb("pbuf", 128, T, BF16, ph)
            ptb = sb("ptb", 128, T, BF16, ph)
            ctm = sb("ctm", 128, D, BF16, ph)
            cfm = sb("cfm", 128, 8 * 128, BF16, ph)
            ssc = sb("ssc", 128, 2 * 512, F32, ph)
            psA = psum("psA", 128, 2048, ph)
            ddx = psum("ddx", 128, 512, ph)
            psC = psum("psC", 128, 1024, ph)
            ptp = psum("ptp", 128, 1024, ph, dt=BF16)

            def sc_banks(qt):
                ntile = ((qt + 1) * 128 + 511) // 512
                if ntile <= 2:
                    return [(qt % 2) * 2 + k for k in range(ntile)]
                return list(range(ntile))

            def sc_base(qt):
                ntile = ((qt + 1) * 128 + 511) // 512
                return 1024 if (ntile <= 2 and qt % 2 == 1) else 0

            def stA_mm(qt):
                Lq = (qt + 1) * 128
                bks = sc_banks(qt)
                base = sc_base(qt)
                for kjt in range(len(bks)):
                    j0 = kjt * 512
                    w512 = min(512, Lq - j0)
                    scp = psA[:, base + j0:base + j0 + w512]
                    for dc in range(8):
                        p.op("tensor", lambda e, dc=dc, qt=qt, j0=j0, w512=w512, scp=scp: e.matmul(
                                scp, q_fm[:, dc * T + qt * 128:dc * T + (qt + 1) * 128],
                                k_fm[:, dc * T + j0:dc * T + j0 + w512],
                                start=(dc == 0), stop=(dc == 7)),
                             r=[(q_fm, (dc, qt * 128 // TS))] +
                               [(k_fm, (dc, (j0 + jj * 128) // TS)) for jj in range(w512 // 128)],
                             w=[(psA, bks[kjt])])

            def stA_chain(qt):
                Lq = (qt + 1) * 128
                bks = sc_banks(qt)
                base = sc_base(qt)
                for kjt in range(len(bks)):
                    j0 = kjt * 512
                    w512 = min(512, Lq - j0)
                    scp = psA[:, base + j0:base + j0 + w512]
                    bk = bks[kjt]
                    dp = ddx[:, 0:w512]
                    tp2 = 0
                    sA = ssc[:, 0:w512]
                    sB = ssc[:, 512:512 + w512]
                    p.op("tensor", lambda e, qt=qt, j0=j0, w512=w512, dp=dp: e.matmul(
                            dp, tabi[0:4, qt * 128:(qt + 1) * 128],
                            tabj[0:4, T + j0:T + j0 + w512], start=True, stop=True),
                         r=[(tabi, (0, qt))] + [(tabj, (T, j0 // 128 + jj)) for jj in range(w512 // 128)],
                         w=[(ddx, 0)])
                    p.op("scalar", lambda e, sB=sB, dp=dp: e.activation(sB, dp, AF.Square),
                         r=[(ddx, 0)], w=[(ssc, (tp2, 1))])
                    p.op("tensor", lambda e, qt=qt, j0=j0, w512=w512, dp=dp: e.matmul(
                            dp, tabi[0:4, qt * 128:(qt + 1) * 128],
                            tabj[0:4, j0:j0 + w512], start=True, stop=True),
                         r=[(tabi, (0, qt))] + [(tabj, (0, j0 // 128 + jj)) for jj in range(w512 // 128)],
                         w=[(ddx, 0)])
                    p.op("scalar", lambda e, sA=sA, dp=dp: e.activation(sA, dp, AF.Square),
                         r=[(ddx, 0)], w=[(ssc, (tp2, 0))])
                    p.op("vector", lambda e, sA=sA, sB=sB: e.tensor_tensor(sB, sB, sA, ALU.add),
                         r=[(ssc, (tp2, 0)), (ssc, (tp2, 1))], w=[(ssc, (tp2, 1))])
                    p.op("vector", lambda e, sA=sA, sB=sB: e.tensor_scalar(
                            sA, sB, tthr[:], NEG, ALU.is_gt, ALU.mult),
                         r=[(ssc, (tp2, 1)), (tthr, None)], w=[(ssc, (tp2, 0))])
                    p.op("vector", lambda e, scp=scp, sA=sA, bk=bk: e.tensor_tensor(
                            scp, scp, sA, ALU.add),
                         r=[(psA, bk), (ssc, (tp2, 0))], w=[(psA, bk)])
                    p.op("scalar", lambda e, sB=sB: e.activation(sB, sB, AF.Sqrt, bias=np.float32(1e-30)),
                         r=[(ssc, (tp2, 1))], w=[(ssc, (tp2, 1))])
                    p.op("vector", lambda e, sB=sB, dp=dp: e.tensor_tensor(sB, sB, dp, ALU.add),
                         r=[(ssc, (tp2, 1)), (ddx, 0)], w=[(ssc, (tp2, 1))])
                    p.op("vector", lambda e, sB=sB: e.tensor_scalar(
                            sB, sB, 0.0, NEG, ALU.is_le, ALU.mult),
                         r=[(ssc, (tp2, 1))], w=[(ssc, (tp2, 1))])
                    p.op("vector", lambda e, scp=scp, sB=sB, bk=bk: e.tensor_tensor(
                            scp, scp, sB, ALU.add),
                         r=[(psA, bk), (ssc, (tp2, 1))], w=[(psA, bk)])
                dbk = bks[-1]
                dcol = base + qt * 128
                p.op("vector", lambda e, dcol=dcol, dbk=dbk: e.tensor_tensor(
                        psA[:, dcol:dcol + 128], psA[:, dcol:dcol + 128],
                        causal[:], ALU.add),
                     r=[(psA, dbk), (causal, None)], w=[(psA, dbk)])

            def stB(qt):
                Lq = (qt + 1) * 128
                bks = sc_banks(qt)
                base = sc_base(qt)
                qb = qt % 2
                rmask = [(psA, b) for b in bks]
                p.op("vector", lambda e, Lq=Lq, base=base, qb=qb: e.reduce_max(
                        sm_sc[:, qb * 4:qb * 4 + 1], psA[:, base:base + Lq],
                        axis=AX.X, negate=True),
                     r=rmask, w=[(sm_sc, (qb, 0))])
                p.op("vector", lambda e, qb=qb: e.tensor_scalar(
                        sm_sc[:, qb * 4 + 1:qb * 4 + 2], sm_sc[:, qb * 4:qb * 4 + 1],
                        1.0 / 32.0, None, ALU.mult),
                     r=[(sm_sc, (qb, 0))], w=[(sm_sc, (qb, 1))])
                p.op("scalar", lambda e, Lq=Lq, base=base, qb=qb, qt=qt: e.activation(
                        pbuf[:, 0:Lq], psA[:, base:base + Lq], AF.Exp,
                        bias=sm_sc[:, qb * 4 + 1:qb * 4 + 2],
                        scale=np.float32(1.0 / 32.0), accum_out=sm_sc[:, qb * 4 + 2:qb * 4 + 3]),
                     r=rmask + [(sm_sc, (qb, 1))],
                     w=[(pbuf, kc) for kc in range(qt + 1)] + [(sm_sc, (qb, 2))])
                p.op("vector", lambda e, qb=qb: e.reciprocal(
                        sm_sc[:, qb * 4 + 3:qb * 4 + 4], sm_sc[:, qb * 4 + 2:qb * 4 + 3]),
                     r=[(sm_sc, (qb, 2))], w=[(sm_sc, (qb, 3))])

            def stC(qt):
                qb = qt % 2
                seg = []
                for kc in range(qt + 1):
                    slot = kc % 8
                    p.op("tensor", lambda e, kc=kc, slot=slot: e.transpose(
                            ptp[:, slot * 128:(slot + 1) * 128],
                            pbuf[:, kc * 128:(kc + 1) * 128], id_b[:]),
                         r=[(pbuf, kc), (id_b, None)], w=[(ptp, 0)])
                    seg.append(kc)
                    if len(seg) == 8 or kc == qt:
                        k0 = seg[0]
                        p.op("scalar", lambda e, k0=k0, kc=kc: e.copy(
                                ptb[:, k0 * 128:(kc + 1) * 128],
                                ptp[:, 0:(kc + 1 - k0) * 128]),
                             r=[(ptp, 0)], w=[(ptb, k) for k in seg])
                        seg = []
                for kc in range(qt + 1):
                    for half in range(2):
                        p.op("tensor", lambda e, kc=kc, half=half, qt=qt: e.matmul(
                                psC[:, half * 512:(half + 1) * 512],
                                ptb[:, kc * 128:(kc + 1) * 128],
                                v_tm[:, kc * D + half * 512:kc * D + (half + 1) * 512],
                                start=(kc == 0), stop=(kc == qt)),
                             r=[(ptb, kc)] + [(v_tm, (kc, oo)) for oo in range(8)],
                             w=[(psC, half)])
                p.op("vector", lambda e, qb=qb: e.tensor_scalar(
                        ctm[:], psC[:, 0:1024],
                        sm_sc[:, qb * 4 + 3:qb * 4 + 4], None, ALU.mult),
                     r=[(psC, 0), (psC, 1), (sm_sc, (qb, 3))], w=[(ctm, None)])

            def stD(qt):
                qb = qt % 2
                for dc in range(8):
                    p.op("tensor", lambda e, dc=dc: e.transpose(
                            ptp[:, dc * 128:(dc + 1) * 128],
                            ctm[:, dc * 128:(dc + 1) * 128], id_b[:]),
                         r=[(ctm, None), (id_b, None)], w=[(ptp, 0)])
                p.op("scalar", lambda e: e.copy(cfm[:], ptp[:, 0:1024]),
                     r=[(ptp, 0)], w=[(cfm, dc) for dc in range(8)])
                for dc in range(8):
                    for half in range(2):
                        p.op("tensor", lambda e, dc=dc, half=half: e.matmul(
                                psC[:, half * 512:(half + 1) * 512],
                                cfm[:, dc * 128:(dc + 1) * 128],
                                wo_sb[:, dc * D + half * 512:dc * D + (half + 1) * 512],
                                start=(dc == 0), stop=(dc == 7)),
                             r=[(cfm, dc), (wo_sb, None)],
                             w=[(psC, half)])
                p.op("vector", lambda e, qt=qt: e.tensor_tensor(
                        x_tm[:, qt * D:(qt + 1) * D], x_tm[:, qt * D:(qt + 1) * D],
                        psC[:, 0:1024], ALU.add),
                     r=[(x_tm, qt), (psC, 0), (psC, 1)],
                     w=[(x_tm, qt)])

            for i in range(NCH + 2):
                if i >= 2:
                    stC(i - 2)
                if 0 <= i - 1 < NCH:
                    stB(i - 1)
                if i < NCH:
                    stA_mm(i)
                if i >= 2:
                    stD(i - 2)
                if i < NCH:
                    stA_chain(i)

        def attn1_pass(ph, q_fm, k_fm, v_tm):
            sc_sb = sb("sc_sb", 128, 2 * T, F32, ph)
            pbufA = sb("pbufA", 128, T, BF16, ph)
            pbufB = sb("pbufB", 128, T, BF16, ph)
            ptbA = sb("ptbA", 128, T, BF16, ph)
            ptbB = sb("ptbB", 128, T, BF16, ph)
            zfm = sb("zfm", 128, 2 * 8 * 128, BF16, ph)
            prring = sb("prring", 128, 2 * D, BF16, ph)
            psA = psum("psA1", 128, 2048, ph)
            psB = psum("psB1", 128, 1024, ph)
            ptp = psum("ptp1", 128, 2048, ph, dt=BF16)

            def st_scores(qt, h, Lq, ntile):
                hb = h % 2
                oc, prt = h // 2, (h % 2) * 64
                for kjt in range(ntile):
                    j0 = kjt * 512
                    w512 = min(512, Lq - j0)
                    bk = (hb * 2 + kjt) % 4
                    p.op("tensor", lambda e, oc=oc, prt=prt, qt=qt, j0=j0, w512=w512, bk=bk: e.matmul(
                            psA[:, bk * 512:bk * 512 + w512],
                            q_fm[prt:prt + 64, oc * T + qt * 128:oc * T + (qt + 1) * 128],
                            k_fm[prt:prt + 64, oc * T + j0:oc * T + j0 + w512],
                            start=True, stop=True),
                         r=[(q_fm, (oc, qt * 128 // TS))] +
                           [(k_fm, (oc, (j0 + jj * 128) // TS)) for jj in range(w512 // 128)],
                         w=[(psA, bk)])
                    wdiag = (j0 + w512 == Lq)
                    ee = "vector" if kjt % 2 == 0 else "scalar"
                    if wdiag:
                        if w512 > 128:
                            p.op(ee, lambda e, hb=hb, j0=j0, w512=w512, bk=bk: (
                                    e.tensor_copy(sc_sb[:, hb * T + j0:hb * T + j0 + w512 - 128],
                                                  psA[:, bk * 512:bk * 512 + w512 - 128])
                                    if hasattr(e, "tensor_copy") else
                                    e.copy(sc_sb[:, hb * T + j0:hb * T + j0 + w512 - 128],
                                           psA[:, bk * 512:bk * 512 + w512 - 128])),
                                 r=[(psA, bk)], w=[(sc_sb, (hb, kjt))])
                        p.op("vector", lambda e, hb=hb, j0=j0, qt=qt, bk=bk: e.tensor_tensor(
                                sc_sb[:, hb * T + qt * 128:hb * T + (qt + 1) * 128],
                                psA[:, bk * 512 + (qt * 128 - j0):bk * 512 + (qt * 128 - j0) + 128],
                                causal[:], ALU.add),
                             r=[(psA, bk), (causal, None)], w=[(sc_sb, (hb, kjt, "d"))])
                    else:
                        p.op(ee, lambda e, hb=hb, j0=j0, w512=w512, bk=bk: (
                                e.tensor_copy(sc_sb[:, hb * T + j0:hb * T + j0 + w512],
                                              psA[:, bk * 512:bk * 512 + w512])
                                if hasattr(e, "tensor_copy") else
                                e.copy(sc_sb[:, hb * T + j0:hb * T + j0 + w512],
                                       psA[:, bk * 512:bk * 512 + w512])),
                             r=[(psA, bk)], w=[(sc_sb, (hb, kjt))])

            def st_soft(qt, h, Lq, ntile):
                hb = h % 2
                pb = pbufA if hb == 0 else pbufB
                rsc = [(sc_sb, (hb, k)) for k in range(ntile)] + [(sc_sb, (hb, ntile - 1, "d"))]
                p.op("vector", lambda e, Lq=Lq, hb=hb: e.reduce_max(
                        sm_sc[:, hb * 4:hb * 4 + 1], sc_sb[:, hb * T:hb * T + Lq],
                        axis=AX.X, negate=True),
                     r=rsc, w=[(sm_sc, (hb, 0))])
                p.op("scalar", lambda e, pb=pb, Lq=Lq, hb=hb, qt=qt: e.activation(
                        pb[:, 0:Lq], sc_sb[:, hb * T:hb * T + Lq], AF.Exp,
                        bias=sm_sc[:, hb * 4:hb * 4 + 1],
                        scale=np.float32(1.0), accum_out=sm_sc[:, hb * 4 + 2:hb * 4 + 3]),
                     r=rsc + [(sm_sc, (hb, 0))],
                     w=[(pb, kc) for kc in range(qt + 1)] + [(sm_sc, (hb, 2))])
                p.op("vector", lambda e, hb=hb: e.reciprocal(
                        sm_sc[:, hb * 4 + 3:hb * 4 + 4], sm_sc[:, hb * 4 + 2:hb * 4 + 3]),
                     r=[(sm_sc, (hb, 2))], w=[(sm_sc, (hb, 3))])
                p.op("vector", lambda e, pb=pb, Lq=Lq, hb=hb, qt=qt: e.tensor_scalar(
                        pb[:, 0:Lq], pb[:, 0:Lq], sm_sc[:, hb * 4 + 3:hb * 4 + 4],
                        0.125, ALU.mult, ALU.mult),
                     r=[(pb, kc) for kc in range(qt + 1)] + [(sm_sc, (hb, 3))],
                     w=[(pb, kc) for kc in range(qt + 1)])

            def st_pt(qt, h):
                hb = h % 2
                pb = pbufA if hb == 0 else pbufB
                ptb = ptbA if hb == 0 else ptbB
                seg = []
                for kc in range(qt + 1):
                    slot = hb * 8 + kc % 8
                    p.op("tensor", lambda e, kc=kc, slot=slot, pb=pb: e.transpose(
                            ptp[:, slot * 128:(slot + 1) * 128],
                            pb[:, kc * 128:(kc + 1) * 128], id_b[:]),
                         r=[(pb, kc), (id_b, None)], w=[(ptp, hb)])
                    seg.append(kc)
                    if len(seg) == 8 or kc == qt:
                        k0 = seg[0]
                        p.op("scalar", lambda e, k0=k0, kc=kc, hb=hb, ptb=ptb: e.copy(
                                ptb[:, k0 * 128:(kc + 1) * 128],
                                ptp[:, hb * 1024 + (k0 % 8) * 128:
                                    hb * 1024 + (k0 % 8) * 128 + (kc + 1 - k0) * 128]),
                             r=[(ptp, hb)],
                             w=[(ptb, k) for k in seg])
                        seg = []

            def st_ctx(qt, h):
                for kc in range(qt + 1):
                    for g in range(2):
                        hh = h - 1 + g
                        ptg = ptbA if g == 0 else ptbB
                        p.op("tensor", lambda e, kc=kc, g=g, hh=hh, ptg=ptg, qt=qt: e.matmul(
                                psB[(g * 64):(g * 64 + 64), g * 512:g * 512 + 128],
                                v_tm[:, kc * D + hh * 64:kc * D + (hh + 1) * 64],
                                ptg[:, kc * 128:(kc + 1) * 128],
                                start=(kc == 0), stop=(kc == qt),
                                tile_position=(0, g * 64)),
                             r=[(v_tm, (kc, hh // 2))] + [(ptg, k) for k in range(qt + 1)],
                             w=[(psB, g)])
                oc = h // 2
                for g in range(2):
                    p.op("scalar", lambda e, oc=oc, qt=qt, g=g: e.copy(
                            zfm[g * 64:(g + 1) * 64,
                                (qt % 2) * 1024 + oc * 128:(qt % 2) * 1024 + (oc + 1) * 128],
                            psB[g * 64:(g + 1) * 64, g * 512:g * 512 + 128]),
                         r=[(psB, g)],
                         w=[(zfm, (qt % 2, oc, g))])

            for qt in range(NCH):
                Lq = (qt + 1) * 128
                ntile = (Lq + 511) // 512
                for i in range(16 + 3):
                    if i < 16:
                        st_scores(qt, i, Lq, ntile)
                    if 0 <= i - 1 < 16:
                        st_soft(qt, i - 1, Lq, ntile)
                    if 0 <= i - 2 < 16:
                        st_pt(qt, i - 2)
                        if (i - 2) % 2 == 1:
                            st_ctx(qt, i - 2)
                if qt % 2 == 1:
                    for dc in range(8):
                        wsl = dc % 2
                        p.dma("gpsimd", prring[:, wsl * D:(wsl + 1) * D],
                              di["proj1"][dc * 128:(dc + 1) * 128, :], None,
                              w=[(prring, wsl)])
                        for q2 in range(2):
                            for half in range(2):
                                p.op("tensor", lambda e, dc=dc, q2=q2, half=half, wsl=wsl: e.matmul(
                                        psA[:, q2 * 1024 + half * 512:q2 * 1024 + (half + 1) * 512],
                                        zfm[:, q2 * 1024 + dc * 128:q2 * 1024 + (dc + 1) * 128],
                                        prring[:, wsl * D + half * 512:wsl * D + (half + 1) * 512],
                                        start=(dc == 0), stop=(dc == 7)),
                                     r=[(zfm, (q2, dc, 0)), (zfm, (q2, dc, 1)), (prring, wsl)],
                                     w=[(psA, q2 * 2 + half)])
                    for q2 in range(2):
                        qq = qt - 1 + q2
                        p.op("vector", lambda e, qq=qq, q2=q2: e.tensor_tensor(
                                x_tm[:, qq * D:(qq + 1) * D], x_tm[:, qq * D:(qq + 1) * D],
                                psA[:, q2 * 1024:(q2 + 1) * 1024], ALU.add),
                             r=[(x_tm, qq), (psA, q2 * 2), (psA, q2 * 2 + 1)],
                             w=[(x_tm, qq)])

        def ffn_pass(layer, ph, xn_fm):
            ne = 9 if layer == 0 else 10
            inter = 512 if layer == 0 else 1408
            nfc = inter // 128
            slices = [(0, 512)] if layer == 0 else [(0, 512), (512, 512), (1024, 384)]
            nsl = len(slices)
            wgd = di["wg0" if layer == 0 else "wg1"]
            wud = di["wu0" if layer == 0 else "wu1"]
            wdd = di["wd0" if layer == 0 else "wd1"]
            wgr = sb(f"wgr{layer}", 128, 2 * 8 * 512, BF16, ph)
            wur = sb(f"wur{layer}", 128, 2 * 8 * 512, BF16, ph)
            wd_sb = sb(f"wd_sb{layer}", 128, nfc * D, BF16, ph)
            hfm = sb(f"hfm{layer}", 128, nfc * T, BF16, ph)
            sil = sb(f"sil{layer}", 128, 2 * 512, F32, ph)
            gps = psum(f"gps{layer}", 128, 2048, ph)
            yps = psum(f"yps{layer}", 128, 2048, ph)

            def load_gu(e_, si, rslot):
                f0, fl = slices[si]
                p.dma("gpsimd",
                      wgr.ap().rearrange("p (s c f) -> p s c f", s=2, c=8)[:, rslot, :, 0:fl],
                      wgd[e_][:, 8 * f0:8 * (f0 + fl)].rearrange("p (c f) -> p c f", c=8),
                      None, w=[(wgr, rslot)])
                p.dma("gpsimd",
                      wur.ap().rearrange("p (s c f) -> p s c f", s=2, c=8)[:, rslot, :, 0:fl],
                      wud[e_][:, 8 * f0:8 * (f0 + fl)].rearrange("p (c f) -> p c f", c=8),
                      None, w=[(wur, rslot)])

            def load_d(e_):
                p.dma("gpsimd", wd_sb[:], wdd[e_], None, w=[(wd_sb, None)])

            NSL4 = max(1, T // 512)
            TSL = min(512, T)
            load_gu(0, 0, 0)
            for e_ in range(ne):
                load_d(e_)
                for (si, (f0, fl)) in enumerate(slices):
                    rslot = (e_ * nsl + si) % 2
                    nxt = e_ * nsl + si + 1
                    if nxt < ne * nsl:
                        load_gu(nxt // nsl, nxt % nsl, nxt % 2)
                    for lo in range(0, fl, 128):
                        fc = f0 // 128 + lo // 128
                        for ts in range(NSL4):
                            gb = ts % 2
                            gp = gps[:, gb * 512:gb * 512 + TSL]
                            up = gps[:, 1024 + gb * 512:1024 + gb * 512 + TSL]
                            for dc in range(8):
                                p.op("tensor", lambda e, dc=dc, ts=ts, lo=lo, rslot=rslot, gp=gp: e.matmul(
                                        gp, wgr[:, rslot * 4096 + dc * 512 + lo:
                                                rslot * 4096 + dc * 512 + lo + 128],
                                        xn_fm[:, dc * T + ts * TSL:dc * T + (ts + 1) * TSL],
                                        start=(dc == 0), stop=(dc == 7)),
                                     r=[(wgr, rslot)] + [(xn_fm, (ts * TSL + j * 128) // 128)
                                                         for j in range(TSL // 128)],
                                     w=[(gps, ("g", gb))])
                                p.op("tensor", lambda e, dc=dc, ts=ts, lo=lo, rslot=rslot, up=up: e.matmul(
                                        up, wur[:, rslot * 4096 + dc * 512 + lo:
                                                rslot * 4096 + dc * 512 + lo + 128],
                                        xn_fm[:, dc * T + ts * TSL:dc * T + (ts + 1) * TSL],
                                        start=(dc == 0), stop=(dc == 7)),
                                     r=[(wur, rslot)] + [(xn_fm, (ts * TSL + j * 128) // 128)
                                                         for j in range(TSL // 128)],
                                     w=[(gps, ("u", gb))])
                            ssl = sil[:, gb * 512:gb * 512 + TSL]
                            p.op("scalar", lambda e, ssl=ssl, gp=gp, gb=gb: e.activation(ssl, gp, AF.Silu),
                                 r=[(gps, ("g", gb))], w=[(sil, gb)])
                            p.op("vector", lambda e, fc=fc, ts=ts, ssl=ssl, up=up, gb=gb: e.tensor_tensor(
                                    hfm[:, fc * T + ts * TSL:fc * T + (ts + 1) * TSL], up, ssl, ALU.mult),
                                 r=[(gps, ("u", gb)), (sil, gb)], w=[(hfm, (fc, ts))])
                for c in range(NCH):
                    yb = c % 2
                    for fc in range(nfc):
                        for half in range(2):
                            p.op("tensor", lambda e, fc=fc, half=half, c=c, yb=yb: e.matmul(
                                    yps[:, yb * 1024 + half * 512:yb * 1024 + (half + 1) * 512],
                                    hfm[:, fc * T + c * 128:fc * T + (c + 1) * 128],
                                    wd_sb[:, fc * D + half * 512:fc * D + (half + 1) * 512],
                                    start=(fc == 0), stop=(fc == nfc - 1)),
                                 r=[(hfm, (fc, (c * 128) // TSL)), (wd_sb, None)],
                                 w=[(yps, yb * 2 + half)])
                    if e_ < 8:
                        p.op("vector", lambda e, c=c, yb=yb, e_=e_: e.scalar_tensor_tensor(
                                x_tm[:, c * D:(c + 1) * D],
                                yps[:, yb * 1024:(yb + 1) * 1024],
                                cw_tm[:, c * 8 + e_:c * 8 + e_ + 1],
                                x_tm[:, c * D:(c + 1) * D],
                                ALU.mult, ALU.add),
                             r=[(x_tm, c), (cw_tm, c), (yps, yb * 2), (yps, yb * 2 + 1)],
                             w=[(x_tm, c)])
                    else:
                        p.op("vector", lambda e, c=c, yb=yb: e.tensor_tensor(
                                x_tm[:, c * D:(c + 1) * D], x_tm[:, c * D:(c + 1) * D],
                                yps[:, yb * 1024:(yb + 1) * 1024], ALU.add),
                             r=[(x_tm, c), (yps, yb * 2), (yps, yb * 2 + 1)],
                             w=[(x_tm, c)])

        try:
            if stop_after == "__stopped__":
                raise _StopBuild()
            for layer in (0, 1):
                xn_scope = PScope()
                xn_fm = sb(f"xn_fm{layer}", 128, 8 * T, BF16, xn_scope)
                with PScope() as ph:
                    ln_pass(layer, ph, xn_fm)
                    p.flush(f"ln{layer}")
                if f"cw{layer}" in taps:
                    a = tap(f"cw{layer}", (T, 8))
                    p.dma("sync", a.rearrange("(c p) o -> p c o", p=128),
                          cw_tm.ap().rearrange("p (c o) -> p c o", c=NCH), f"tapcw{layer}",
                          r=[(cw_tm, c) for c in range(NCH)])
                    p.flush(f"tapcw{layer}")
                with PScope() as ph:
                    ffn_pass(layer, ph, xn_fm)
                    p.flush(f"ffn{layer}")
                if f"xf{layer}" in taps:
                    a = tap(f"xf{layer}", (T, D))
                    p.dma("sync", a.rearrange("(c p) o -> p c o", p=128),
                          x_tm.ap().rearrange("p (c o) -> p c o", c=NCH), f"tapxf{layer}",
                          r=[(x_tm, c) for c in range(NCH)])
                    p.flush(f"tapxf{layer}")
                attn_scope = PScope()
                q_fm = sb(f"q_fm{layer}", 128, 8 * T, BF16, attn_scope)
                k_fm = sb(f"k_fm{layer}", 128, 8 * T, BF16, attn_scope)
                v_tm = sb(f"v_tm{layer}", 128, NCH * D, BF16, attn_scope)
                with PScope() as ph2:
                    qkv_pass(layer, ph2, q_fm, k_fm, v_tm, xn_fm)
                    p.flush(f"qkv{layer}")
                xn_scope.close()
                if layer == 0:
                    rows_scope = PScope()
                    tabi = sb("tabi", 4, T, BF16, rows_scope)
                    tabj = sb("tabj", 4, 2 * T, BF16, rows_scope)
                    r01_scope = PScope()
                    r01f = sb("r01f", 2, T, F32, r01_scope)
                    with PScope() as ph:
                        stats_pass(ph, tabi, tabj, r01f)
                        p.flush("stats")
                    with PScope() as ph:
                        quantum_pass(ph, r01f)
                        p.flush("quantum")
                    r01_scope.close()
                    if "qp" in taps:
                        a = tap("qp", (T, 2))
                        p.dma("sync", a.rearrange("(c p) o -> p c o", p=128),
                              qp_tm.ap().rearrange("p (c o) -> p c o", c=NCH), "tapqp",
                              r=[(qp_tm, c) for c in range(NCH)])
                        p.flush("tapqp")
                if f"v{layer}" in taps:
                    a = tap(f"v{layer}", (T, D))
                    p.dma("gpsimd", a.rearrange("(c p) o -> p c o", p=128),
                          v_tm.ap().rearrange("p (c o) -> p c o", c=NCH), f"tapv{layer}",
                          r=[(v_tm, None)])
                    p.flush(f"tapv{layer}")
                with PScope() as ph2:
                    if layer == 0:
                        attn0_pass(ph2, q_fm, k_fm, v_tm, tabi, tabj)
                    else:
                        attn1_pass(ph2, q_fm, k_fm, v_tm)
                    p.flush(f"attn{layer}")
                attn_scope.close()
                if layer == 0:
                    rows_scope.close()
                if f"x{layer}" in taps:
                    a = tap(f"x{layer}", (T, D))
                    p.dma("sync", a.rearrange("(c p) o -> p c o", p=128),
                          x_tm.ap().rearrange("p (c o) -> p c o", c=NCH), f"tapx{layer}",
                          r=[(x_tm, c) for c in range(NCH)])
                    p.flush(f"tapx{layer}")


        except _StopBuild:
            pass
        p.flush = lambda name=None: _orig_flush(p, name)

        p.dma("sync", out_hbm.rearrange("(c p) o -> p c o", p=128),
              x_tm.ap().rearrange("p (c o) -> p c o", c=NCH), "st",
              r=[(x_tm, c) for c in range(NCH)])
        p.wait_all_dma("sync", "st")
        p.flush("store")

    return nc


def kernel(**inputs):
    w = _prep_weights(inputs)
    x = np.ascontiguousarray(np.asarray(inputs["L0_attn_hidden_states"], np.float32))
    nc = build_program(T=2048)
    in_maps = []
    for b in range(8):
        m = dict(w)
        m["x"] = np.ascontiguousarray(x[b])
        in_maps.append(m)
    res = run_bass_kernel_spmd(nc, in_maps, core_ids=list(range(8)))
    out = np.stack([np.asarray(res.results[b]["out"]) for b in range(8)], axis=0)
    return out.astype(np.float32)


# version 16
# speedup vs baseline: 1.0753x; 1.0001x over previous
import numpy as np
from contextlib import ExitStack

import concourse.bass as bass
import concourse.mybir as mybir
from concourse.bass_utils import run_bass_kernel_spmd

F32 = mybir.dt.float32
BF16 = mybir.dt.bfloat16
AF = mybir.ActivationFunctionType
ALU = mybir.AluOpType
AX = mybir.AxisListType

ENGINES = ("tensor", "vector", "scalar", "gpsimd", "sync")
NEG = -1.0e30
D = 1024



class Buf:
    __slots__ = ("h", "name", "dep", "is_psum", "eng")

    def __init__(self, h, name, is_psum=False):
        self.h = h
        self.name = name
        self.dep = {}
        self.is_psum = is_psum
        self.eng = {}

    def __getitem__(self, s):
        return self.h[s]

    def ap(self):
        return self.h.ap()


class _Op:
    __slots__ = ("fn", "waits", "incs")

    def __init__(self, fn):
        self.fn = fn
        self.waits = []
        self.incs = []


class Prog:

    def __init__(self, nc, ctx):
        self.nc = nc
        self.ctx = ctx
        self.phase = 0
        self.ops = {e: [] for e in ENGINES}
        self.chan = {}
        self.know = {e: {} for e in ENGINES}
        self.dma_sems = {}
        self.pending = []
        self.nblk = 0

    def _sem(self, name):
        return self.ctx.enter_context(self.nc.semaphore(name))

    def chan_sem(self, pe):
        if pe not in self.chan:
            self.chan[pe] = [self._sem(f"dn_{pe[:2]}"), 0]
        return self.chan[pe]

    def dma_sem(self, name):
        if name not in self.dma_sems:
            self.dma_sems[name] = [self._sem(f"dm_{name}"), 0]
        return self.dma_sems[name]

    def _edges_for(self, reads, writes):
        toks = []
        for (buf, key) in reads:
            for k, st in buf.dep.items():
                if key is None or k is None or k == key:
                    if st[0] is not None:
                        toks.append(st[0])
        for (buf, key) in writes:
            for k, st in buf.dep.items():
                if key is None or k is None or k == key:
                    if st[0] is not None:
                        toks.append(st[0])
                    toks.extend(st[1])
        return toks

    _CLS = {"tensor": "t", "vector": "v", "scalar": "s"}

    def op(self, eng, fn, r=(), w=(), record=True):
        pos = len(self.ops[eng])
        toks = self._edges_for(r, w)
        cls = self._CLS.get(eng)
        mytok_pending = []
        if cls is not None:
            for (buf, key) in tuple(r) + tuple(w):
                if not getattr(buf, "is_psum", False):
                    continue
                for k, st in buf.eng.items():
                    if key is None or k is None or k == key:
                        for c2, tok2 in st.items():
                            if c2 != cls and tok2 is not None:
                                toks.append(tok2)
                mytok_pending.append((buf, key))
        entry = _Op(fn)
        self.ops[eng].append(entry)
        imax = {}
        smax = {}
        for tok in toks:
            if tok[0] == 'i':
                _, pe, ph, p = tok
                if ph < self.phase:
                    continue
                if pe == eng == "tensor":
                    continue
                if imax.get(pe, -1) < p:
                    imax[pe] = p
            elif tok[0] == 'S':
                name = tok[1]
                val = self.dma_sems[name][1]
                if smax.get(name, 0) < val:
                    smax[name] = val
            else:
                _, name, val = tok
                if smax.get(name, 0) < val:
                    smax[name] = val
        for pe, p in imax.items():
            self.pending.append((eng, pos, pe, p))
        for name, val in smax.items():
            if self.know[eng].get(name, 0) < val:
                self.know[eng][name] = val
                entry.waits.append((self.dma_sems[name][0], val))
        tok = ('i', eng, self.phase, pos)
        if cls is not None:
            for (buf, key) in mytok_pending:
                buf.eng.setdefault(key, {})[cls] = tok
        if record:
            for (buf, key) in r:
                buf.dep.setdefault(key, [None, []])[1].append(tok)
            for (buf, key) in w:
                if key is None:
                    buf.dep.clear()
                buf.dep[key] = [tok, []]
        return tok

    def dma(self, qeng, out, in_, sem_name=None, r=(), w=(), transpose=False):
        if sem_name is None:
            assert w, "dma needs sem_name or a write target"
            buf, key = w[0]
            sem_name = f"{buf.name}.{key}"
        sem = self.dma_sem(sem_name)

        if transpose:
            def fn(e, out=out, in_=in_):
                return e.dma_start_transpose(out, in_)
        else:
            def fn(e, out=out, in_=in_):
                return e.dma_start(out=out, in_=in_)

        self.op(qeng, fn, r=r, w=w, record=False)
        self.ops[qeng][-1].incs.append((sem[0], 16))
        sem[1] += 16
        tok_done = ('s', sem_name, sem[1])
        tok_tot = ('S', sem_name)
        for (buf, key) in w:
            if key is None:
                buf.dep.clear()
            buf.dep[key] = [tok_tot, []]
        for (buf, key) in r:
            buf.dep.setdefault(key, [None, []])[1].append(tok_tot)
        return tok_done

    def wait_all_dma(self, eng, sem_name):
        s = self.dma_sems[sem_name]
        h, v = s[0], s[1]
        self.op(eng, lambda e: e.wait_ge(h, v))

    def flush(self, name=None):
        self._last_flush = name
        needed = {}
        for (ce, cpos, pe, p) in self.pending:
            needed.setdefault(pe, set()).add(p)
        tagval = {}
        for pe, ps in needed.items():
            ch = self.chan_sem(pe)
            for pp in sorted(ps):
                ch[1] += 1
                assert not self.ops[pe][pp].incs, (pe, pp)
                self.ops[pe][pp].incs.append((ch[0], 1))
                tagval[(pe, pp)] = ch[1]
        lastwait = {}
        for (ce, cpos, pe, pp) in sorted(self.pending, key=lambda t: (t[0], t[1])):
            v = tagval[(pe, pp)]
            if lastwait.get((pe, ce), 0) >= v:
                continue
            lastwait[(pe, ce)] = v
            self.ops[ce][cpos].waits.append((self.chan_sem(pe)[0], v))

        self.nblk += 1
        with self.nc.Block(name or f"ph{self.nblk}", no_gpsimd_drain=True) as blk:
            for eng in ENGINES:
                entries = self.ops[eng]
                if not entries:
                    continue

                def body(e, entries=entries):
                    for ent in entries:
                        for (s, v) in ent.waits:
                            e.wait_ge(s, v)
                        inst = ent.fn(e)
                        for (s, k) in ent.incs:
                            inst.then_inc(s, k)

                getattr(blk, eng)(body)
        self.ops = {e: [] for e in ENGINES}
        self.pending = []
        self.phase += 1




def _prep_weights(i):
    out = {}
    f32 = np.float32

    def c(a):
        return np.ascontiguousarray(np.asarray(a, dtype=f32))

    def pm_proj(wT):
        a = np.asarray(wT, f32).reshape(8, 128, 4, 256)
        return c(a.transpose(2, 1, 0, 3).reshape(4, 128, 8 * 256))

    out["wqT"] = pm_proj(np.asarray(i["L0_attn_Wq"]).T)
    out["wkT"] = pm_proj(np.asarray(i["L0_attn_Wk"]).T)
    out["wvT"] = pm_proj(np.asarray(i["L0_attn_Wv"]).T)
    out["woPM"] = c(np.asarray(i["L0_attn_Wo"]).T.reshape(8, 128, 1024)
                    .transpose(1, 0, 2).reshape(128, 8192))
    out["qg10"] = c(np.concatenate([np.asarray(i["L0_attn_Wtq"]).T,
                                    np.asarray(i["L0_ffn_gate_w"]).T], axis=1))
    out["bqp10"] = c(np.concatenate([np.asarray(i["L0_attn_btq"]), np.zeros(8, f32)]))[None, :]
    out["bq0"] = c(np.asarray(i["L0_attn_bq"]).reshape(8, 128).T)
    out["bk0"] = c(np.asarray(i["L0_attn_bk"]).reshape(8, 128).T)
    out["bv0_row"] = c(i["L0_attn_bv"])[None, :]
    out["bv0c"] = c(np.asarray(i["L0_attn_bv"]).reshape(8, 128).T)
    out["bo_row"] = c(i["L0_attn_bo"])[None, :]
    out["bfq_row"] = c(i["L0_attn_bfq"])[None, :]
    out["wfqT"] = c(np.asarray(i["L0_attn_Wfq"]).T)
    def pm_df(a, slices):
        e_, d_, f_ = a.shape
        parts = []
        for (f0, fl) in slices:
            b = a[:, :, f0:f0 + fl].reshape(e_, 8, 128, fl)
            parts.append(b.transpose(0, 2, 1, 3).reshape(e_, 128, 8 * fl))
        return c(np.concatenate(parts, axis=2))

    def pm_fd(a):
        e_, f_, o_ = a.shape
        nfc = f_ // 128
        return c(a.reshape(e_, nfc, 128, o_).transpose(0, 2, 1, 3).reshape(e_, 128, nfc * o_))

    wg0a = np.concatenate([np.transpose(np.asarray(i["L0_ffn_wg"]), (0, 2, 1)),
                           np.asarray(i["L0_ffn_sg"]).T[None]], axis=0)
    wu0a = np.concatenate([np.transpose(np.asarray(i["L0_ffn_wu"]), (0, 2, 1)),
                           np.asarray(i["L0_ffn_su"]).T[None]], axis=0)
    wd0a = np.concatenate([np.transpose(np.asarray(i["L0_ffn_wd"]), (0, 2, 1)),
                           np.asarray(i["L0_ffn_sd"]).T[None]], axis=0)
    out["wg0"] = pm_df(wg0a, [(0, 512)])
    out["wu0"] = pm_df(wu0a, [(0, 512)])
    out["wd0"] = pm_fd(wd0a)
    out["wq1T"] = pm_proj(np.asarray(i["L1_attn_Wq"]).reshape(D, D).T)
    out["wk1T"] = pm_proj(np.asarray(i["L1_attn_Wk"]).reshape(D, D).T)
    out["wv1T"] = pm_proj(np.asarray(i["L1_attn_Wv"]).reshape(D, D).T)
    out["bq1"] = c(np.asarray(i["L1_attn_bq"]).reshape(D).reshape(8, 128).T)
    out["bk1"] = c(np.asarray(i["L1_attn_bk"]).reshape(D).reshape(8, 128).T)
    out["bv1_row"] = c(np.asarray(i["L1_attn_bv"]).reshape(D))[None, :]
    out["bv1c"] = c(np.asarray(i["L1_attn_bv"]).reshape(D).reshape(8, 128).T)
    out["proj1"] = c(i["L1_attn_proj"])
    out["gate1T"] = c(np.asarray(i["L1_ffn_gate_w"]).T)
    swg = np.asarray(i["L1_ffn_swg"], f32)
    swu = np.asarray(i["L1_ffn_swu"], f32)
    swd = np.asarray(i["L1_ffn_swd"], f32)
    wg1a = np.concatenate([np.asarray(i["L1_ffn_wg"]),
                           np.stack([swg[:, :1408], swg[:, 1408:]])], axis=0)
    wu1a = np.concatenate([np.asarray(i["L1_ffn_wu"]),
                           np.stack([swu[:, :1408], swu[:, 1408:]])], axis=0)
    wd1a = np.concatenate([np.asarray(i["L1_ffn_wd"]),
                           swd.reshape(2, 1408, 1024)], axis=0)
    L1SL = [(0, 512), (512, 512), (1024, 384)]
    out["wg1"] = pm_df(wg1a, L1SL)
    out["wu1"] = pm_df(wu1a, L1SL)
    out["wd1"] = pm_fd(wd1a)
    out["id_f"] = np.eye(128, dtype=f32)
    out["ones_col"] = np.ones((1, 128), f32)
    out["ones128"] = np.ones((128, 1), f32)
    cb = np.zeros((128, 128), f32)
    cb[np.triu_indices(128, 1)] = NEG
    out["causal"] = cb
    return out


WEIGHT_SHAPES = [
    ("wqT", (4, 128, 2048)), ("wkT", (4, 128, 2048)), ("wvT", (4, 128, 2048)), ("woPM", (128, 8 * D)),
    ("qg10", (D, 10)), ("bqp10", (1, 10)), ("bq0", (128, 8)), ("bk0", (128, 8)),
    ("bv0_row", (1, D)), ("bo_row", (1, D)), ("bfq_row", (1, D)), ("wfqT", (2, D)),
    ("wg0", (9, 128, 4096)), ("wu0", (9, 128, 4096)), ("wd0", (9, 128, 4096)),
    ("wq1T", (4, 128, 2048)), ("wk1T", (4, 128, 2048)), ("wv1T", (4, 128, 2048)),
    ("bq1", (128, 8)), ("bk1", (128, 8)), ("bv1_row", (1, D)),
    ("bv0c", (128, 8)), ("bv1c", (128, 8)),
    ("proj1", (D, D)), ("gate1T", (D, 8)),
    ("wg1", (10, 128, 11264)), ("wu1", (10, 128, 11264)), ("wd1", (10, 128, 11264)),
    ("id_f", (128, 128)), ("ones_col", (1, 128)), ("ones128", (128, 1)),
    ("causal", (128, 128)),
]




def build_program(T=2048, taps=(), stop_after=None):
    NCH = T // 128
    TS = min(512, T)
    NTS = T // TS
    nc = bass.Bass("TRN2", target_bir_lowering=False)

    di = {}
    di["x"] = nc.dram_tensor("x", [T, D], F32, kind="ExternalInput").ap()
    for name, shape in WEIGHT_SHAPES:
        di[name] = nc.dram_tensor(name, list(shape), F32, kind="ExternalInput").ap()
    out_hbm = nc.dram_tensor("out", [T, D], F32, kind="ExternalOutput").ap()
    tap_ap = {}

    def tap(name, shape):
        if name in taps and name not in tap_ap:
            tap_ap[name] = nc.dram_tensor("tap_" + name, list(shape), F32,
                                          kind="ExternalOutput").ap()
        return tap_ap.get(name)

    ctx = ExitStack()
    with ctx:
        p = Prog(nc, ctx)

        SB_TOP = nc.SBUF_PARTITION_SIZE_BYTES

        SB_BASE = (SB_TOP - nc.sbuf_bytes_remaining + 127) & ~63

        class SAlloc:
            def __init__(self):
                self.free_l = [[SB_BASE, SB_TOP, []]]

            def alloc(self, size):
                size = (size + 63) & ~63
                for reg in self.free_l:
                    if reg[1] - reg[0] >= size:
                        off = reg[0]
                        toks = list(reg[2])
                        reg[0] += size
                        if reg[0] >= reg[1]:
                            self.free_l.remove(reg)
                        return off, size, toks
                raise RuntimeError(f"SBUF OOM: need {size}, free={[(r[0], r[1]) for r in self.free_l]}")

            def release(self, off, size, toks):
                self.free_l.append([off, off + size, toks])
                self.free_l.sort(key=lambda r: r[0])
                merged = []
                for reg in self.free_l:
                    if merged and merged[-1][1] == reg[0]:
                        merged[-1][1] = reg[1]
                        merged[-1][2] = merged[-1][2] + reg[2]
                    else:
                        merged.append(reg)
                self.free_l = merged

        salloc = SAlloc()
        _uid = [0]

        class _StopBuild(Exception):
            pass

        _orig_flush = Prog.flush

        def _flush_ck(name=None):
            _orig_flush(p, name)
            if stop_after is not None and name == stop_after:
                raise _StopBuild()
        p.flush = _flush_ck

        class PScope:
            def __init__(self):
                self.bufs = []
                self.es = ExitStack()

            def enter_context(self, cm):
                return self.es.enter_context(cm)

            def register(self, buf, off, size):
                self.bufs.append((buf, off, size))

            def close(self):
                self.es.close()
                for (buf, off, size) in self.bufs:
                    toks = []
                    for st in buf.dep.values():
                        if st[0] is not None:
                            toks.append(st[0])
                        toks.extend(st[1])
                    salloc.release(off, size, toks)
                self.bufs = []

            def __enter__(self):
                return self

            def __exit__(self, *a):
                self.close()
                return False

        perm_scope = PScope()

        def sb(name, part, free, dt, scope=None):
            nbytes = free * (2 if dt == BF16 else 4)
            off, size, toks = salloc.alloc(nbytes)
            _uid[0] += 1
            h = nc.alloc_sbuf_tensor_at(f"s{_uid[0]}_{name}", [part, free], dt, offset=off)
            b = Buf(h, name)
            if toks:
                b.dep[("alias",)] = [None, toks]
            (scope or perm_scope).register(b, off, size)
            return b

        def psum(name, part, free, scope=None, dt=F32):
            _uid[0] += 1
            return Buf((scope or ctx).enter_context(
                nc.psum_tensor(f"p{_uid[0]}_{name}", [part, free], dt)), name,
                is_psum=True)

        x_tm = sb("x_tm", 128, NCH * D, F32)
        qp_tm = sb("qp_tm", 128, NCH * 2, F32)
        g0_tm = sb("g0_tm", 128, NCH * 8, F32)
        cw_tm = sb("cw_tm", 128, NCH * 8, F32)
        id_f = sb("id_f", 128, 128, F32)
        id_b = sb("id_b", 128, 128, BF16)
        ones_col = sb("ones_col", 1, 128, F32)
        ones128 = sb("ones128", 128, 1, F32)
        causal = sb("causal", 128, 128, F32)
        qg10 = sb("qg10", 128, 8 * 10, F32)
        bqp10 = sb("bqp10", 1, 10, F32)
        bcol = sb("bcol", 128, 4 * 8, F32)
        tthr = sb("tthr", 128, 1, F32)
        stat = sb("stat", 1, 32, F32)
        sm_sc = sb("sm_sc", 128, 8, F32)

        p.dma("sync", x_tm.ap().rearrange("p (c o) -> p c o", c=NCH),
              di["x"].rearrange("(c p) o -> p c o", p=128), None,
              w=[(x_tm, c) for c in range(NCH)])
        p.dma("sync", id_f[:], di["id_f"], None, w=[(id_f, None)])
        p.dma("gpsimd", id_b[:], di["id_f"], None, w=[(id_b, None)])
        p.dma("sync", ones_col[:], di["ones_col"], None, w=[(ones_col, None)])
        p.dma("sync", ones128[:], di["ones128"], None, w=[(ones128, None)])
        p.dma("sync", causal[:], di["causal"], None, w=[(causal, None)])
        p.dma("sync", qg10.ap().rearrange("p (c o) -> p c o", c=8),
              di["qg10"].rearrange("(c p) o -> p c o", p=128), None, w=[(qg10, None)])
        p.dma("sync", bqp10[:], di["bqp10"], None, w=[(bqp10, None)])
        p.dma("sync", bcol[:, 0:8], di["bq0"], None, w=[(bcol, 0)])
        p.dma("sync", bcol[:, 8:16], di["bk0"], None, w=[(bcol, 1)])
        p.dma("sync", bcol[:, 16:24], di["bq1"], None, w=[(bcol, 2)])
        p.dma("sync", bcol[:, 24:32], di["bk1"], None, w=[(bcol, 3)])
        try:
            p.flush("load")
        except _StopBuild:
            stop_after = "__stopped__"

        def ln_pass(layer, ph, xn_fm):
            xn_rot = sb(f"xnrot{layer}", 128, 4 * D, F32, ph)
            stg = sb(f"stg{layer}", 128, 4 * D, F32, ph)
            sqs = sb(f"sqs{layer}", 128, 2 * D, F32, ph)
            e8 = sb(f"e8_{layer}", 128, 4 * 8, F32, ph)
            lsc = sb(f"lsc{layer}", 128, 32, F32, ph)
            gsc = sb(f"gsc{layer}", 128, 32, F32, ph)
            xt_ps = psum(f"xtp{layer}", 128, 2 * D, ph)
            qpg_ps = psum(f"qgp{layer}", 128, 1024, ph)
            nqg = 10 if layer == 0 else 8
            if layer == 1:
                p.dma("sync", qg10.ap().rearrange("p (c o) -> p c o", c=8)[:, :, 0:8],
                      di["gate1T"].rearrange("(c p) o -> p c o", p=128), None,
                      w=[(qg10, None)])
            for c in range(NCH):
                par = c % 4
                p2 = c % 2
                L = lambda k, par=par: (lsc, (par, k))
                Ls = lambda k, par=par: lsc[:, par * 8 + k:par * 8 + k + 1]
                xsl = x_tm[:, c * D:(c + 1) * D]
                xnsl = xn_rot[:, par * D:(par + 1) * D]
                stgs = stg[:, par * D:(par + 1) * D]
                p.op("vector", lambda e, xsl=xsl, a=Ls(0), c=c: e.reduce_sum(a, xsl, axis=AX.X, negate=True),
                     r=[(x_tm, c)], w=[L(0)])
                p.op("scalar", lambda e, xsl=xsl, b=Ls(2), c=c, p2=p2: e.activation(
                        sqs[:, p2 * D:(p2 + 1) * D], xsl, AF.Square, accum_out=b),
                     r=[(x_tm, c)], w=[(sqs, p2), L(2)])
                p.op("vector", lambda e, a=Ls(0): e.tensor_scalar(a, a, 1.0 / D, None, ALU.mult),
                     r=[L(0)], w=[L(0)])
                p.op("vector", lambda e, a=Ls(3), b=Ls(0): e.tensor_tensor(a, b, b, ALU.mult),
                     r=[L(0)], w=[L(3)])
                p.op("vector", lambda e, a=Ls(4), b=Ls(2), m2=Ls(3): e.scalar_tensor_tensor(
                        a, b, 1.0 / D, m2, ALU.mult, ALU.subtract),
                     r=[L(2), L(3)], w=[L(4)])
                p.op("scalar", lambda e, a=Ls(5), b=Ls(4): e.activation(a, b, AF.Sqrt, bias=np.float32(1e-5)),
                     r=[L(4)], w=[L(5)])
                p.op("vector", lambda e, a=Ls(6), b=Ls(5): e.reciprocal(a, b),
                     r=[L(5)], w=[L(6)])
                p.op("vector", lambda e, xsl=xsl, xnsl=xnsl, m=Ls(0), rs=Ls(6), c=c: e.tensor_scalar(
                        xnsl, xsl, m, rs, ALU.add, ALU.mult),
                     r=[(x_tm, c), L(0), L(6)], w=[(xn_rot, par)])
                for dc in range(8):
                    p.op("tensor", lambda e, xnsl=xnsl, dc=dc, p2=p2: e.transpose(
                            xt_ps[:, p2 * D + dc * 128:p2 * D + (dc + 1) * 128],
                            xnsl[:, dc * 128:(dc + 1) * 128], id_f[:]),
                         r=[(xn_rot, par), (id_f, None)], w=[(xt_ps, (p2, dc))])
                rd_xt = [(xt_ps, (p2, dc)) for dc in range(8)]
                p.op("vector", lambda e, c=c, p2=p2: e.tensor_copy(
                        xn_fm.ap().rearrange("p (d t) -> p d t", d=8)[:, :, c * 128:(c + 1) * 128],
                        xt_ps.ap().rearrange("p (b d t) -> p b d t", b=2, d=8)[:, p2]),
                     r=rd_xt, w=[(xn_fm, c)])
                p.op("scalar", lambda e, stgs=stgs, p2=p2: e.copy(
                        stgs, xt_ps[:, p2 * D:(p2 + 1) * D]),
                     r=rd_xt, w=[(stg, par)])
                if layer == 0:
                    p.op("vector", lambda e, xsl=xsl, xnsl=xnsl, c=c, par=par: e.tensor_tensor(
                            xsl, xsl, xnsl, ALU.add),
                         r=[(x_tm, c), (xn_rot, par)], w=[(x_tm, c)])
                for dc in range(8):
                    p.op("tensor", lambda e, dc=dc, stgs=stgs, p2=p2: e.matmul(
                            qpg_ps[:, p2 * 512:p2 * 512 + nqg],
                            stgs[:, dc * 128:(dc + 1) * 128],
                            qg10[:, dc * 10:dc * 10 + nqg],
                            start=(dc == 0), stop=(layer == 1 and dc == 7)),
                         r=[(stg, par), (qg10, None)], w=[(qpg_ps, p2)])
                if layer == 0:
                    p.op("tensor", lambda e, p2=p2: e.matmul(
                            qpg_ps[:, p2 * 512:p2 * 512 + nqg], ones_col[:],
                            bqp10[0:1, 0:nqg], start=False, stop=True),
                         r=[(ones_col, None), (bqp10, None)], w=[(qpg_ps, p2)])
                    p.op("vector", lambda e, c=c, p2=p2: e.tensor_copy(
                            qp_tm[:, c * 2:(c + 1) * 2], qpg_ps[:, p2 * 512:p2 * 512 + 2]),
                         r=[(qpg_ps, p2)], w=[(qp_tm, c)])
                    p.op("vector", lambda e, c=c, p2=p2: e.tensor_copy(
                            g0_tm[:, c * 8:(c + 1) * 8], qpg_ps[:, p2 * 512 + 2:p2 * 512 + 10]),
                         r=[(qpg_ps, p2)], w=[(g0_tm, c)])
                else:
                    p.op("vector", lambda e, c=c, p2=p2: e.tensor_copy(
                            g0_tm[:, c * 8:(c + 1) * 8], qpg_ps[:, p2 * 512:p2 * 512 + 8]),
                         r=[(qpg_ps, p2)], w=[(g0_tm, c)])
                G = lambda k, par=par: (gsc, (par, k))
                Gs = lambda k, par=par: gsc[:, par * 8 + k:par * 8 + k + 1]
                gsl = g0_tm[:, c * 8:(c + 1) * 8]
                esl = e8[:, par * 8:(par + 1) * 8]
                csl = cw_tm[:, c * 8:(c + 1) * 8]
                p.op("vector", lambda e, gsl=gsl, a=Gs(0), c=c: e.reduce_max(a, gsl, axis=AX.X, negate=True),
                     r=[(g0_tm, c)], w=[G(0)])
                p.op("scalar", lambda e, gsl=gsl, esl=esl, a=Gs(0), l=Gs(1), c=c, par=par: e.activation(
                        esl, gsl, AF.Exp, bias=a, accum_out=l),
                     r=[(g0_tm, c), G(0)], w=[(e8, par), G(1)])
                p.op("vector", lambda e, a=Gs(2), l=Gs(1): e.reciprocal(a, l),
                     r=[G(1)], w=[G(2)])
                p.op("vector", lambda e, esl=esl, a=Gs(2), par=par: e.tensor_scalar(esl, esl, a, None, ALU.mult),
                     r=[(e8, par), G(2)], w=[(e8, par)])
                p.op("vector", lambda e, esl=esl, a=Gs(3), par=par: e.reduce_max(a, esl, axis=AX.X),
                     r=[(e8, par)], w=[G(3)])
                p.op("vector", lambda e, esl=esl, csl=csl, m1=Gs(3), c=c, par=par: e.tensor_scalar(
                        csl, esl, m1, m1, ALU.is_equal, ALU.mult),
                     r=[(e8, par), G(3)], w=[(cw_tm, c)])
                p.op("vector", lambda e, esl=esl, csl=csl, c=c, par=par: e.tensor_tensor(
                        esl, esl, csl, ALU.subtract),
                     r=[(e8, par), (cw_tm, c)], w=[(e8, par)])
                p.op("vector", lambda e, esl=esl, a=Gs(4), par=par: e.reduce_max(a, esl, axis=AX.X),
                     r=[(e8, par)], w=[G(4)])
                p.op("vector", lambda e, esl=esl, m2=Gs(4), par=par: e.tensor_scalar(
                        esl, esl, m2, m2, ALU.is_equal, ALU.mult),
                     r=[(e8, par), G(4)], w=[(e8, par)])
                p.op("vector", lambda e, esl=esl, csl=csl, c=c, par=par: e.tensor_tensor(
                        csl, csl, esl, ALU.add),
                     r=[(e8, par), (cw_tm, c)], w=[(cw_tm, c)])
                if layer == 1:
                    p.op("vector", lambda e, a=Gs(5), m1=Gs(3), m2=Gs(4): e.tensor_tensor(
                            a, m1, m2, ALU.add),
                         r=[G(3), G(4)], w=[G(5)])
                    p.op("vector", lambda e, a=Gs(5): e.tensor_scalar(a, a, 1e-20, None, ALU.add),
                         r=[G(5)], w=[G(5)])
                    p.op("vector", lambda e, a=Gs(6), b=Gs(5): e.reciprocal(a, b),
                         r=[G(5)], w=[G(6)])
                    p.op("vector", lambda e, csl=csl, a=Gs(6), c=c: e.tensor_scalar(
                            csl, csl, a, None, ALU.mult),
                         r=[(cw_tm, c), G(6)], w=[(cw_tm, c)])

        def stats_pass(ph, tabi, tabj, r01f):
            d9 = sb("d9", 128, NCH * 9, F32, ph)
            rtm = sb("rtm", 128, 4, F32, ph)
            s12 = sb("s12", 128, 12, F32, ph)
            st_ps = psum("st_ps", 1, 16, ph)
            tp_ps = psum("tp_ps", 4, 1024, ph)
            th_ps = psum("th_ps", 128, 8, ph)
            N2 = float(T) * float(T)
            for c in range(NCH):
                xs = qp_tm[:, c * 2:c * 2 + 1]
                ys = qp_tm[:, c * 2 + 1:c * 2 + 2]
                ds = lambda k, c=c: d9[:, c * 9 + k:c * 9 + k + 1]
                K = lambda k, c=c: (d9, (c, k))
                p.op("vector", lambda e, a=ds(4), xs=xs, c=c: e.tensor_tensor(a, xs, xs, ALU.mult),
                     r=[(qp_tm, c)], w=[K(4)])
                p.op("vector", lambda e, a=ds(6), ys=ys, c=c: e.tensor_tensor(a, ys, ys, ALU.mult),
                     r=[(qp_tm, c)], w=[K(6)])
                p.op("vector", lambda e, a=ds(0), b=ds(4), cc=ds(6), c=c: e.tensor_tensor(a, b, cc, ALU.add),
                     r=[K(4), K(6)], w=[K(0)])
                p.op("vector", lambda e, a=ds(1), b=ds(0), c=c: e.tensor_tensor(a, b, b, ALU.mult),
                     r=[K(0)], w=[K(1)])
                p.op("vector", lambda e, a=ds(2), xs=xs, c=c: e.tensor_copy(a, xs),
                     r=[(qp_tm, c)], w=[K(2)])
                p.op("vector", lambda e, a=ds(3), ys=ys, c=c: e.tensor_copy(a, ys),
                     r=[(qp_tm, c)], w=[K(3)])
                p.op("vector", lambda e, a=ds(5), xs=xs, ys=ys, c=c: e.tensor_tensor(a, xs, ys, ALU.mult),
                     r=[(qp_tm, c)], w=[K(5)])
                p.op("vector", lambda e, a=ds(7), b=ds(0), xs=xs, c=c: e.tensor_tensor(a, b, xs, ALU.mult),
                     r=[K(0), (qp_tm, c)], w=[K(7)])
                p.op("vector", lambda e, a=ds(8), b=ds(0), ys=ys, c=c: e.tensor_tensor(a, b, ys, ALU.mult),
                     r=[K(0), (qp_tm, c)], w=[K(8)])
            for c in range(NCH):
                p.op("tensor", lambda e, c=c: e.matmul(
                        st_ps[0:1, 0:9], ones128[:], d9[:, c * 9:(c + 1) * 9],
                        start=(c == 0), stop=(c == NCH - 1)),
                     r=[(ones128, None)] + [(d9, (c, k)) for k in range(9)], w=[(st_ps, 0)])
            p.op("vector", lambda e: e.tensor_copy(stat[0:1, 0:9], st_ps[0:1, 0:9]),
                 r=[(st_ps, 0)], w=[(stat, 0)])
            S = lambda k: stat[0:1, k:k + 1]

            def vop(fn, rks, wk):
                p.op("vector", fn, r=[(stat, 0)] + [(stat, k) for k in rks], w=[(stat, wk)])
            vop(lambda e: e.tensor_tensor(S(9), S(2), S(2), ALU.mult), [], 9)
            vop(lambda e: e.tensor_tensor(S(10), S(3), S(3), ALU.mult), [], 10)
            vop(lambda e: e.tensor_tensor(S(9), S(9), S(10), ALU.add), [9, 10], 9)
            vop(lambda e: e.tensor_scalar(S(11), S(9), -2.0, None, ALU.mult), [9], 11)
            vop(lambda e: e.scalar_tensor_tensor(S(11), S(0), 2.0 * T, S(11), ALU.mult, ALU.add), [11], 11)
            vop(lambda e: e.tensor_tensor(S(9), S(4), S(4), ALU.mult), [], 9)
            vop(lambda e: e.tensor_tensor(S(10), S(5), S(5), ALU.mult), [], 10)
            vop(lambda e: e.scalar_tensor_tensor(S(9), S(10), 2.0, S(9), ALU.mult, ALU.add), [9, 10], 9)
            vop(lambda e: e.tensor_tensor(S(10), S(6), S(6), ALU.mult), [], 10)
            vop(lambda e: e.tensor_tensor(S(9), S(9), S(10), ALU.add), [9, 10], 9)
            vop(lambda e: e.tensor_tensor(S(10), S(7), S(2), ALU.mult), [], 10)
            vop(lambda e: e.tensor_tensor(S(12), S(8), S(3), ALU.mult), [], 12)
            vop(lambda e: e.tensor_tensor(S(10), S(10), S(12), ALU.add), [10, 12], 10)
            vop(lambda e: e.tensor_tensor(S(12), S(0), S(0), ALU.mult), [], 12)
            vop(lambda e: e.scalar_tensor_tensor(S(12), S(1), float(T), S(12), ALU.mult, ALU.add), [12], 12)
            vop(lambda e: e.tensor_scalar(S(12), S(12), 2.0, None, ALU.mult), [12], 12)
            vop(lambda e: e.scalar_tensor_tensor(S(12), S(9), 4.0, S(12), ALU.mult, ALU.add), [9, 12], 12)
            vop(lambda e: e.scalar_tensor_tensor(S(12), S(10), -8.0, S(12), ALU.mult, ALU.add), [10, 12], 12)
            vop(lambda e: e.tensor_scalar(S(13), S(11), 1.0 / N2, None, ALU.mult), [11], 13)
            vop(lambda e: e.tensor_scalar(S(14), S(12), 1.0 / N2, None, ALU.mult), [12], 14)
            vop(lambda e: e.tensor_tensor(S(12), S(13), S(13), ALU.mult), [13], 12)
            vop(lambda e: e.tensor_tensor(S(14), S(14), S(12), ALU.subtract), [12, 14], 14)
            vop(lambda e: e.tensor_scalar(S(14), S(14), N2 / (N2 - 1.0), None, ALU.mult), [14], 14)
            p.op("scalar", lambda e: e.activation(S(15), S(14), AF.Sqrt),
                 r=[(stat, 14)], w=[(stat, 15)])
            vop(lambda e: e.scalar_tensor_tensor(S(15), S(15), 1.25, S(13), ALU.mult, ALU.add), [13, 15], 15)
            p.op("tensor", lambda e: e.matmul(th_ps[:, 0:1], ones_col[:], S(15), start=True, stop=True),
                 r=[(ones_col, None), (stat, 15)], w=[(th_ps, 0)])
            p.op("vector", lambda e: e.tensor_copy(tthr[:], th_ps[:, 0:1]),
                 r=[(th_ps, 0)], w=[(tthr, None)])
            for c in range(NCH):
                sl = c % 2
                xs = qp_tm[:, c * 2:c * 2 + 1]
                ys = qp_tm[:, c * 2 + 1:c * 2 + 2]
                asl = d9[:, c * 9:c * 9 + 1]
                p.op("vector", lambda e, xs=xs, c=c: e.tensor_copy(s12[:, 0:1], xs),
                     r=[(qp_tm, c)], w=[(s12, 0)])
                p.op("vector", lambda e, ys=ys, c=c: e.tensor_copy(s12[:, 1:2], ys),
                     r=[(qp_tm, c)], w=[(s12, 1)])
                p.op("vector", lambda e: e.memset(s12[:, 2:3], 1.0), w=[(s12, 2)])
                p.op("vector", lambda e: e.memset(s12[:, 3:5], 0.0), w=[(s12, 3), (s12, 4)])
                p.op("vector", lambda e: e.memset(s12[:, 4:5], 1.0), w=[(s12, 4)])
                p.op("vector", lambda e: e.memset(s12[:, 5:6], 0.0), w=[(s12, 5)])
                p.op("vector", lambda e, xs=xs, c=c: e.tensor_scalar(s12[:, 6:7], xs, -1.0, None, ALU.mult),
                     r=[(qp_tm, c)], w=[(s12, 6)])
                p.op("vector", lambda e: e.memset(s12[:, 7:9], 0.0), w=[(s12, 7), (s12, 8)])
                p.op("vector", lambda e: e.memset(s12[:, 9:10], 1.0), w=[(s12, 9)])
                p.op("vector", lambda e, ys=ys, c=c: e.tensor_scalar(s12[:, 10:11], ys, -1.0, None, ALU.mult),
                     r=[(qp_tm, c)], w=[(s12, 10)])
                p.op("vector", lambda e: e.memset(s12[:, 11:12], 0.0), w=[(s12, 11)])
                rall = [(s12, k) for k in range(12)]
                for (dst, off, c0) in ((tabi, 0, 0), (tabj, 0, 4), (tabj, T, 8)):
                    p.op("tensor", lambda e, c0=c0, sl=sl: e.transpose(
                            tp_ps[0:4, sl * 512:sl * 512 + 128], s12[:, c0:c0 + 4], id_f[:]),
                         r=rall + [(id_f, None)], w=[(tp_ps, sl)])
                    p.op("vector", lambda e, dst=dst, off=off, c=c, sl=sl: e.tensor_copy(
                            dst[0:4, off + c * 128:off + (c + 1) * 128],
                            tp_ps[0:4, sl * 512:sl * 512 + 128]),
                         r=[(tp_ps, sl)], w=[(dst, (off, c))])
                p.op("scalar", lambda e, asl=asl, a=rtm[:, 0:1], c=c: e.activation(a, asl, AF.Sqrt),
                     r=[(d9, (c, 0))], w=[(rtm, 0)])
                p.op("vector", lambda e, a=rtm[:, 0:1]: e.tensor_scalar(a, a, 1e-12, None, ALU.max),
                     r=[(rtm, 0)], w=[(rtm, 0)])
                p.op("vector", lambda e, a=rtm[:, 1:2], b=rtm[:, 0:1]: e.reciprocal(a, b),
                     r=[(rtm, 0)], w=[(rtm, 1)])
                p.op("vector", lambda e, a=rtm[:, 1:2]: e.tensor_scalar(a, a, 0.1, None, ALU.mult),
                     r=[(rtm, 1)], w=[(rtm, 1)])
                p.op("vector", lambda e, c=c, a=rtm[:, 2:4], sc=rtm[:, 1:2]: e.tensor_scalar(
                        a, qp_tm[:, c * 2:(c + 1) * 2], sc, None, ALU.mult),
                     r=[(qp_tm, c), (rtm, 1)], w=[(rtm, 2)])
                p.op("tensor", lambda e, c=c, sl=sl: e.transpose(
                        tp_ps[0:2, sl * 512:sl * 512 + 128], rtm[:, 2:4], id_f[:]),
                     r=[(rtm, 2), (id_f, None)], w=[(tp_ps, sl)])
                p.op("vector", lambda e, c=c, sl=sl: e.tensor_copy(
                        r01f[0:2, c * 128:(c + 1) * 128], tp_ps[0:2, sl * 512:sl * 512 + 128]),
                     r=[(tp_ps, sl)], w=[(r01f, c)])

        def quantum_pass(ph, r01f):
            wfq_sb = sb("wfq_sb", 2, D, F32, ph)
            b01 = sb("b01", 1, 2 * D, F32, ph)
            qps = psum("qps", 128, 2 * 1024, ph)
            p.dma("sync", wfq_sb[:], di["wfqT"], None, w=[(wfq_sb, None)])
            p.dma("sync", b01[0:1, 0:D], di["bo_row"], None, w=[(b01, 0)])
            p.dma("sync", b01[0:1, D:2 * D], di["bfq_row"], None, w=[(b01, 1)])
            p.op("vector", lambda e: e.scalar_tensor_tensor(
                    b01[0:1, 0:D], b01[0:1, D:2 * D], 0.1, b01[0:1, 0:D], ALU.mult, ALU.add),
                 r=[(b01, 0), (b01, 1)], w=[(b01, 0)])
            for c in range(NCH):
                sl = c % 2
                for half in range(2):
                    p.op("tensor", lambda e, c=c, sl=sl, half=half: e.matmul(
                            qps[:, sl * 1024 + half * 512:sl * 1024 + (half + 1) * 512],
                            r01f[0:2, c * 128:(c + 1) * 128],
                            wfq_sb[:, half * 512:(half + 1) * 512],
                            start=True, stop=False),
                         r=[(r01f, c), (wfq_sb, None)], w=[(qps, (sl, half))])
                    p.op("tensor", lambda e, sl=sl, half=half: e.matmul(
                            qps[:, sl * 1024 + half * 512:sl * 1024 + (half + 1) * 512],
                            ones_col[:], b01[0:1, half * 512:(half + 1) * 512],
                            start=False, stop=True),
                         r=[(ones_col, None), (b01, 0)], w=[(qps, (sl, half))])
                p.op("vector", lambda e, c=c, sl=sl: e.tensor_tensor(
                        x_tm[:, c * D:(c + 1) * D], x_tm[:, c * D:(c + 1) * D],
                        qps[:, sl * 1024:(sl + 1) * 1024], ALU.add),
                     r=[(x_tm, c), (qps, (sl, 0)), (qps, (sl, 1))], w=[(x_tm, c)])

        def qkv_pass(layer, ph, q_fm, k_fm, v_tm, xn_fm):
            wq = di["wqT"] if layer == 0 else di["wq1T"]
            wk = di["wkT"] if layer == 0 else di["wk1T"]
            wv = di["wvT"] if layer == 0 else di["wv1T"]
            bq_k = 0 if layer == 0 else 2
            bk_k = 1 if layer == 0 else 3
            wring = sb(f"wring{layer}", 128, 2 * 2048, BF16, ph)
            bvrow_ = sb(f"bvrow{layer}", 1, D, F32, ph)
            pq = psum(f"pq{layer}", 128, 2048, ph)
            p.dma("sync", bvrow_[:], di["bv0_row" if layer == 0 else "bv1_row"], None,
                  w=[(bvrow_, None)])
            for (mode, dst, wsrc, bk) in (("q", q_fm, wq, bq_k), ("k", k_fm, wk, bk_k),
                                          ("v", v_tm, wv, None)):
                for grp in range(4):
                    slot = grp % 2
                    p.dma("gpsimd", wring[:, slot * 2048:(slot + 1) * 2048],
                          wsrc[grp], None, w=[(wring, slot)])
                    for oc_l in range(2):
                        oc = grp * 2 + oc_l
                        if mode != "v":
                            for dc in range(8):
                                for ts in range(NTS):
                                    p.op("tensor", lambda e, dc=dc, ts=ts, slot=slot, oc_l=oc_l: e.matmul(
                                            pq[:, (ts % 4) * 512:(ts % 4) * 512 + TS],
                                            wring[:, slot * 2048 + dc * 256 + oc_l * 128:
                                                  slot * 2048 + dc * 256 + oc_l * 128 + 128],
                                            xn_fm[:, dc * T + ts * TS:dc * T + (ts + 1) * TS],
                                            start=(dc == 0), stop=(dc == 7)),
                                         r=[(wring, slot)] + [(xn_fm, (ts * TS + j * 128) // 128)
                                                              for j in range(TS // 128)],
                                         w=[(pq, ts % 4)])
                            qpre = (0.125 if layer == 1 else 0.03125) if mode == "q" else None
                            for ts in range(NTS):
                                if qpre is not None:
                                    fn = (lambda e, dst=dst, oc=oc, ts=ts, bk=bk, qpre=qpre: e.tensor_scalar(
                                            dst[:, oc * T + ts * TS:oc * T + (ts + 1) * TS],
                                            pq[:, (ts % 4) * 512:(ts % 4) * 512 + TS],
                                            bcol[:, bk * 8 + oc:bk * 8 + oc + 1],
                                            qpre, ALU.add, ALU.mult))
                                else:
                                    fn = (lambda e, dst=dst, oc=oc, ts=ts, bk=bk: e.tensor_scalar(
                                            dst[:, oc * T + ts * TS:oc * T + (ts + 1) * TS],
                                            pq[:, (ts % 4) * 512:(ts % 4) * 512 + TS],
                                            bcol[:, bk * 8 + oc:bk * 8 + oc + 1], None, ALU.add))
                                p.op("vector", fn,
                                     r=[(pq, ts % 4), (bcol, bk)], w=[(dst, (oc, ts))])
                    if mode == "v":
                        for c in range(NCH):
                            pb = c % 4
                            for dc in range(8):
                                p.op("tensor", lambda e, dc=dc, c=c, slot=slot, pb=pb: e.matmul(
                                        pq[:, pb * 512:pb * 512 + 256],
                                        xn_fm[:, dc * T + c * 128:dc * T + (c + 1) * 128],
                                        wring[:, slot * 2048 + dc * 256:slot * 2048 + (dc + 1) * 256],
                                        start=(dc == 0), stop=False),
                                     r=[(xn_fm, c), (wring, slot)], w=[(pq, pb)])
                            p.op("tensor", lambda e, c=c, grp=grp, pb=pb: e.matmul(
                                    pq[:, pb * 512:pb * 512 + 256],
                                    ones_col[:], bvrow_[0:1, grp * 256:(grp + 1) * 256],
                                    start=False, stop=True),
                                 r=[(ones_col, None), (bvrow_, None)], w=[(pq, pb)])
                            p.op("vector", lambda e, c=c, grp=grp, pb=pb: e.tensor_copy(
                                    v_tm[:, c * D + grp * 256:c * D + (grp + 1) * 256],
                                    pq[:, pb * 512:pb * 512 + 256]),
                                 r=[(pq, pb)],
                                 w=[(v_tm, (c, grp * 2)), (v_tm, (c, grp * 2 + 1))])

        def attn0_pass(ph, q_fm, k_fm, v_tm, tabi, tabj):
            wo_sb = sb("wo_sb", 128, 8 * D, BF16, ph)
            p.dma("gpsimd", wo_sb[:], di["woPM"], None, w=[(wo_sb, None)])
            pbuf = sb("pbuf", 128, T, BF16, ph)
            ptb = sb("ptb", 128, T, BF16, ph)
            ctm = sb("ctm", 128, D, BF16, ph)
            cfm = sb("cfm", 128, 8 * 128, BF16, ph)
            ssc = sb("ssc", 128, 2 * 512, F32, ph)
            psA = psum("psA", 128, 2048, ph)
            ddx = psum("ddx", 128, 512, ph)
            psC = psum("psC", 128, 1024, ph)
            ptp = psum("ptp", 128, 1024, ph, dt=BF16)

            def sc_banks(qt):
                ntile = ((qt + 1) * 128 + 511) // 512
                if ntile <= 2:
                    return [(qt % 2) * 2 + k for k in range(ntile)]
                return list(range(ntile))

            def sc_base(qt):
                ntile = ((qt + 1) * 128 + 511) // 512
                return 1024 if (ntile <= 2 and qt % 2 == 1) else 0

            def stA_mm(qt):
                Lq = (qt + 1) * 128
                bks = sc_banks(qt)
                base = sc_base(qt)
                for kjt in range(len(bks)):
                    j0 = kjt * 512
                    w512 = min(512, Lq - j0)
                    scp = psA[:, base + j0:base + j0 + w512]
                    for dc in range(8):
                        p.op("tensor", lambda e, dc=dc, qt=qt, j0=j0, w512=w512, scp=scp: e.matmul(
                                scp, q_fm[:, dc * T + qt * 128:dc * T + (qt + 1) * 128],
                                k_fm[:, dc * T + j0:dc * T + j0 + w512],
                                start=(dc == 0), stop=(dc == 7)),
                             r=[(q_fm, (dc, qt * 128 // TS))] +
                               [(k_fm, (dc, (j0 + jj * 128) // TS)) for jj in range(w512 // 128)],
                             w=[(psA, bks[kjt])])

            def stA_chain(qt):
                Lq = (qt + 1) * 128
                bks = sc_banks(qt)
                base = sc_base(qt)
                for kjt in range(len(bks)):
                    j0 = kjt * 512
                    w512 = min(512, Lq - j0)
                    scp = psA[:, base + j0:base + j0 + w512]
                    bk = bks[kjt]
                    dp = ddx[:, 0:w512]
                    tp2 = 0
                    sA = ssc[:, 0:w512]
                    sB = ssc[:, 512:512 + w512]
                    p.op("tensor", lambda e, qt=qt, j0=j0, w512=w512, dp=dp: e.matmul(
                            dp, tabi[0:4, qt * 128:(qt + 1) * 128],
                            tabj[0:4, T + j0:T + j0 + w512], start=True, stop=True),
                         r=[(tabi, (0, qt))] + [(tabj, (T, j0 // 128 + jj)) for jj in range(w512 // 128)],
                         w=[(ddx, 0)])
                    p.op("scalar", lambda e, sB=sB, dp=dp: e.activation(sB, dp, AF.Square),
                         r=[(ddx, 0)], w=[(ssc, (tp2, 1))])
                    p.op("tensor", lambda e, qt=qt, j0=j0, w512=w512, dp=dp: e.matmul(
                            dp, tabi[0:4, qt * 128:(qt + 1) * 128],
                            tabj[0:4, j0:j0 + w512], start=True, stop=True),
                         r=[(tabi, (0, qt))] + [(tabj, (0, j0 // 128 + jj)) for jj in range(w512 // 128)],
                         w=[(ddx, 0)])
                    p.op("scalar", lambda e, sA=sA, dp=dp: e.activation(sA, dp, AF.Square),
                         r=[(ddx, 0)], w=[(ssc, (tp2, 0))])
                    p.op("vector", lambda e, sA=sA, sB=sB: e.tensor_tensor(sB, sB, sA, ALU.add),
                         r=[(ssc, (tp2, 0)), (ssc, (tp2, 1))], w=[(ssc, (tp2, 1))])
                    p.op("vector", lambda e, sA=sA, sB=sB: e.tensor_scalar(
                            sA, sB, tthr[:], NEG, ALU.is_gt, ALU.mult),
                         r=[(ssc, (tp2, 1)), (tthr, None)], w=[(ssc, (tp2, 0))])
                    p.op("vector", lambda e, scp=scp, sA=sA, bk=bk: e.tensor_tensor(
                            scp, scp, sA, ALU.add),
                         r=[(psA, bk), (ssc, (tp2, 0))], w=[(psA, bk)])
                    p.op("scalar", lambda e, sB=sB: e.activation(sB, sB, AF.Sqrt, bias=np.float32(1e-30)),
                         r=[(ssc, (tp2, 1))], w=[(ssc, (tp2, 1))])
                    p.op("vector", lambda e, sB=sB, dp=dp: e.tensor_tensor(sB, sB, dp, ALU.add),
                         r=[(ssc, (tp2, 1)), (ddx, 0)], w=[(ssc, (tp2, 1))])
                    p.op("vector", lambda e, sB=sB: e.tensor_scalar(
                            sB, sB, 0.0, NEG, ALU.is_le, ALU.mult),
                         r=[(ssc, (tp2, 1))], w=[(ssc, (tp2, 1))])
                    p.op("vector", lambda e, scp=scp, sB=sB, bk=bk: e.tensor_tensor(
                            scp, scp, sB, ALU.add),
                         r=[(psA, bk), (ssc, (tp2, 1))], w=[(psA, bk)])
                dbk = bks[-1]
                dcol = base + qt * 128
                p.op("vector", lambda e, dcol=dcol, dbk=dbk: e.tensor_tensor(
                        psA[:, dcol:dcol + 128], psA[:, dcol:dcol + 128],
                        causal[:], ALU.add),
                     r=[(psA, dbk), (causal, None)], w=[(psA, dbk)])

            def stB(qt):
                Lq = (qt + 1) * 128
                bks = sc_banks(qt)
                base = sc_base(qt)
                qb = qt % 2
                rmask = [(psA, b) for b in bks]
                p.op("vector", lambda e, Lq=Lq, base=base, qb=qb: e.reduce_max(
                        sm_sc[:, qb * 4:qb * 4 + 1], psA[:, base:base + Lq],
                        axis=AX.X, negate=True),
                     r=rmask, w=[(sm_sc, (qb, 0))])
                p.op("scalar", lambda e, Lq=Lq, base=base, qb=qb, qt=qt: e.activation(
                        pbuf[:, 0:Lq], psA[:, base:base + Lq], AF.Exp,
                        bias=sm_sc[:, qb * 4:qb * 4 + 1],
                        scale=np.float32(1.0), accum_out=sm_sc[:, qb * 4 + 2:qb * 4 + 3]),
                     r=rmask + [(sm_sc, (qb, 0))],
                     w=[(pbuf, kc) for kc in range(qt + 1)] + [(sm_sc, (qb, 2))])
                p.op("vector", lambda e, qb=qb: e.reciprocal(
                        sm_sc[:, qb * 4 + 3:qb * 4 + 4], sm_sc[:, qb * 4 + 2:qb * 4 + 3]),
                     r=[(sm_sc, (qb, 2))], w=[(sm_sc, (qb, 3))])

            def stC(qt):
                qb = qt % 2
                seg = []
                for kc in range(qt + 1):
                    slot = kc % 8
                    p.op("tensor", lambda e, kc=kc, slot=slot: e.transpose(
                            ptp[:, slot * 128:(slot + 1) * 128],
                            pbuf[:, kc * 128:(kc + 1) * 128], id_b[:]),
                         r=[(pbuf, kc), (id_b, None)], w=[(ptp, 0)])
                    seg.append(kc)
                    if len(seg) == 8 or kc == qt:
                        k0 = seg[0]
                        p.op("scalar", lambda e, k0=k0, kc=kc: e.copy(
                                ptb[:, k0 * 128:(kc + 1) * 128],
                                ptp[:, 0:(kc + 1 - k0) * 128]),
                             r=[(ptp, 0)], w=[(ptb, k) for k in seg])
                        seg = []
                for kc in range(qt + 1):
                    for half in range(2):
                        p.op("tensor", lambda e, kc=kc, half=half, qt=qt: e.matmul(
                                psC[:, half * 512:(half + 1) * 512],
                                ptb[:, kc * 128:(kc + 1) * 128],
                                v_tm[:, kc * D + half * 512:kc * D + (half + 1) * 512],
                                start=(kc == 0), stop=(kc == qt)),
                             r=[(ptb, kc)] + [(v_tm, (kc, oo)) for oo in range(8)],
                             w=[(psC, half)])
                p.op("vector", lambda e, qb=qb: e.tensor_scalar(
                        ctm[:], psC[:, 0:1024],
                        sm_sc[:, qb * 4 + 3:qb * 4 + 4], None, ALU.mult),
                     r=[(psC, 0), (psC, 1), (sm_sc, (qb, 3))], w=[(ctm, None)])

            def stD(qt):
                qb = qt % 2
                for dc in range(8):
                    p.op("tensor", lambda e, dc=dc: e.transpose(
                            ptp[:, dc * 128:(dc + 1) * 128],
                            ctm[:, dc * 128:(dc + 1) * 128], id_b[:]),
                         r=[(ctm, None), (id_b, None)], w=[(ptp, 0)])
                p.op("scalar", lambda e: e.copy(cfm[:], ptp[:, 0:1024]),
                     r=[(ptp, 0)], w=[(cfm, dc) for dc in range(8)])
                for dc in range(8):
                    for half in range(2):
                        p.op("tensor", lambda e, dc=dc, half=half: e.matmul(
                                psC[:, half * 512:(half + 1) * 512],
                                cfm[:, dc * 128:(dc + 1) * 128],
                                wo_sb[:, dc * D + half * 512:dc * D + (half + 1) * 512],
                                start=(dc == 0), stop=(dc == 7)),
                             r=[(cfm, dc), (wo_sb, None)],
                             w=[(psC, half)])
                p.op("vector", lambda e, qt=qt: e.tensor_tensor(
                        x_tm[:, qt * D:(qt + 1) * D], x_tm[:, qt * D:(qt + 1) * D],
                        psC[:, 0:1024], ALU.add),
                     r=[(x_tm, qt), (psC, 0), (psC, 1)],
                     w=[(x_tm, qt)])

            for i in range(NCH + 2):
                if i >= 2:
                    stC(i - 2)
                if 0 <= i - 1 < NCH:
                    stB(i - 1)
                if i < NCH:
                    stA_mm(i)
                if i >= 2:
                    stD(i - 2)
                if i < NCH:
                    stA_chain(i)

        def attn1_pass(ph, q_fm, k_fm, v_tm):
            sc_sb = sb("sc_sb", 128, 2 * T, F32, ph)
            pbufA = sb("pbufA", 128, T, BF16, ph)
            pbufB = sb("pbufB", 128, T, BF16, ph)
            ptbA = sb("ptbA", 128, T, BF16, ph)
            ptbB = sb("ptbB", 128, T, BF16, ph)
            zfm = sb("zfm", 128, 2 * 8 * 128, BF16, ph)
            prring = sb("prring", 128, 2 * D, BF16, ph)
            psA = psum("psA1", 128, 2048, ph)
            psB = psum("psB1", 128, 1024, ph)
            ptp = psum("ptp1", 128, 2048, ph, dt=BF16)

            def st_scores(qt, h, Lq, ntile):
                hb = h % 2
                oc, prt = h // 2, (h % 2) * 64
                for kjt in range(ntile):
                    j0 = kjt * 512
                    w512 = min(512, Lq - j0)
                    bk = (hb * 2 + kjt) % 4
                    p.op("tensor", lambda e, oc=oc, prt=prt, qt=qt, j0=j0, w512=w512, bk=bk: e.matmul(
                            psA[:, bk * 512:bk * 512 + w512],
                            q_fm[prt:prt + 64, oc * T + qt * 128:oc * T + (qt + 1) * 128],
                            k_fm[prt:prt + 64, oc * T + j0:oc * T + j0 + w512],
                            start=True, stop=True),
                         r=[(q_fm, (oc, qt * 128 // TS))] +
                           [(k_fm, (oc, (j0 + jj * 128) // TS)) for jj in range(w512 // 128)],
                         w=[(psA, bk)])
                    wdiag = (j0 + w512 == Lq)
                    ee = "vector" if kjt % 2 == 0 else "scalar"
                    if wdiag:
                        if w512 > 128:
                            p.op(ee, lambda e, hb=hb, j0=j0, w512=w512, bk=bk: (
                                    e.tensor_copy(sc_sb[:, hb * T + j0:hb * T + j0 + w512 - 128],
                                                  psA[:, bk * 512:bk * 512 + w512 - 128])
                                    if hasattr(e, "tensor_copy") else
                                    e.copy(sc_sb[:, hb * T + j0:hb * T + j0 + w512 - 128],
                                           psA[:, bk * 512:bk * 512 + w512 - 128])),
                                 r=[(psA, bk)], w=[(sc_sb, (hb, kjt))])
                        p.op("vector", lambda e, hb=hb, j0=j0, qt=qt, bk=bk: e.tensor_tensor(
                                sc_sb[:, hb * T + qt * 128:hb * T + (qt + 1) * 128],
                                psA[:, bk * 512 + (qt * 128 - j0):bk * 512 + (qt * 128 - j0) + 128],
                                causal[:], ALU.add),
                             r=[(psA, bk), (causal, None)], w=[(sc_sb, (hb, kjt, "d"))])
                    else:
                        p.op(ee, lambda e, hb=hb, j0=j0, w512=w512, bk=bk: (
                                e.tensor_copy(sc_sb[:, hb * T + j0:hb * T + j0 + w512],
                                              psA[:, bk * 512:bk * 512 + w512])
                                if hasattr(e, "tensor_copy") else
                                e.copy(sc_sb[:, hb * T + j0:hb * T + j0 + w512],
                                       psA[:, bk * 512:bk * 512 + w512])),
                             r=[(psA, bk)], w=[(sc_sb, (hb, kjt))])

            def st_soft(qt, h, Lq, ntile):
                hb = h % 2
                pb = pbufA if hb == 0 else pbufB
                rsc = [(sc_sb, (hb, k)) for k in range(ntile)] + [(sc_sb, (hb, ntile - 1, "d"))]
                p.op("vector", lambda e, Lq=Lq, hb=hb: e.reduce_max(
                        sm_sc[:, hb * 4:hb * 4 + 1], sc_sb[:, hb * T:hb * T + Lq],
                        axis=AX.X, negate=True),
                     r=rsc, w=[(sm_sc, (hb, 0))])
                p.op("scalar", lambda e, pb=pb, Lq=Lq, hb=hb, qt=qt: e.activation(
                        pb[:, 0:Lq], sc_sb[:, hb * T:hb * T + Lq], AF.Exp,
                        bias=sm_sc[:, hb * 4:hb * 4 + 1],
                        scale=np.float32(1.0), accum_out=sm_sc[:, hb * 4 + 2:hb * 4 + 3]),
                     r=rsc + [(sm_sc, (hb, 0))],
                     w=[(pb, kc) for kc in range(qt + 1)] + [(sm_sc, (hb, 2))])
                p.op("vector", lambda e, hb=hb: e.reciprocal(
                        sm_sc[:, hb * 4 + 3:hb * 4 + 4], sm_sc[:, hb * 4 + 2:hb * 4 + 3]),
                     r=[(sm_sc, (hb, 2))], w=[(sm_sc, (hb, 3))])
                p.op("vector", lambda e, pb=pb, Lq=Lq, hb=hb, qt=qt: e.tensor_scalar(
                        pb[:, 0:Lq], pb[:, 0:Lq], sm_sc[:, hb * 4 + 3:hb * 4 + 4],
                        0.125, ALU.mult, ALU.mult),
                     r=[(pb, kc) for kc in range(qt + 1)] + [(sm_sc, (hb, 3))],
                     w=[(pb, kc) for kc in range(qt + 1)])

            def st_pt(qt, h):
                hb = h % 2
                pb = pbufA if hb == 0 else pbufB
                ptb = ptbA if hb == 0 else ptbB
                seg = []
                for kc in range(qt + 1):
                    slot = hb * 8 + kc % 8
                    p.op("tensor", lambda e, kc=kc, slot=slot, pb=pb: e.transpose(
                            ptp[:, slot * 128:(slot + 1) * 128],
                            pb[:, kc * 128:(kc + 1) * 128], id_b[:]),
                         r=[(pb, kc), (id_b, None)], w=[(ptp, hb)])
                    seg.append(kc)
                    if len(seg) == 8 or kc == qt:
                        k0 = seg[0]
                        p.op("scalar", lambda e, k0=k0, kc=kc, hb=hb, ptb=ptb: e.copy(
                                ptb[:, k0 * 128:(kc + 1) * 128],
                                ptp[:, hb * 1024 + (k0 % 8) * 128:
                                    hb * 1024 + (k0 % 8) * 128 + (kc + 1 - k0) * 128]),
                             r=[(ptp, hb)],
                             w=[(ptb, k) for k in seg])
                        seg = []

            def st_ctx(qt, h):
                for kc in range(qt + 1):
                    for g in range(2):
                        hh = h - 1 + g
                        ptg = ptbA if g == 0 else ptbB
                        p.op("tensor", lambda e, kc=kc, g=g, hh=hh, ptg=ptg, qt=qt: e.matmul(
                                psB[(g * 64):(g * 64 + 64), g * 512:g * 512 + 128],
                                v_tm[:, kc * D + hh * 64:kc * D + (hh + 1) * 64],
                                ptg[:, kc * 128:(kc + 1) * 128],
                                start=(kc == 0), stop=(kc == qt),
                                tile_position=(0, g * 64)),
                             r=[(v_tm, (kc, hh // 2))] + [(ptg, k) for k in range(qt + 1)],
                             w=[(psB, g)])
                oc = h // 2
                for g in range(2):
                    p.op("scalar", lambda e, oc=oc, qt=qt, g=g: e.copy(
                            zfm[g * 64:(g + 1) * 64,
                                (qt % 2) * 1024 + oc * 128:(qt % 2) * 1024 + (oc + 1) * 128],
                            psB[g * 64:(g + 1) * 64, g * 512:g * 512 + 128]),
                         r=[(psB, g)],
                         w=[(zfm, (qt % 2, oc, g))])

            for qt in range(NCH):
                Lq = (qt + 1) * 128
                ntile = (Lq + 511) // 512
                for i in range(16 + 3):
                    if i < 16:
                        st_scores(qt, i, Lq, ntile)
                    if 0 <= i - 1 < 16:
                        st_soft(qt, i - 1, Lq, ntile)
                    if 0 <= i - 2 < 16:
                        st_pt(qt, i - 2)
                        if (i - 2) % 2 == 1:
                            st_ctx(qt, i - 2)
                if qt % 2 == 1:
                    for dc in range(8):
                        wsl = dc % 2
                        p.dma("gpsimd", prring[:, wsl * D:(wsl + 1) * D],
                              di["proj1"][dc * 128:(dc + 1) * 128, :], None,
                              w=[(prring, wsl)])
                        for q2 in range(2):
                            for half in range(2):
                                p.op("tensor", lambda e, dc=dc, q2=q2, half=half, wsl=wsl: e.matmul(
                                        psA[:, q2 * 1024 + half * 512:q2 * 1024 + (half + 1) * 512],
                                        zfm[:, q2 * 1024 + dc * 128:q2 * 1024 + (dc + 1) * 128],
                                        prring[:, wsl * D + half * 512:wsl * D + (half + 1) * 512],
                                        start=(dc == 0), stop=(dc == 7)),
                                     r=[(zfm, (q2, dc, 0)), (zfm, (q2, dc, 1)), (prring, wsl)],
                                     w=[(psA, q2 * 2 + half)])
                    for q2 in range(2):
                        qq = qt - 1 + q2
                        p.op("vector", lambda e, qq=qq, q2=q2: e.tensor_tensor(
                                x_tm[:, qq * D:(qq + 1) * D], x_tm[:, qq * D:(qq + 1) * D],
                                psA[:, q2 * 1024:(q2 + 1) * 1024], ALU.add),
                             r=[(x_tm, qq), (psA, q2 * 2), (psA, q2 * 2 + 1)],
                             w=[(x_tm, qq)])

        def ffn_pass(layer, ph, xn_fm):
            ne = 9 if layer == 0 else 10
            inter = 512 if layer == 0 else 1408
            nfc = inter // 128
            slices = [(0, 512)] if layer == 0 else [(0, 512), (512, 512), (1024, 384)]
            nsl = len(slices)
            wgd = di["wg0" if layer == 0 else "wg1"]
            wud = di["wu0" if layer == 0 else "wu1"]
            wdd = di["wd0" if layer == 0 else "wd1"]
            wgr = sb(f"wgr{layer}", 128, 2 * 8 * 512, BF16, ph)
            wur = sb(f"wur{layer}", 128, 2 * 8 * 512, BF16, ph)
            wd_sb = sb(f"wd_sb{layer}", 128, nfc * D, BF16, ph)
            hfm = sb(f"hfm{layer}", 128, nfc * T, BF16, ph)
            sil = sb(f"sil{layer}", 128, 2 * 512, F32, ph)
            gps = psum(f"gps{layer}", 128, 2048, ph)
            yps = psum(f"yps{layer}", 128, 2048, ph)

            def load_gu(e_, si, rslot):
                f0, fl = slices[si]
                p.dma("gpsimd",
                      wgr.ap().rearrange("p (s c f) -> p s c f", s=2, c=8)[:, rslot, :, 0:fl],
                      wgd[e_][:, 8 * f0:8 * (f0 + fl)].rearrange("p (c f) -> p c f", c=8),
                      None, w=[(wgr, rslot)])
                p.dma("gpsimd",
                      wur.ap().rearrange("p (s c f) -> p s c f", s=2, c=8)[:, rslot, :, 0:fl],
                      wud[e_][:, 8 * f0:8 * (f0 + fl)].rearrange("p (c f) -> p c f", c=8),
                      None, w=[(wur, rslot)])

            def load_d(e_):
                p.dma("gpsimd", wd_sb[:], wdd[e_], None, w=[(wd_sb, None)])

            NSL4 = max(1, T // 512)
            TSL = min(512, T)
            load_gu(0, 0, 0)
            for e_ in range(ne):
                load_d(e_)
                for (si, (f0, fl)) in enumerate(slices):
                    rslot = (e_ * nsl + si) % 2
                    nxt = e_ * nsl + si + 1
                    if nxt < ne * nsl:
                        load_gu(nxt // nsl, nxt % nsl, nxt % 2)
                    for lo in range(0, fl, 128):
                        fc = f0 // 128 + lo // 128
                        for ts in range(NSL4):
                            gb = ts % 2
                            gp = gps[:, gb * 512:gb * 512 + TSL]
                            up = gps[:, 1024 + gb * 512:1024 + gb * 512 + TSL]
                            for dc in range(8):
                                p.op("tensor", lambda e, dc=dc, ts=ts, lo=lo, rslot=rslot, gp=gp: e.matmul(
                                        gp, wgr[:, rslot * 4096 + dc * 512 + lo:
                                                rslot * 4096 + dc * 512 + lo + 128],
                                        xn_fm[:, dc * T + ts * TSL:dc * T + (ts + 1) * TSL],
                                        start=(dc == 0), stop=(dc == 7)),
                                     r=[(wgr, rslot)] + [(xn_fm, (ts * TSL + j * 128) // 128)
                                                         for j in range(TSL // 128)],
                                     w=[(gps, ("g", gb))])
                                p.op("tensor", lambda e, dc=dc, ts=ts, lo=lo, rslot=rslot, up=up: e.matmul(
                                        up, wur[:, rslot * 4096 + dc * 512 + lo:
                                                rslot * 4096 + dc * 512 + lo + 128],
                                        xn_fm[:, dc * T + ts * TSL:dc * T + (ts + 1) * TSL],
                                        start=(dc == 0), stop=(dc == 7)),
                                     r=[(wur, rslot)] + [(xn_fm, (ts * TSL + j * 128) // 128)
                                                         for j in range(TSL // 128)],
                                     w=[(gps, ("u", gb))])
                            ssl = sil[:, gb * 512:gb * 512 + TSL]
                            p.op("scalar", lambda e, ssl=ssl, gp=gp, gb=gb: e.activation(ssl, gp, AF.Silu),
                                 r=[(gps, ("g", gb))], w=[(sil, gb)])
                            p.op("vector", lambda e, fc=fc, ts=ts, ssl=ssl, up=up, gb=gb: e.tensor_tensor(
                                    hfm[:, fc * T + ts * TSL:fc * T + (ts + 1) * TSL], up, ssl, ALU.mult),
                                 r=[(gps, ("u", gb)), (sil, gb)], w=[(hfm, (fc, ts))])
                for c in range(NCH):
                    yb = c % 2
                    for fc in range(nfc):
                        for half in range(2):
                            p.op("tensor", lambda e, fc=fc, half=half, c=c, yb=yb: e.matmul(
                                    yps[:, yb * 1024 + half * 512:yb * 1024 + (half + 1) * 512],
                                    hfm[:, fc * T + c * 128:fc * T + (c + 1) * 128],
                                    wd_sb[:, fc * D + half * 512:fc * D + (half + 1) * 512],
                                    start=(fc == 0), stop=(fc == nfc - 1)),
                                 r=[(hfm, (fc, (c * 128) // TSL)), (wd_sb, None)],
                                 w=[(yps, yb * 2 + half)])
                    if e_ < 8:
                        p.op("vector", lambda e, c=c, yb=yb, e_=e_: e.scalar_tensor_tensor(
                                x_tm[:, c * D:(c + 1) * D],
                                yps[:, yb * 1024:(yb + 1) * 1024],
                                cw_tm[:, c * 8 + e_:c * 8 + e_ + 1],
                                x_tm[:, c * D:(c + 1) * D],
                                ALU.mult, ALU.add),
                             r=[(x_tm, c), (cw_tm, c), (yps, yb * 2), (yps, yb * 2 + 1)],
                             w=[(x_tm, c)])
                    else:
                        p.op("vector", lambda e, c=c, yb=yb: e.tensor_tensor(
                                x_tm[:, c * D:(c + 1) * D], x_tm[:, c * D:(c + 1) * D],
                                yps[:, yb * 1024:(yb + 1) * 1024], ALU.add),
                             r=[(x_tm, c), (yps, yb * 2), (yps, yb * 2 + 1)],
                             w=[(x_tm, c)])

        try:
            if stop_after == "__stopped__":
                raise _StopBuild()
            for layer in (0, 1):
                xn_scope = PScope()
                xn_fm = sb(f"xn_fm{layer}", 128, 8 * T, BF16, xn_scope)
                with PScope() as ph:
                    ln_pass(layer, ph, xn_fm)
                    p.flush(f"ln{layer}")
                if f"cw{layer}" in taps:
                    a = tap(f"cw{layer}", (T, 8))
                    p.dma("sync", a.rearrange("(c p) o -> p c o", p=128),
                          cw_tm.ap().rearrange("p (c o) -> p c o", c=NCH), f"tapcw{layer}",
                          r=[(cw_tm, c) for c in range(NCH)])
                    p.flush(f"tapcw{layer}")
                with PScope() as ph:
                    ffn_pass(layer, ph, xn_fm)
                    p.flush(f"ffn{layer}")
                if f"xf{layer}" in taps:
                    a = tap(f"xf{layer}", (T, D))
                    p.dma("sync", a.rearrange("(c p) o -> p c o", p=128),
                          x_tm.ap().rearrange("p (c o) -> p c o", c=NCH), f"tapxf{layer}",
                          r=[(x_tm, c) for c in range(NCH)])
                    p.flush(f"tapxf{layer}")
                attn_scope = PScope()
                q_fm = sb(f"q_fm{layer}", 128, 8 * T, BF16, attn_scope)
                k_fm = sb(f"k_fm{layer}", 128, 8 * T, BF16, attn_scope)
                v_tm = sb(f"v_tm{layer}", 128, NCH * D, BF16, attn_scope)
                with PScope() as ph2:
                    qkv_pass(layer, ph2, q_fm, k_fm, v_tm, xn_fm)
                    p.flush(f"qkv{layer}")
                xn_scope.close()
                if layer == 0:
                    rows_scope = PScope()
                    tabi = sb("tabi", 4, T, BF16, rows_scope)
                    tabj = sb("tabj", 4, 2 * T, BF16, rows_scope)
                    r01_scope = PScope()
                    r01f = sb("r01f", 2, T, F32, r01_scope)
                    with PScope() as ph:
                        stats_pass(ph, tabi, tabj, r01f)
                        p.flush("stats")
                    with PScope() as ph:
                        quantum_pass(ph, r01f)
                        p.flush("quantum")
                    r01_scope.close()
                    if "qp" in taps:
                        a = tap("qp", (T, 2))
                        p.dma("sync", a.rearrange("(c p) o -> p c o", p=128),
                              qp_tm.ap().rearrange("p (c o) -> p c o", c=NCH), "tapqp",
                              r=[(qp_tm, c) for c in range(NCH)])
                        p.flush("tapqp")
                if f"v{layer}" in taps:
                    a = tap(f"v{layer}", (T, D))
                    p.dma("gpsimd", a.rearrange("(c p) o -> p c o", p=128),
                          v_tm.ap().rearrange("p (c o) -> p c o", c=NCH), f"tapv{layer}",
                          r=[(v_tm, None)])
                    p.flush(f"tapv{layer}")
                with PScope() as ph2:
                    if layer == 0:
                        attn0_pass(ph2, q_fm, k_fm, v_tm, tabi, tabj)
                    else:
                        attn1_pass(ph2, q_fm, k_fm, v_tm)
                    p.flush(f"attn{layer}")
                attn_scope.close()
                if layer == 0:
                    rows_scope.close()
                if f"x{layer}" in taps:
                    a = tap(f"x{layer}", (T, D))
                    p.dma("sync", a.rearrange("(c p) o -> p c o", p=128),
                          x_tm.ap().rearrange("p (c o) -> p c o", c=NCH), f"tapx{layer}",
                          r=[(x_tm, c) for c in range(NCH)])
                    p.flush(f"tapx{layer}")


        except _StopBuild:
            pass
        p.flush = lambda name=None: _orig_flush(p, name)

        p.dma("sync", out_hbm.rearrange("(c p) o -> p c o", p=128),
              x_tm.ap().rearrange("p (c o) -> p c o", c=NCH), "st",
              r=[(x_tm, c) for c in range(NCH)])
        p.wait_all_dma("sync", "st")
        p.flush("store")

    return nc


def kernel(**inputs):
    w = _prep_weights(inputs)
    x = np.ascontiguousarray(np.asarray(inputs["L0_attn_hidden_states"], np.float32))
    nc = build_program(T=2048)
    in_maps = []
    for b in range(8):
        m = dict(w)
        m["x"] = np.ascontiguousarray(x[b])
        in_maps.append(m)
    res = run_bass_kernel_spmd(nc, in_maps, core_ids=list(range(8)))
    out = np.stack([np.asarray(res.results[b]["out"]) for b in range(8)], axis=0)
    return out.astype(np.float32)
